# version 06
# baseline (speedup 1.0000x reference)
import contextlib
import numpy as np
import ml_dtypes
import concourse.bass as bass
import concourse.mybir as mybir
from concourse.bass import IndirectOffsetOnAxis
from concourse.bass_utils import run_bass_kernel_spmd

F32 = mybir.dt.float32
BF16 = mybir.dt.bfloat16
I32 = mybir.dt.int32
AF = mybir.ActivationFunctionType
ALU = mybir.AluOpType
AX = mybir.AxisListType

NCORE = 8
D = 4096
NF = 32
CH = 512
NQH, NKVH, HD = 32, 8, 128
NSLOT = 7
SCALE = 1.0 / float(np.sqrt(128.0))
EXPB = -35.0
TWO_PI = 6.283185307179586
MAGIC = 12582912.0
LN10K = float(np.log(10000.0))


class SemT:
    def __init__(self, name):
        self.name = name
        self.n = 0

class Plan:
    ENGS = ("sync", "gpsimd", "vector", "scalar", "tensor")

    def __init__(self):
        self.ops = {e: [] for e in self.ENGS}
        self.lastwait = {}

    def op(self, eng, fn, waits=(), inc=None, inc_by=1):
        ws = []
        for s, v in waits:
            if v <= 0:
                continue
            key = (eng, s.name)
            if self.lastwait.get(key, -1) >= v:
                continue
            self.lastwait[key] = v
            ws.append((s, v))
        tgt = None
        if inc is not None:
            inc.n += inc_by
            tgt = inc.n
        self.ops[eng].append((fn, ws, (inc, inc_by) if inc is not None else None))
        return tgt

    def barrier(self, engs, sems, exclude=()):
        for e in engs:
            for s in sems:
                if s.name in exclude:
                    continue
                v = s.n
                if v <= 0:
                    continue
                key = (e, s.name)
                if self.lastwait.get(key, -1) >= v:
                    continue
                self.lastwait[key] = v
                self.ops[e].append((None, [(s, v)], None))



def build_program():
    nc = bass.Bass("TRN2", target_bir_lowering=False, detect_race_conditions=False)

    xTA = nc.dram_tensor("xTA", [D, CH], F32, kind="ExternalInput")
    xTB = nc.dram_tensor("xTB", [D, CH], F32, kind="ExternalInput")
    wq = nc.dram_tensor("wq", [NQH, 128, D], F32, kind="ExternalInput")
    wk = nc.dram_tensor("wk", [NKVH, 128, D], F32, kind="ExternalInput")
    wv = nc.dram_tensor("wv", [2, NF, 128, 512], F32, kind="ExternalInput")
    wo = nc.dram_tensor("wo", [NQH, 128, D], F32, kind="ExternalInput")
    wa = nc.dram_tensor("wa", [NF, 128, 68], F32, kind="ExternalInput")
    wb = nc.dram_tensor("wb", [32, 64, 128], F32, kind="ExternalInput")
    tri_d = nc.dram_tensor("tri", [128, 128], F32, kind="ExternalInput")
    id_d = nc.dram_tensor("ident", [128, 128], F32, kind="ExternalInput")
    e2g_d = nc.dram_tensor("e2g", [4, 64], F32, kind="ExternalInput")
    pos_d = nc.dram_tensor("pos", [2, CH], F32, kind="ExternalInput")
    kidx_d = nc.dram_tensor("kidx", [128, NKVH * NSLOT], I32, kind="ExternalInput")
    vai_d = nc.dram_tensor("vai", [128, NKVH * NSLOT * 4], I32, kind="ExternalInput")
    vbi_d = nc.dram_tensor("vbi", [128, NKVH * NSLOT * 4], I32, kind="ExternalInput")
    qix_d = nc.dram_tensor("qix", [128, NQH * NSLOT], I32, kind="ExternalInput")
    fix_d = nc.dram_tensor("fix", [128, 2 * NSLOT], I32, kind="ExternalInput")

    outA = nc.dram_tensor("outA", [D, CH], F32, kind="ExternalOutput")
    outB = nc.dram_tensor("outB", [D, CH], F32, kind="ExternalOutput")

    ag_in = nc.dram_tensor("ag_in", [4096, 512], BF16)
    ag_out = nc.dram_tensor("ag_out", [NCORE * 4096 + 32, 512], BF16)
    qsp = nc.dram_tensor("qsp", [NQH * 256, 512], BF16)
    c2 = nc.dram_tensor("c2", [33, 128], F32)

    ag_k = ag_out.ap()
    ag_v = ag_out.ap().rearrange("r (a b) -> (r a) b", a=4, b=128)

    P = Plan()
    ld = SemT("ld")
    gd = SemT("gd")
    mm = SemT("mm")
    ve = SemT("ve")
    sc = SemT("sc")
    cc = SemT("cc")
    st = SemT("st")
    sd = SemT("sd")
    io = SemT("io")
    SEMS = [ld, gd, mm, ve, sc, cc, st, sd, io]

    stack = contextlib.ExitStack()
    E = stack.enter_context

    kslot = E(nc.sbuf_tensor("kslot", [128, NSLOT * 512], BF16))
    vslotA = E(nc.sbuf_tensor("vslotA", [128, NSLOT * 512], BF16))
    vslotB = E(nc.sbuf_tensor("vslotB", [128, NSLOT * 512], BF16))
    qslot = E(nc.sbuf_tensor("qslot", [128, NSLOT * 512], BF16))
    flg_f = E(nc.sbuf_tensor("flg_f", [128, 128], F32))
    spcr = E(nc.sbuf_tensor("spcr", [128, 8], F32))
    cosq = E(nc.sbuf_tensor("cosq", [128, 1024], F32))
    sinq = E(nc.sbuf_tensor("sinq", [128, 1024], F32))
    k_own = E(nc.sbuf_tensor("k_own", [128, NKVH * 1024], BF16))
    v_own = E(nc.sbuf_tensor("v_own", [128, 8 * 1024], BF16))
    mfin = E(nc.sbuf_tensor("mfin", [128, 1024], BF16))
    flagA = E(nc.sbuf_tensor("flagA", [128, NSLOT * 128], BF16))
    flagB = E(nc.sbuf_tensor("flagB", [128, NSLOT * 128], BF16))
    tri_sb = E(nc.sbuf_tensor("tri_sb", [128, 128], F32))
    id_sb = E(nc.sbuf_tensor("id_sb", [128, 128], F32))
    e2g_sb = E(nc.sbuf_tensor("e2g_sb", [4, 64], F32))
    kidx_sb = E(nc.sbuf_tensor("kidx_sb", [128, NKVH * NSLOT], I32))
    vai_sb = E(nc.sbuf_tensor("vai_sb", [128, NKVH * NSLOT * 4], I32))
    vbi_sb = E(nc.sbuf_tensor("vbi_sb", [128, NKVH * NSLOT * 4], I32))
    qix_sb = E(nc.sbuf_tensor("qix_sb", [128, NQH * NSLOT], I32))
    fix_sb = E(nc.sbuf_tensor("fix_sb", [128, 2 * NSLOT], I32))
    ones128 = E(nc.sbuf_tensor("ones128", [128, 128], BF16))
    onesinv = E(nc.sbuf_tensor("onesinv", [128, 128], BF16))
    biasv = E(nc.sbuf_tensor("biasv", [128, 1], F32))
    epsv = E(nc.sbuf_tensor("epsv", [128, 1], F32))
    zrow = E(nc.sbuf_tensor("zrow", [32, 512], BF16))
    c2_sb = E(nc.sbuf_tensor("c2_sb", [33, 128], F32))

    def dma(eng, out_ap, in_ap, sem, waits=()):
        return P.op(eng, lambda e, o=out_ap, i=in_ap: e.dma_start(out=o, in_=i),
                    waits=waits, inc=sem, inc_by=16)

    t0 = []
    t0.append(dma("sync", tri_sb[:, :], tri_d[:, :], ld))
    t0.append(dma("sync", id_sb[:, :], id_d[:, :], ld))
    t0.append(dma("sync", e2g_sb[:, :], e2g_d[:, :], ld))
    t0.append(dma("sync", kidx_sb[:, :], kidx_d[:, :], ld))
    t0.append(dma("sync", vai_sb[:, :], vai_d[:, :], ld))
    t0.append(dma("sync", vbi_sb[:, :], vbi_d[:, :], ld))
    t0.append(dma("sync", qix_sb[:, :], qix_d[:, :], ld))
    t0.append(dma("sync", fix_sb[:, :], fix_d[:, :], ld))
    tbl_ld = ld.n

    mset = []
    P.op("vector", lambda e: e.memset(ones128[:, :], 1.0))
    P.op("vector", lambda e: e.memset(onesinv[:, :], 1.0 / 4096.0))
    P.op("vector", lambda e: e.memset(biasv[:, :], EXPB))
    P.op("vector", lambda e: e.memset(epsv[:, :], 1e-5))
    P.op("vector", lambda e: e.memset(zrow[:, :], 0.0))
    P.op("vector", lambda e: e.memset(c2_sb[:, :], 0.0))
    msets_done = P.op("vector", lambda e: e.memset(c2_sb[0:1, :], 1.0), inc=ve)

    c2w = dma("scalar", c2[:, :], c2_sb[:, :], sd, waits=[(ve, msets_done)])
    zpw = dma("scalar", ag_out[NCORE * 4096: NCORE * 4096 + 32, :], zrow[:, :], sd,
              waits=[(ve, msets_done)])

    flg_done = []
    for t in range(2):
        for p in range(NSLOT):
            g = P.op("gpsimd",
                     lambda e, t=t, p=p: e.indirect_dma_start(
                         flg_f[:, :], None, c2[:, :],
                         IndirectOffsetOnAxis(fix_sb[:, t * NSLOT + p: t * NSLOT + p + 1], 0)),
                     waits=[(ld, tbl_ld), (sd, c2w), (ve, flg_done[-1] if flg_done else 0)],
                     inc=gd, inc_by=16)
            dstf = flagA if t == 0 else flagB
            flg_done.append(P.op("vector",
                                 lambda e, d=dstf, p=p: e.tensor_copy(d[:, p * 128:(p + 1) * 128], flg_f[:, :]),
                                 waits=[(gd, g)], inc=ve))
    flags_ve = ve.n

    with contextlib.ExitStack() as ph:
        EE = ph.enter_context
        invf = EE(nc.sbuf_tensor("invf", [64, 1], F32))
        posr = EE(nc.sbuf_tensor("posr", [64, 512], F32))
        th = EE(nc.sbuf_tensor("th", [64, 512], F32))
        u1 = EE(nc.sbuf_tensor("u1", [64, 512], F32))
        n1 = EE(nc.sbuf_tensor("n1", [64, 512], F32))
        r1 = EE(nc.sbuf_tensor("r1", [64, 512], F32))
        P.op("gpsimd", lambda e: e.iota(invf[:, :], pattern=[[0, 1]], channel_multiplier=1,
                                        allow_small_or_imprecise_dtypes=True), inc=io, inc_by=1)
        iv = P.op("scalar", lambda e: e.activation(invf[:, :], invf[:, :], AF.Exp, scale=-LN10K / 64.0),
                  waits=[(io, 1)], inc=sc)
        prev_tv = 0
        prev_tr = 0
        for own in range(2):
            pr = P.op("scalar",
                      lambda e, own=own: e.dma_start(out=posr[:, :],
                                                     in_=pos_d[own:own + 1, :].broadcast_to([64, CH])),
                      waits=[(ld, tbl_ld), (ve, prev_tv)], inc=sd, inc_by=16)
            tv = P.op("vector", lambda e: e.tensor_scalar_mul(th[:, :], posr[:, :], invf[0:64, 0:1]),
                      waits=[(sd, pr), (sc, iv)], inc=ve)
            prev_tv = tv
            for trig, off in (("sin", 0.0), ("cos", 0.25)):
                P.op("vector", lambda e, off=off: e.tensor_scalar(u1[:, :], th[:, :], 1.0 / TWO_PI, off, ALU.mult, ALU.add),
                     waits=[(ve, tv), (sc, prev_tr)])
                P.op("vector", lambda e: e.tensor_scalar_add(n1[:, :], u1[:, :], MAGIC))
                P.op("vector", lambda e: e.tensor_scalar_add(n1[:, :], n1[:, :], -MAGIC))
                P.op("vector", lambda e: e.tensor_sub(r1[:, :], u1[:, :], n1[:, :]))
                rv = P.op("vector", lambda e: e.tensor_scalar_mul(r1[:, :], r1[:, :], TWO_PI), inc=ve)
                if trig == "sin":
                    sn = P.op("scalar", lambda e, own=own: e.activation(sinq[64:128, own * 512:(own + 1) * 512],
                                                                        r1[:, :], AF.Sin),
                              waits=[(ve, rv)], inc=sc)
                    prev_tr = sn
                    P.op("vector", lambda e, own=own: e.tensor_scalar_mul(
                        sinq[0:64, own * 512:(own + 1) * 512], sinq[64:128, own * 512:(own + 1) * 512], -1.0),
                        waits=[(sc, sn)], inc=ve)
                else:
                    cs = P.op("scalar", lambda e, own=own: e.activation(cosq[0:64, own * 512:(own + 1) * 512],
                                                                        r1[:, :], AF.Sin),
                              waits=[(ve, rv)], inc=sc)
                    prev_tr = cs
                    P.op("scalar", lambda e, own=own: e.dma_start(
                        out=cosq[64:128, own * 512:(own + 1) * 512],
                        in_=cosq[0:64, own * 512:(own + 1) * 512]),
                        waits=[(sc, cs)], inc=sd, inc_by=16)
    rope_tabs_ve = ve.n
    rope_tabs_sd = sd.n

    s1 = contextlib.ExitStack()
    E1 = s1.enter_context
    xn = E1(nc.sbuf_tensor("xn", [128, 2 * NF * 512], BF16))
    xst = E1(nc.sbuf_tensor("xst", [128, 2 * 512], F32))
    xbst = E1(nc.sbuf_tensor("xbst", [128, 2 * 512], BF16))
    xsq = E1(nc.sbuf_tensor("xsq", [128, 2 * 512], BF16))
    wf32 = E1(nc.sbuf_tensor("wf32", [128, 2 * 1024], F32))
    wbf = E1(nc.sbuf_tensor("wbf", [128, 2 * 4096], BF16))
    mean_sb = E1(nc.sbuf_tensor("mean_sb", [128, 1024], F32))[0:128]
    rstd_sb = E1(nc.sbuf_tensor("rstd_sb", [128, 1024], F32))[0:128]
    qf = E1(nc.sbuf_tensor("qf", [128, 1024], F32))[0:128]
    qs = E1(nc.sbuf_tensor("qs", [128, 1024], F32))[0:128]
    t1s = E1(nc.sbuf_tensor("t1s", [128, 1024], F32))[0:128]
    wabf = qslot[:, 0:NF * 68]
    lg_sb = E1(nc.sbuf_tensor("lg_sb", [4, 1024], F32))[0:4]
    gt_sb = E1(nc.sbuf_tensor("gt_sb", [4, 1024], F32))[0:4]
    gb_sb = kslot[:, 0:1024].bitcast(F32)[0:64]
    gsm = E1(nc.sbuf_tensor("gsm", [128, 128], F32))
    gsm1 = E1(nc.sbuf_tensor("gsm1", [128, 48], F32))

    wal = dma("sync", wf32[:, 0:2040].rearrange("p (f e) -> p f e", e=68),
              wa.ap()[0:30].rearrange("f i e -> i f e"), ld)
    wc1 = P.op("vector", lambda e: e.tensor_copy(wabf[:, 0:2040], wf32[:, 0:2040]),
               waits=[(ld, wal)], inc=ve)
    wal2 = dma("sync", wf32[:, 0:136].rearrange("p (f e) -> p f e", e=68),
               wa.ap()[30:32].rearrange("f i e -> i f e"), ld, waits=[(ve, wc1)])
    P.op("vector", lambda e: e.tensor_copy(wabf[:, 2040:2176], wf32[:, 0:136]),
         waits=[(ld, wal2)], inc=ve)
    wacast = P.op("vector", lambda e: e.tensor_copy(spcr[:, :], spcr[:, :]), inc=ve)

    with contextlib.ExitStack() as ph:
        EE = ph.enter_context
        mean_ps = EE(nc.psum_tensor("mean_ps", [128, 1024], F32))
        sq_ps = EE(nc.psum_tensor("sq_ps", [128, 1024], F32))
        xcons = {}
        xsrc = (xTA, xTB)
        for own in range(2):
            for f in range(NF):
                i = own * NF + f
                b = i % 2
                l_ = dma("sync", xst[:, b * 512:(b + 1) * 512], xsrc[own][f * 128:(f + 1) * 128, :], ld,
                         waits=[(mm, xcons.get(i - 2, 0)), (ve, wacast if i == 0 else 0)])
                cv_ = P.op("vector", lambda e, b=b: e.tensor_copy(xbst[:, b * 512:(b + 1) * 512],
                                                                  xst[:, b * 512:(b + 1) * 512]),
                           waits=[(ld, l_)], inc=ve)
                sq_ = P.op("scalar", lambda e, b=b: e.activation(xsq[:, b * 512:(b + 1) * 512],
                                                                 xst[:, b * 512:(b + 1) * 512], AF.Square),
                           waits=[(ld, l_)], inc=sc)
                P.op("tensor", lambda e, own=own, f=f, b=b: e.matmul(
                    mean_ps[:, own * 512:(own + 1) * 512], onesinv[:, :], xbst[:, b * 512:(b + 1) * 512],
                    start=(f == 0), stop=(f == NF - 1)),
                    waits=[(ve, cv_)])
                xcons[i] = P.op("tensor", lambda e, own=own, f=f, b=b: e.matmul(
                    sq_ps[:, own * 512:(own + 1) * 512], onesinv[:, :], xsq[:, b * 512:(b + 1) * 512],
                    start=(f == 0), stop=(f == NF - 1)),
                    waits=[(sc, sq_)], inc=mm)
        stats_mm = mm.n
        P.op("vector", lambda e: e.tensor_copy(mean_sb, mean_ps[:, :]), waits=[(mm, stats_mm)])
        P.op("vector", lambda e: e.tensor_mul(qf, mean_sb, mean_sb))
        vfin = P.op("vector", lambda e: e.tensor_tensor(qf, sq_ps[:, :], qf, ALU.subtract), inc=ve)
        sfin = P.op("scalar", lambda e: e.activation(qf, qf, AF.Sqrt, bias=epsv[:, 0:1]),
                    waits=[(ve, vfin)], inc=sc)
        rfin = P.op("vector", lambda e: e.reciprocal(rstd_sb, qf), waits=[(sc, sfin)], inc=ve)

        xn_done = {}
        for own in range(2):
            for f in range(NF):
                i = own * NF + f
                b = i % 2
                l_ = dma("sync", xst[:, b * 512:(b + 1) * 512], xsrc[own][f * 128:(f + 1) * 128, :], ld,
                         waits=[(ve, xn_done.get(i - 2, rfin))])
                P.op("vector", lambda e, b=b, own=own: e.tensor_tensor(
                    qf[:, 0:512], xst[:, b * 512:(b + 1) * 512], mean_sb[:, own * 512:(own + 1) * 512],
                    ALU.subtract), waits=[(ld, l_), (ve, rfin)])
                xn_done[i] = P.op("vector", lambda e, i=i, own=own: e.tensor_tensor(
                    xn[:, i * 512:(i + 1) * 512], qf[:, 0:512], rstd_sb[:, own * 512:(own + 1) * 512],
                    ALU.mult), inc=ve)
        xn_all = ve.n

    rope_state = {"last_swap": 0}

    def rope_emit(ps_ap, out_bf_ap, wait_mm, final_wait_sd=0):
        c1 = P.op("vector", lambda e: e.tensor_copy(qf, ps_ap),
                  waits=[(mm, wait_mm), (sd, rope_state["last_swap"])], inc=ve)
        s1_ = P.op("scalar", lambda e: e.dma_start(out=qs[0:64], in_=qf[64:128]),
                   waits=[(ve, c1)], inc=sd, inc_by=16)
        s2_ = P.op("scalar", lambda e: e.dma_start(out=qs[64:128], in_=qf[0:64, :]),
                   inc=sd, inc_by=16)
        rope_state["last_swap"] = s2_
        P.op("vector", lambda e: e.tensor_mul(t1s, qf, cosq[:, :]),
             waits=[(ve, rope_tabs_ve), (sd, rope_tabs_sd)])
        P.op("vector", lambda e: e.tensor_mul(qs, qs, sinq[:, :]), waits=[(sd, s2_)])
        done = P.op("vector", lambda e: e.tensor_add(out_bf_ap, t1s, qs),
                    waits=[(sd, final_wait_sd)], inc=ve)
        return c1, done

    with contextlib.ExitStack() as ph:
        EE = ph.enter_context
        kps = [EE(nc.psum_tensor(f"kps{j}", [128, 1024], F32)) for j in range(2)]
        kfree = {}
        kagw = []
        cast_done = {}
        mm_done = {}
        for kvh in range(NKVH):
            b = kvh % 2
            prevc = cast_done.get(kvh - 2, wacast if kvh < 2 else 0)
            cpiece = prevc
            for qtr in range(4):
                lq = dma("sync", wf32[:, b * 1024:(b + 1) * 1024],
                         wk[kvh][:, qtr * 1024:(qtr + 1) * 1024], ld,
                         waits=[(ve, cpiece)])
                cpiece = P.op("vector", lambda e, b=b, kvh=kvh, qtr=qtr: e.tensor_copy(
                    wbf[:, b * 4096 + qtr * 1024: b * 4096 + (qtr + 1) * 1024],
                    wf32[:, b * 1024:(b + 1) * 1024]),
                    waits=[(ld, lq), (mm, mm_done.get(kvh - 2, 0) if qtr == 0 else 0)], inc=ve)
            cast_done[kvh] = P.op("vector", lambda e: e.tensor_copy(spcr[:, :], spcr[:, :]), inc=ve)
            for own in range(2):
                for f in range(NF):
                    w = [(ve, cast_done[kvh]), (ve, xn_all)]
                    if f == 0 and own == 0:
                        w.append((ve, kfree.get(kvh - 2, 0)))
                    last = (own == 1 and f == NF - 1)
                    mm_done[kvh] = P.op("tensor", lambda e, kvh=kvh, own=own, f=f, b=b: e.matmul(
                        kps[b][:, own * 512:(own + 1) * 512],
                        wbf[:, b * 4096 + f * 128: b * 4096 + (f + 1) * 128],
                        xn[:, (own * NF + f) * 512:(own * NF + f + 1) * 512],
                        start=(f == 0), stop=(f == NF - 1)),
                        waits=w, inc=(mm if last else None))
            kf, kdone = rope_emit(kps[b][:, :], k_own[:, kvh * 1024:(kvh + 1) * 1024], mm_done[kvh])
            kfree[kvh] = kf
            for own in range(2):
                kagw.append(P.op("scalar", lambda e, kvh=kvh, own=own: e.dma_start(
                    out=ag_in[kvh * 256 + own: kvh * 256 + 256: 2, :],
                    in_=k_own[:, kvh * 1024 + own * 512: kvh * 1024 + (own + 1) * 512]),
                    waits=[(ve, kdone)], inc=sd, inc_by=16))
        k_phase_ve = ve.n
    kcast_last = cast_done[NKVH - 1]

    P.barrier(["tensor"], [ve])
    with contextlib.ExitStack() as ph:
        EE = ph.enter_context
        vps = [EE(nc.psum_tensor(f"vps{j}", [128, 512], F32)) for j in range(8)]
        vagw = []
        vcast = {}
        vfcons = {}
        for h in range(2):
            vmm = None
            for f in range(NF):
                b = f % 2
                l_ = dma("sync", wf32[:, b * 1024: b * 1024 + 512],
                         wv[h][f][:, :], ld,
                         waits=[(ve, vcast.get((h, f - 2),
                                               vcast.get((0, NF - 2 + f), 0) if h == 1 else 0)),
                                (ve, kcast_last)])
                P.op("vector", lambda e, b=b: e.tensor_copy(
                    wbf[:, b * 512:(b + 1) * 512], wf32[:, b * 1024: b * 1024 + 512]),
                    waits=[(ld, l_), (mm, vfcons.get((h, f - 2),
                                                    vfcons.get((0, NF - 1), 0) if h == 1 else 0))], inc=ve)
                vcast[(h, f)] = P.op("vector", lambda e: e.tensor_copy(spcr[:, :], spcr[:, :]), inc=ve)
                for own in range(2):
                    for kt in range(4):
                        j = own * 4 + kt
                        w = [(ve, vcast[(h, f)])]
                        if f == 0:
                            w.append((ve, k_phase_ve if h == 0 else 0))
                            w.append((ve, vevac if h == 1 else 0))
                        last = (own == 1 and kt == 3)
                        vmm = P.op("tensor", lambda e, own=own, kt=kt, f=f, b=b, j=j: e.matmul(
                            vps[j][:, :],
                            xn[:, (own * NF + f) * 512 + kt * 128: (own * NF + f) * 512 + (kt + 1) * 128],
                            wbf[:, b * 512:(b + 1) * 512],
                            start=(f == 0), stop=(f == NF - 1)),
                            waits=w, inc=(mm if last else None))
                vfcons[(h, f)] = vmm
            ev = []
            for own in range(2):
                for kt in range(4):
                    j = own * 4 + kt
                    ev.append(P.op("vector", lambda e, own=own, kt=kt, j=j, h=h: e.tensor_copy(
                        v_own[:, (own * 4 + kt) * 1024 + h * 512: (own * 4 + kt) * 1024 + (h + 1) * 512],
                        vps[j][:, :]), waits=[(mm, vmm)], inc=ve))
            vevac = ve.n
            for own in range(2):
                for kt in range(4):
                    for kk in range(4):
                        kvh = h * 4 + kk
                        base = 8192 + kvh * 1024 + own * 512 + kt * 128
                        vagw.append(P.op("scalar", lambda e, own=own, kt=kt, kk=kk, base=base, h=h: e.dma_start(
                            out=ag_in.ap().rearrange("r (a b) -> (r a) b", a=4, b=128)[base: base + 128, :],
                            in_=v_own[:, (own * 4 + kt) * 1024 + h * 512 + kk * 128:
                                      (own * 4 + kt) * 1024 + h * 512 + (kk + 1) * 128]),
                            waits=[(ve, vevac)], inc=sd, inc_by=16))

    agq = P.op("gpsimd", lambda e: e.collective_compute(
        "AllGather", ALU.bypass, replica_groups=[list(range(NCORE))],
        ins=[ag_in[:, :]], outs=[ag_out[0:NCORE * 4096, :]]),
        waits=[(sd, sd.n), (gd, gd.n)], inc=cc)

    vcast_last = vcast[(1, NF - 1)]
    v_mm_last = vfcons[(1, NF - 1)]
    P.barrier(["tensor"], [ve])
    with contextlib.ExitStack() as ph:
        EE = ph.enter_context
        qps = [EE(nc.psum_tensor(f"qps{j}", [128, 1024], F32)) for j in range(2)]
        qfree = {}
        qspw = {}
        qcast = {}
        qmm = {}
        for qh in range(NQH):
            b = qh % 2
            cpiece = qcast.get(qh - 2, vcast_last)
            for qtr in range(4):
                lq = dma("sync", wf32[:, b * 1024:(b + 1) * 1024],
                         wq[qh][:, qtr * 1024:(qtr + 1) * 1024], ld,
                         waits=[(ve, cpiece)])
                cpiece = P.op("vector", lambda e, b=b, qtr=qtr: e.tensor_copy(
                    wbf[:, b * 4096 + qtr * 1024: b * 4096 + (qtr + 1) * 1024],
                    wf32[:, b * 1024:(b + 1) * 1024]),
                    waits=[(ld, lq), (mm, qmm.get(qh - 2, v_mm_last) if qtr == 0 else 0)], inc=ve)
            qcast[qh] = P.op("vector", lambda e: e.tensor_copy(spcr[:, :], spcr[:, :]), inc=ve)
            for own in range(2):
                for f in range(NF):
                    w = [(ve, qcast[qh])]
                    if f == 0 and own == 0:
                        w.append((ve, qfree.get(qh - 2, 0)))
                    last = (own == 1 and f == NF - 1)
                    qmm[qh] = P.op("tensor", lambda e, own=own, f=f, b=b: e.matmul(
                        qps[b][:, own * 512:(own + 1) * 512],
                        wbf[:, b * 4096 + f * 128: b * 4096 + (f + 1) * 128],
                        xn[:, (own * NF + f) * 512:(own * NF + f + 1) * 512],
                        start=(f == 0), stop=(f == NF - 1)),
                        waits=w, inc=(mm if last else None))
            qfh, qdone = rope_emit(qps[b][:, :], xsq[:, :], qmm[qh],
                                   final_wait_sd=qspw.get((qh - 1, 1), 0))
            qfree[qh] = qfh
            for own in range(2):
                qspw[(qh, own)] = P.op("scalar", lambda e, qh=qh, own=own: e.dma_start(
                    out=qsp[qh * 256 + own: qh * 256 + 256: 2, :],
                    in_=xsq[:, own * 512:(own + 1) * 512]),
                    waits=[(ve, qdone)], inc=sd, inc_by=16)
    q_phase_done_sd = sd.n
    q_phase_done_ve = ve.n

    P.barrier(["tensor"], [ve])
    with contextlib.ExitStack() as ph:
        EE = ph.enter_context
        maug = EE(nc.psum_tensor("maug", [68, 1024], F32))
        smps = EE(nc.psum_tensor("smps", [128, 4], F32))
        gtps = EE(nc.psum_tensor("gtps", [4, 128], F32))
        gbps = EE(nc.psum_tensor("gbps", [64, 512], F32))
        amm = None
        for own in range(2):
            for f in range(NF):
                last = (own == 1 and f == NF - 1)
                amm = P.op("tensor", lambda e, own=own, f=f: e.matmul(
                    maug[:, own * 512:(own + 1) * 512],
                    wabf[:, f * 68:(f + 1) * 68],
                    xn[:, (own * NF + f) * 512:(own * NF + f + 1) * 512],
                    start=(f == 0), stop=(f == NF - 1)),
                    waits=[(ve, wacast), (ve, xn_all)], inc=(mm if last else None))
        lgc = P.op("vector", lambda e: e.tensor_copy(lg_sb, maug[64:68, :]), waits=[(mm, amm)], inc=ve)
        tdone = {}
        prev_lt = 0
        for k in range(8):
            own, s = divmod(k, 4)
            tmm = P.op("tensor", lambda e, own=own, s=s: e.matmul(
                smps[:, :], lg_sb[:, own * 512 + s * 128: own * 512 + (s + 1) * 128],
                id_sb[0:4, 0:4], start=True, stop=True),
                waits=[(ve, lgc), (ve, prev_lt)], inc=mm)
            prev_lt = P.op("vector", lambda e, k=k: e.tensor_copy(gsm[:, 4 * k:4 * k + 4], smps[:, :]),
                           waits=[(mm, tmm)], inc=ve)
        GL, GE, T1, T2 = 0, 32, 64, 96
        M1, NM, M2, ZS, RZ, RZ4 = 0, 8, 16, 24, 32, 40
        for k in range(8):
            P.op("vector", lambda e, k=k: e.reduce_max(gsm1[:, M1 + k:M1 + k + 1],
                                                       gsm[:, 4 * k:4 * k + 4], axis=AX.X))
        nmv = None
        for k in range(8):
            nmv = P.op("vector", lambda e, k=k: e.tensor_scalar_mul(
                gsm1[:, NM + k:NM + k + 1], gsm1[:, M1 + k:M1 + k + 1], -1.0),
                inc=(ve if k == 7 else None))
        eev = None
        for k in range(8):
            eev = P.op("scalar", lambda e, k=k: e.activation(
                gsm[:, GE + 4 * k:GE + 4 * k + 4], gsm[:, 4 * k:4 * k + 4], AF.Exp,
                bias=gsm1[:, NM + k:NM + k + 1]),
                waits=[(ve, nmv)], inc=(sc if k == 7 else None))
        for k in range(8):
            P.op("vector", lambda e, k=k: e.tensor_scalar(
                gsm[:, T1 + 4 * k:T1 + 4 * k + 4], gsm[:, 4 * k:4 * k + 4],
                gsm1[:, M1 + k:M1 + k + 1], None, ALU.is_ge))
        for k in range(8):
            P.op("vector", lambda e, k=k: e.tensor_scalar_mul(
                gsm[:, T2 + 4 * k:T2 + 4 * k + 4], gsm[:, T1 + 4 * k:T1 + 4 * k + 4], -30000.0))
        for k in range(8):
            P.op("vector", lambda e, k=k: e.tensor_add(
                gsm[:, T1 + 4 * k:T1 + 4 * k + 4], gsm[:, T2 + 4 * k:T2 + 4 * k + 4],
                gsm[:, 4 * k:4 * k + 4]))
        for k in range(8):
            P.op("vector", lambda e, k=k: e.reduce_max(gsm1[:, M2 + k:M2 + k + 1],
                                                       gsm[:, T1 + 4 * k:T1 + 4 * k + 4], axis=AX.X))
        for k in range(8):
            w = [(sc, eev)] if k == 0 else []
            P.op("vector", lambda e, k=k: e.tensor_scalar(
                gsm[:, T2 + 4 * k:T2 + 4 * k + 4], gsm[:, 4 * k:4 * k + 4],
                gsm1[:, M2 + k:M2 + k + 1], None, ALU.is_ge), waits=w)
        for k in range(8):
            P.op("vector", lambda e, k=k: e.tensor_mul(
                gsm[:, T1 + 4 * k:T1 + 4 * k + 4], gsm[:, GE + 4 * k:GE + 4 * k + 4],
                gsm[:, T2 + 4 * k:T2 + 4 * k + 4]))
        for k in range(8):
            P.op("vector", lambda e, k=k: e.reduce_sum(gsm1[:, ZS + k:ZS + k + 1],
                                                       gsm[:, T1 + 4 * k:T1 + 4 * k + 4], axis=AX.X))
        for k in range(8):
            P.op("vector", lambda e, k=k: e.reciprocal(gsm1[:, RZ + k:RZ + k + 1],
                                                       gsm1[:, ZS + k:ZS + k + 1]))
        for k in range(8):
            P.op("vector", lambda e, k=k: e.tensor_scalar_mul(
                gsm1[:, RZ4 + k:RZ4 + k + 1], gsm1[:, RZ + k:RZ + k + 1], 4.0))
        gv = None
        for k in range(8):
            gv = P.op("vector", lambda e, k=k: e.tensor_scalar_mul(
                gsm[:, GE + 4 * k:GE + 4 * k + 4], gsm[:, T1 + 4 * k:T1 + 4 * k + 4],
                gsm1[:, RZ4 + k:RZ4 + k + 1]), inc=(ve if k == 7 else None))
        prev_gt = 0
        for k in range(8):
            own, s = divmod(k, 4)
            t2m = P.op("tensor", lambda e, k=k: e.matmul(gtps[:, :], gsm[:, GE + 4 * k:GE + 4 * k + 4],
                                                         id_sb[:, :], start=True, stop=True),
                       waits=[(ve, gv), (ve, prev_gt)], inc=mm)
            prev_gt = P.op("vector", lambda e, own=own, s=s: e.tensor_copy(
                gt_sb[:, own * 512 + s * 128: own * 512 + (s + 1) * 128], gtps[:, :]),
                waits=[(mm, t2m)], inc=ve)
        for own in range(2):
            gbm = P.op("tensor", lambda e, own=own: e.matmul(
                gbps[:, :], e2g_sb[:, :], gt_sb[:, own * 512:(own + 1) * 512], start=True, stop=True),
                waits=[(ve, ve.n)], inc=mm)
            gbc = P.op("vector", lambda e: e.tensor_copy(gb_sb, gbps[:, :]), waits=[(mm, gbm)], inc=ve)
            P.op("vector", lambda e, own=own: e.tensor_tensor(
                mfin[0:64, own * 512:(own + 1) * 512], maug[0:64, own * 512:(own + 1) * 512],
                gb_sb, ALU.mult), waits=[(ve, gbc)], inc=ve)
    ffn_done_ve = ve.n

    s1.close()

    ctx = E(nc.sbuf_tensor("ctx", [128, NQH * 2 * 512], BF16))
    sA = contextlib.ExitStack()
    EA = sA.enter_context
    qdiag = EA(nc.sbuf_tensor("qdiag", [128, 2 * 1024], BF16))
    ptb = EA(nc.sbuf_tensor("ptb", [128, 2 * 512], BF16))
    rcp = EA(nc.sbuf_tensor("rcp", [128, 512], F32))
    sps = [EA(nc.psum_tensor(f"sps{j}", [128, 512], F32)) for j in range(2)]
    ctxA_ps = EA(nc.psum_tensor("ctxA_ps", [128, 512], F32))
    ctxB_ps = EA(nc.psum_tensor("ctxB_ps", [128, 512], F32))
    sumA_ps = EA(nc.psum_tensor("sumA_ps", [128, 512], F32))
    sumB_ps = EA(nc.psum_tensor("sumB_ps", [128, 512], F32))

    P.barrier(["gpsimd"], [cc])
    P.barrier(["tensor", "scalar", "vector", "gpsimd", "sync"], [ve, mm, sc])

    expd = {}
    ptfree = {}
    sfree = {}
    gath = {}
    qgat = {}
    slotcons = {}
    tcons = {}
    ctxfree = {}
    g = 0

    for kvh in range(NKVH):
        bk = kvh % 2
        for p in range(NSLOT):
            w = [(mm, slotcons.get(kvh - 1, 0))]
            P.op("gpsimd", lambda e, kvh=kvh, p=p, bk=bk: e.indirect_dma_start(
                kslot[:, p * 512:(p + 1) * 512], None,
                ag_k, IndirectOffsetOnAxis(kidx_sb[:, kvh * NSLOT + p: kvh * NSLOT + p + 1], 0)),
                waits=w, inc=gd, inc_by=16)
            for kt in range(4):
                col = kvh * NSLOT * 4 + p * 4 + kt
                P.op("gpsimd", lambda e, col=col, bk=bk, p=p, kt=kt: e.indirect_dma_start(
                    vslotA[:, p * 512 + kt * 128: p * 512 + (kt + 1) * 128],
                    None, ag_v, IndirectOffsetOnAxis(vai_sb[:, col:col + 1], 0)),
                    inc=gd, inc_by=16)
                gath[kvh] = P.op("gpsimd", lambda e, col=col, bk=bk, p=p, kt=kt: e.indirect_dma_start(
                    vslotB[:, p * 512 + kt * 128: p * 512 + (kt + 1) * 128],
                    None, ag_v, IndirectOffsetOnAxis(vbi_sb[:, col:col + 1], 0)),
                    inc=gd, inc_by=16)

        for qq in range(4):
            qh = kvh * 4 + qq
            t = qh
            bt = t % 2
            qd0 = P.op("sync", lambda e, qh=qh, bt=bt: e.dma_start(
                out=qdiag[:, bt * 1024: bt * 1024 + 512],
                in_=qsp[qh * 256: qh * 256 + 256: 2, :]),
                waits=[(sd, qspw[(qh, 0)]), (mm, tcons.get(t - 1, 0))], inc=ld, inc_by=16)
            qd1 = P.op("sync", lambda e, qh=qh, bt=bt: e.dma_start(
                out=qdiag[:, bt * 1024 + 512: bt * 1024 + 1024],
                in_=qsp[qh * 256 + 1: qh * 256 + 256: 2, :]),
                waits=[(sd, qspw[(qh, 1)])], inc=ld, inc_by=16)
            for p in range(NSLOT):
                qgat[t] = P.op("gpsimd", lambda e, qh=qh, p=p, bt=bt: e.indirect_dma_start(
                    qslot[:, p * 512:(p + 1) * 512], None,
                    qsp[:, :],
                    IndirectOffsetOnAxis(qix_sb[:, qh * NSLOT + p: qh * NSLOT + p + 1], 0)),
                    waits=[(sd, qspw[(qh, 1)]), (mm, tcons.get(t - 1, 0))], inc=gd, inc_by=16)

            for own in range(2):
                own_ps_c = ctxA_ps if own == 0 else ctxB_ps
                own_ps_s = sumA_ps if own == 0 else sumB_ps
                for kt in range(4):
                    r = g % 2
                    n0 = kt * 128
                    smm = P.op("tensor", lambda e, r=r, kvh=kvh, own=own, kt=kt, n0=n0, bt=bt: e.matmul(
                        sps[r][:, n0:512],
                        k_own[:, kvh * 1024 + own * 512 + kt * 128: kvh * 1024 + own * 512 + (kt + 1) * 128],
                        qdiag[:, bt * 1024 + own * 512 + n0: bt * 1024 + (own + 1) * 512],
                        start=True, stop=True, skip_group_check=True),
                        waits=[(sc, sfree.get(g - 2, 0)), (ld, qd1 if own == 0 and kt == 0 else 0),
                               (ve, ctxfree.get(t - 1, 0) if kt == 0 else 0)],
                        inc=mm)
                    tad = P.op("vector", lambda e, r=r, n0=n0: e.tensor_tensor(
                        sps[r][:, n0:n0 + 128], sps[r][:, n0:n0 + 128], tri_sb[:, :], ALU.add),
                        waits=[(mm, smm)], inc=ve)
                    expd[g] = P.op("scalar", lambda e, r=r, n0=n0: e.activation(
                        ptb[:, r * 512 + n0: (r + 1) * 512], sps[r][:, n0:512], AF.Exp,
                        bias=biasv[:, 0:1], scale=SCALE),
                        waits=[(ve, tad), (mm, ptfree.get(g - 2, 0))], inc=sc)
                    sfree[g] = expd[g]
                    P.op("tensor", lambda e, r=r, n0=n0, ps=own_ps_s, kt=kt: e.matmul(
                        ps[:, n0:512], ones128[:, :], ptb[:, r * 512 + n0: (r + 1) * 512],
                        start=(kt == 0), stop=False, skip_group_check=True),
                        waits=[(sc, expd[g])])
                    ptfree[g] = P.op("tensor", lambda e, r=r, n0=n0, ps=own_ps_c, own=own, kt=kt, kvh=kvh: e.matmul(
                        ps[:, n0:512],
                        v_own[:, (own * 4 + kt) * 1024 + kvh * 128: (own * 4 + kt) * 1024 + (kvh + 1) * 128],
                        ptb[:, r * 512 + n0: (r + 1) * 512],
                        start=(kt == 0), stop=False, skip_group_check=True),
                        inc=mm)
                    g += 1

            for p in range(NSLOT):
                for kt in range(4):
                    r = g % 2
                    smm = P.op("tensor", lambda e, r=r, bk=bk, bt=bt, p=p, kt=kt: e.matmul(
                        sps[r][:, :],
                        kslot[:, p * 512 + kt * 128: p * 512 + (kt + 1) * 128],
                        qslot[:, p * 512:(p + 1) * 512],
                        start=True, stop=True, skip_group_check=True),
                        waits=[(sc, sfree.get(g - 2, 0)), (gd, gath[kvh]), (gd, qgat[t])],
                        inc=mm)
                    expd[g] = P.op("scalar", lambda e, r=r: e.activation(
                        ptb[:, r * 512:(r + 1) * 512], sps[r][:, :], AF.Exp,
                        bias=biasv[:, 0:1], scale=SCALE),
                        waits=[(mm, smm), (mm, ptfree.get(g - 2, 0))], inc=sc)
                    sfree[g] = expd[g]
                    P.op("tensor", lambda e, r=r, p=p: e.matmul(
                        sumA_ps[:, :], flagA[:, p * 128:(p + 1) * 128], ptb[:, r * 512:(r + 1) * 512],
                        start=False, stop=False, skip_group_check=True),
                        waits=[(sc, expd[g])])
                    P.op("tensor", lambda e, r=r, p=p: e.matmul(
                        sumB_ps[:, :], flagB[:, p * 128:(p + 1) * 128], ptb[:, r * 512:(r + 1) * 512],
                        start=False, stop=False, skip_group_check=True))
                    P.op("tensor", lambda e, r=r, bk=bk, p=p, kt=kt: e.matmul(
                        ctxA_ps[:, :],
                        vslotA[:, p * 512 + kt * 128: p * 512 + (kt + 1) * 128],
                        ptb[:, r * 512:(r + 1) * 512],
                        start=False, stop=False, skip_group_check=True))
                    ptfree[g] = P.op("tensor", lambda e, r=r, bk=bk, p=p, kt=kt: e.matmul(
                        ctxB_ps[:, :],
                        vslotB[:, p * 512 + kt * 128: p * 512 + (kt + 1) * 128],
                        ptb[:, r * 512:(r + 1) * 512],
                        start=False, stop=False, skip_group_check=True),
                        inc=mm)
                    g += 1
            slotcons[kvh] = ptfree[g - 1]
            tcons[t] = ptfree[g - 1]

            last_mm = ptfree[g - 1]
            fin = None
            for own in range(2):
                own_ps_c = ctxA_ps if own == 0 else ctxB_ps
                own_ps_s = sumA_ps if own == 0 else sumB_ps
                P.op("vector", lambda e, ps=own_ps_s: e.reciprocal(rcp[:, :], ps[:, :]),
                     waits=[(mm, last_mm)])
                fin = P.op("vector", lambda e, ps=own_ps_c, t=t, own=own: e.tensor_tensor(
                    ctx[:, (t * 2 + own) * 512:(t * 2 + own + 1) * 512], ps[:, :], rcp[:, :], ALU.mult),
                    inc=ve)
            ctxfree[t] = fin
    attn_ve = ve.n

    sA.close()

    sW = contextlib.ExitStack()
    EW = sW.enter_context
    wof = k_own[:, :].bitcast(F32)
    wbbf = v_own[0:64, 0:4096]
    wobf = EW(nc.sbuf_tensor("wobf", [128, 2 * 4096], BF16))
    xres = EW(nc.sbuf_tensor("xres", [128, 2 * 1024], F32))
    osb = EW(nc.sbuf_tensor("osb", [128, 2 * 1024], F32))
    ops_ = [EW(nc.psum_tensor(f"ops{j}", [128, 1024], F32)) for j in range(2)]

    P.barrier(["tensor", "vector", "sync"], [ve, mm])

    wbl = dma("sync", wof[0:64, :].rearrange("p (o j) -> p o j", j=128),
              wb.ap().rearrange("o e j -> e o j"), ld)
    P.op("vector", lambda e: e.tensor_copy(wbbf, wof[0:64, :]), waits=[(ld, wbl)], inc=ve)
    wbc = P.op("vector", lambda e: e.tensor_copy(spcr[:, :], spcr[:, :]), inc=ve)

    ocast = {}
    oevac = {}
    ommd = {}
    ostw = {}
    for oc in range(NQH):
        b = oc % 2
        prevc = ocast.get(oc - 1, wbc)
        cpiece = prevc
        for qtr in range(2):
            lq = dma("sync", wof[:, qtr * 2048:(qtr + 1) * 2048],
                     wo[oc][:, qtr * 2048:(qtr + 1) * 2048], ld,
                     waits=[(ve, cpiece)])
            cpiece = P.op("vector", lambda e, b=b, qtr=qtr: e.tensor_copy(
                wobf[:, b * 4096 + qtr * 2048: b * 4096 + (qtr + 1) * 2048],
                wof[:, qtr * 2048:(qtr + 1) * 2048]),
                waits=[(ld, lq), (mm, ommd.get(oc - 2, 0) if qtr == 0 else 0)], inc=ve)
        ocast[oc] = P.op("vector", lambda e: e.tensor_copy(spcr[:, :], spcr[:, :]), inc=ve)
        xl0 = dma("sync", xres[:, b * 1024: b * 1024 + 512], xTA[oc * 128:(oc + 1) * 128, :], ld,
                  waits=[(ve, oevac.get(oc - 2, 0))])
        xl1 = dma("sync", xres[:, b * 1024 + 512: b * 1024 + 1024], xTB[oc * 128:(oc + 1) * 128, :], ld)
        omm = None
        for own in range(2):
            for ccc in range(NF):
                w = [(ve, ocast[oc]), (ve, attn_ve), (ve, wbc)]
                if ccc == 0 and own == 0:
                    w.append((ve, oevac.get(oc - 2, 0)))
                P.op("tensor", lambda e, oc=oc, own=own, ccc=ccc, b=b: e.matmul(
                    ops_[b][:, own * 512:(own + 1) * 512],
                    wobf[:, b * 4096 + ccc * 128: b * 4096 + (ccc + 1) * 128],
                    ctx[:, (ccc * 2 + own) * 512:(ccc * 2 + own + 1) * 512],
                    start=(ccc == 0), stop=False, skip_group_check=True),
                    waits=w)
            omm = P.op("tensor", lambda e, oc=oc, own=own, b=b: e.matmul(
                ops_[b][:, own * 512:(own + 1) * 512],
                wbbf[:, oc * 128:(oc + 1) * 128],
                mfin[0:64, own * 512:(own + 1) * 512],
                start=False, stop=True, skip_group_check=True),
                waits=[(ve, ffn_done_ve)], inc=mm)
        ommd[oc] = omm
        ev = []
        for own in range(2):
            ev.append(P.op("vector", lambda e, own=own, b=b: e.tensor_tensor(
                osb[:, b * 1024 + own * 512: b * 1024 + (own + 1) * 512],
                ops_[b][:, own * 512:(own + 1) * 512],
                xres[:, b * 1024 + own * 512: b * 1024 + (own + 1) * 512], ALU.add),
                waits=[(mm, omm), (ld, xl1), (st, ostw.get(oc - 2, 0))], inc=ve))
        oevac[oc] = ev[-1]
        P.op("sync", lambda e, oc=oc, b=b: e.dma_start(
            out=outA[oc * 128:(oc + 1) * 128, :], in_=osb[:, b * 1024: b * 1024 + 512]),
            waits=[(ve, ev[0])], inc=st, inc_by=16)
        ostw[oc] = P.op("sync", lambda e, oc=oc, b=b: e.dma_start(
            out=outB[oc * 128:(oc + 1) * 128, :], in_=osb[:, b * 1024 + 512: b * 1024 + 1024]),
            waits=[(ve, ev[1])], inc=st, inc_by=16)

    P.op("sync", lambda e: None, waits=[(st, st.n)])

    with (nc.semaphore("s_ld") as h_ld, nc.semaphore("s_gd") as h_gd,
          nc.semaphore("s_mm") as h_mm, nc.semaphore("s_ve") as h_ve,
          nc.semaphore("s_sc") as h_sc, nc.semaphore("s_cc") as h_cc,
          nc.semaphore("s_st") as h_st, nc.semaphore("s_sd") as h_sd,
          nc.semaphore("s_io") as h_io, nc.Block() as block):
        H = {"ld": h_ld, "gd": h_gd, "mm": h_mm, "ve": h_ve, "sc": h_sc, "cc": h_cc,
             "st": h_st, "sd": h_sd, "io": h_io}

        def runner(name):
            def run(eng):
                for fn, ws, inc in P.ops[name]:
                    for s, v in ws:
                        eng.wait_ge(H[s.name], v)
                    inst = fn(eng) if fn is not None else None
                    if inc is not None:
                        if inst is not None:
                            inst.then_inc(H[inc[0].name], inc[1])
                        else:
                            eng.sem_inc(H[inc[0].name], inc[1])
            return run

        block.sync(runner("sync"))
        block.gpsimd(runner("gpsimd"))
        block.vector(runner("vector"))
        block.scalar(runner("scalar"))
        block.tensor(runner("tensor"))

    sW.close()
    stack.close()
    return nc



def _stage_shared(Wq, Wk, Wv, Wo, router_w, A, Bm):
    perm = np.empty(128, np.int64)
    perm[0:64] = np.arange(64) * 2
    perm[64:128] = np.arange(64) * 2 + 1
    wq = np.ascontiguousarray(Wq.reshape(NQH, 128, NF, 128)[:, perm].transpose(0, 3, 2, 1)
                              .reshape(NQH, 128, D))
    wk = np.ascontiguousarray(Wk.reshape(NKVH, 128, NF, 128)[:, perm].transpose(0, 3, 2, 1)
                              .reshape(NKVH, 128, D))
    wv = np.ascontiguousarray(Wv.reshape(2, 512, NF, 128).transpose(0, 2, 3, 1))
    wo = np.ascontiguousarray(Wo.reshape(NQH, 128, NF, 128).transpose(0, 3, 2, 1)
                              .reshape(NQH, 128, D))
    a_aug = np.concatenate([A.reshape(64, D), router_w], axis=0)
    wa = np.ascontiguousarray(a_aug.reshape(68, NF, 128).transpose(1, 2, 0))
    wb = np.ascontiguousarray(Bm.reshape(4, 32, 128, 16).transpose(1, 0, 3, 2).reshape(32, 64, 128))
    tri = np.where(np.arange(128)[:, None] <= np.arange(128)[None, :], 0.0, -1e9).astype(np.float32)
    ident = np.eye(128, dtype=np.float32)
    e2g = np.zeros((4, 64), np.float32)
    for ee in range(4):
        e2g[ee, ee * 16:(ee + 1) * 16] = 1.0
    return wq, wk, wv, wo, wa, wb, tri, ident, e2g


def _stage_core(c):
    cA, cB = c, 7 - c
    pos = np.stack([512 * cA + np.arange(CH, dtype=np.float32),
                    512 * cB + np.arange(CH, dtype=np.float32)]).astype(np.float32)
    slots = []
    for p in range(NSLOT):
        if p < cA:
            slots.append((p, 0, 0))
        else:
            j = p - cA
            slots.append((7 - j, 1, 1))
    dims = np.arange(128, dtype=np.int64)
    kidx = np.zeros((128, NKVH * NSLOT), np.int32)
    vai = np.zeros((128, NKVH * NSLOT * 4), np.int32)
    vbi = np.zeros((128, NKVH * NSLOT * 4), np.int32)
    qix = np.zeros((128, NQH * NSLOT), np.int32)
    fix = np.zeros((128, 2 * NSLOT), np.int32)
    ZROW = NCORE * 4096 * 4
    for p, (rank, seg, own) in enumerate(slots):
        for qh in range(NQH):
            qix[:, qh * NSLOT + p] = qh * 256 + dims * 2 + own
        fix[:, 0 * NSLOT + p] = 0 if own == 0 else 32
        fix[:, 1 * NSLOT + p] = 0 if own == 1 else 32
        for kvh in range(NKVH):
            kidx[:, kvh * NSLOT + p] = rank * 4096 + kvh * 256 + dims * 2 + seg
            for kt in range(4):
                col = kvh * NSLOT * 4 + p * 4 + kt
                vrow = rank * 16384 + 8192 + kvh * 1024 + seg * 512 + kt * 128 + dims
                if own == 0:
                    vai[:, col] = vrow
                    vbi[:, col] = ZROW + dims
                else:
                    vai[:, col] = ZROW + dims
                    vbi[:, col] = vrow
    return pos, kidx, vai, vbi, qix, fix


def kernel(L0_attn_hidden_states, L0_attn_Wq, L0_attn_Wk, L0_attn_Wv, L0_attn_Wo,
           L0_ffn_router_w, L0_ffn_A, L0_ffn_B):
    x = np.asarray(L0_attn_hidden_states, dtype=np.float32)
    wq, wk, wv, wo, wa, wb, tri, ident, e2g = _stage_shared(
        np.asarray(L0_attn_Wq, np.float32), np.asarray(L0_attn_Wk, np.float32),
        np.asarray(L0_attn_Wv, np.float32), np.asarray(L0_attn_Wo, np.float32),
        np.asarray(L0_ffn_router_w, np.float32), np.asarray(L0_ffn_A, np.float32),
        np.asarray(L0_ffn_B, np.float32))

    in_maps = []
    for c in range(NCORE):
        cA, cB = c, 7 - c
        pos, kidx, vai, vbi, qix, fix = _stage_core(c)
        in_maps.append({
            "xTA": np.ascontiguousarray(x[0, 512 * cA:512 * (cA + 1), :].T),
            "xTB": np.ascontiguousarray(x[1, 512 * cB:512 * (cB + 1), :].T),
            "wq": wq, "wk": wk, "wv": wv, "wo": wo, "wa": wa, "wb": wb,
            "tri": tri, "ident": ident, "e2g": e2g, "pos": pos,
            "kidx": kidx, "vai": vai, "vbi": vbi, "qix": qix, "fix": fix,
        })

    nc = build_program()
    res = run_bass_kernel_spmd(nc, in_maps, core_ids=list(range(NCORE)))

    out = np.empty((2, 4096, 4096), dtype=np.float32)
    for c in range(NCORE):
        cA, cB = c, 7 - c
        out[0, 512 * cA:512 * (cA + 1), :] = np.asarray(res.results[c]["outA"]).T
        out[1, 512 * cB:512 * (cB + 1), :] = np.asarray(res.results[c]["outB"]).T
    return out


# version 07
# speedup vs baseline: 1.1789x; 1.1789x over previous
import contextlib
import numpy as np
import ml_dtypes
import concourse.bass as bass
import concourse.mybir as mybir
from concourse.bass import IndirectOffsetOnAxis
from concourse.bass_utils import run_bass_kernel_spmd

F32 = mybir.dt.float32
BF16 = mybir.dt.bfloat16
I32 = mybir.dt.int32
AF = mybir.ActivationFunctionType
ALU = mybir.AluOpType
AX = mybir.AxisListType

NCORE = 8
D = 4096
NF = 32
CH = 512
NQH, NKVH, HD = 32, 8, 128
NSLOT = 7
SCALE = 1.0 / float(np.sqrt(128.0))
EXPB = -35.0
TWO_PI = 6.283185307179586
MAGIC = 12582912.0
LN10K = float(np.log(10000.0))


class SemT:
    def __init__(self, name):
        self.name = name
        self.n = 0

class Plan:
    ENGS = ("sync", "gpsimd", "vector", "scalar", "tensor")

    def __init__(self):
        self.ops = {e: [] for e in self.ENGS}
        self.lastwait = {}

    def op(self, eng, fn, waits=(), inc=None, inc_by=1):
        ws = []
        for s, v in waits:
            if v <= 0:
                continue
            key = (eng, s.name)
            if self.lastwait.get(key, -1) >= v:
                continue
            self.lastwait[key] = v
            ws.append((s, v))
        tgt = None
        if inc is not None:
            inc.n += inc_by
            tgt = inc.n
        self.ops[eng].append((fn, ws, (inc, inc_by) if inc is not None else None))
        return tgt

    def barrier(self, engs, sems, exclude=()):
        for e in engs:
            for s in sems:
                if s.name in exclude:
                    continue
                v = s.n
                if v <= 0:
                    continue
                key = (e, s.name)
                if self.lastwait.get(key, -1) >= v:
                    continue
                self.lastwait[key] = v
                self.ops[e].append((None, [(s, v)], None))



def build_program():
    nc = bass.Bass("TRN2", target_bir_lowering=False, detect_race_conditions=False)

    xTA = nc.dram_tensor("xTA", [D, CH], F32, kind="ExternalInput")
    xTB = nc.dram_tensor("xTB", [D, CH], F32, kind="ExternalInput")
    wq = nc.dram_tensor("wq", [NQH, 128, D], F32, kind="ExternalInput")
    wk = nc.dram_tensor("wk", [NKVH, 128, D], F32, kind="ExternalInput")
    wv = nc.dram_tensor("wv", [2, NF, 128, 512], F32, kind="ExternalInput")
    wo = nc.dram_tensor("wo", [NQH, 128, D], F32, kind="ExternalInput")
    wa = nc.dram_tensor("wa", [NF, 128, 68], F32, kind="ExternalInput")
    wb = nc.dram_tensor("wb", [32, 64, 128], F32, kind="ExternalInput")
    tri_d = nc.dram_tensor("tri", [128, 128], F32, kind="ExternalInput")
    id_d = nc.dram_tensor("ident", [128, 128], F32, kind="ExternalInput")
    e2g_d = nc.dram_tensor("e2g", [4, 64], F32, kind="ExternalInput")
    pos_d = nc.dram_tensor("pos", [2, CH], F32, kind="ExternalInput")
    kidx_d = nc.dram_tensor("kidx", [128, NKVH * NSLOT], I32, kind="ExternalInput")
    vai_d = nc.dram_tensor("vai", [128, NKVH * NSLOT * 4], I32, kind="ExternalInput")
    vbi_d = nc.dram_tensor("vbi", [128, NKVH * NSLOT * 4], I32, kind="ExternalInput")
    qix_d = nc.dram_tensor("qix", [128, NQH * NSLOT], I32, kind="ExternalInput")
    fix_d = nc.dram_tensor("fix", [128, 2 * NSLOT], I32, kind="ExternalInput")

    outA = nc.dram_tensor("outA", [D, CH], F32, kind="ExternalOutput")
    outB = nc.dram_tensor("outB", [D, CH], F32, kind="ExternalOutput")

    ag_in = nc.dram_tensor("ag_in", [4096, 512], BF16)
    ag_out = nc.dram_tensor("ag_out", [NCORE * 4096 + 32, 512], BF16)
    qsp = nc.dram_tensor("qsp", [NQH * 256, 512], BF16)
    c2 = nc.dram_tensor("c2", [33, 128], F32)

    ag_k = ag_out.ap()
    ag_v = ag_out.ap().rearrange("r (a b) -> (r a) b", a=4, b=128)

    P = Plan()
    ld = SemT("ld")
    gd = SemT("gd")
    mm = SemT("mm")
    ve = SemT("ve")
    sc = SemT("sc")
    cc = SemT("cc")
    st = SemT("st")
    sd = SemT("sd")
    io = SemT("io")
    SEMS = [ld, gd, mm, ve, sc, cc, st, sd, io]

    stack = contextlib.ExitStack()
    E = stack.enter_context

    kslot = E(nc.sbuf_tensor("kslot", [128, NSLOT * 512], BF16))
    vslotA = E(nc.sbuf_tensor("vslotA", [128, NSLOT * 512], BF16))
    vslotB = E(nc.sbuf_tensor("vslotB", [128, NSLOT * 512], BF16))
    qslot = E(nc.sbuf_tensor("qslot", [128, 2 * NSLOT * 512], BF16))
    flg_f = E(nc.sbuf_tensor("flg_f", [128, 128], F32))
    spcr = E(nc.sbuf_tensor("spcr", [128, 8], F32))
    cosq = E(nc.sbuf_tensor("cosq", [128, 1024], F32))
    sinq = E(nc.sbuf_tensor("sinq", [128, 1024], F32))
    k_own = E(nc.sbuf_tensor("k_own", [128, NKVH * 1024], BF16))
    v_own = E(nc.sbuf_tensor("v_own", [128, 8 * 1024], BF16))
    mfin = E(nc.sbuf_tensor("mfin", [128, 1024], BF16))
    flagA = E(nc.sbuf_tensor("flagA", [128, NSLOT * 128], BF16))
    flagB = E(nc.sbuf_tensor("flagB", [128, NSLOT * 128], BF16))
    tri_sb = E(nc.sbuf_tensor("tri_sb", [128, 128], F32))
    id_sb = E(nc.sbuf_tensor("id_sb", [128, 128], F32))
    e2g_sb = E(nc.sbuf_tensor("e2g_sb", [4, 64], F32))
    kidx_sb = E(nc.sbuf_tensor("kidx_sb", [128, NKVH * NSLOT], I32))
    vai_sb = E(nc.sbuf_tensor("vai_sb", [128, NKVH * NSLOT * 4], I32))
    vbi_sb = E(nc.sbuf_tensor("vbi_sb", [128, NKVH * NSLOT * 4], I32))
    qix_sb = E(nc.sbuf_tensor("qix_sb", [128, NQH * NSLOT], I32))
    fix_sb = E(nc.sbuf_tensor("fix_sb", [128, 2 * NSLOT], I32))
    ones128 = E(nc.sbuf_tensor("ones128", [128, 128], BF16))
    onesinv = E(nc.sbuf_tensor("onesinv", [128, 128], BF16))
    biasv = E(nc.sbuf_tensor("biasv", [128, 1], F32))
    epsv = E(nc.sbuf_tensor("epsv", [128, 1], F32))
    zrow = E(nc.sbuf_tensor("zrow", [32, 512], BF16))
    c2_sb = E(nc.sbuf_tensor("c2_sb", [33, 128], F32))

    def dma(eng, out_ap, in_ap, sem, waits=()):
        return P.op(eng, lambda e, o=out_ap, i=in_ap: e.dma_start(out=o, in_=i),
                    waits=waits, inc=sem, inc_by=16)

    t0 = []
    t0.append(dma("sync", tri_sb[:, :], tri_d[:, :], ld))
    t0.append(dma("sync", id_sb[:, :], id_d[:, :], ld))
    t0.append(dma("sync", e2g_sb[:, :], e2g_d[:, :], ld))
    t0.append(dma("sync", kidx_sb[:, :], kidx_d[:, :], ld))
    t0.append(dma("sync", vai_sb[:, :], vai_d[:, :], ld))
    t0.append(dma("sync", vbi_sb[:, :], vbi_d[:, :], ld))
    t0.append(dma("sync", qix_sb[:, :], qix_d[:, :], ld))
    t0.append(dma("sync", fix_sb[:, :], fix_d[:, :], ld))
    tbl_ld = ld.n

    mset = []
    P.op("vector", lambda e: e.memset(ones128[:, :], 1.0))
    P.op("vector", lambda e: e.memset(onesinv[:, :], 1.0 / 4096.0))
    P.op("vector", lambda e: e.memset(biasv[:, :], EXPB))
    P.op("vector", lambda e: e.memset(epsv[:, :], 1e-5))
    P.op("vector", lambda e: e.memset(zrow[:, :], 0.0))
    P.op("vector", lambda e: e.memset(c2_sb[:, :], 0.0))
    msets_done = P.op("vector", lambda e: e.memset(c2_sb[0:1, :], 1.0), inc=ve)

    c2w = dma("scalar", c2[:, :], c2_sb[:, :], sd, waits=[(ve, msets_done)])
    zpw = dma("scalar", ag_out[NCORE * 4096: NCORE * 4096 + 32, :], zrow[:, :], sd,
              waits=[(ve, msets_done)])

    flg_done = []
    for t in range(2):
        for p in range(NSLOT):
            g = P.op("gpsimd",
                     lambda e, t=t, p=p: e.indirect_dma_start(
                         flg_f[:, :], None, c2[:, :],
                         IndirectOffsetOnAxis(fix_sb[:, t * NSLOT + p: t * NSLOT + p + 1], 0)),
                     waits=[(ld, tbl_ld), (sd, c2w), (ve, flg_done[-1] if flg_done else 0)],
                     inc=gd, inc_by=16)
            dstf = flagA if t == 0 else flagB
            flg_done.append(P.op("vector",
                                 lambda e, d=dstf, p=p: e.tensor_copy(d[:, p * 128:(p + 1) * 128], flg_f[:, :]),
                                 waits=[(gd, g)], inc=ve))
    flags_ve = ve.n

    with contextlib.ExitStack() as ph:
        EE = ph.enter_context
        invf = EE(nc.sbuf_tensor("invf", [64, 1], F32))
        posr = EE(nc.sbuf_tensor("posr", [64, 512], F32))
        th = EE(nc.sbuf_tensor("th", [64, 512], F32))
        u1 = EE(nc.sbuf_tensor("u1", [64, 512], F32))
        n1 = EE(nc.sbuf_tensor("n1", [64, 512], F32))
        r1 = EE(nc.sbuf_tensor("r1", [64, 512], F32))
        P.op("gpsimd", lambda e: e.iota(invf[:, :], pattern=[[0, 1]], channel_multiplier=1,
                                        allow_small_or_imprecise_dtypes=True), inc=io, inc_by=1)
        iv = P.op("scalar", lambda e: e.activation(invf[:, :], invf[:, :], AF.Exp, scale=-LN10K / 64.0),
                  waits=[(io, 1)], inc=sc)
        prev_tv = 0
        prev_tr = 0
        for own in range(2):
            pr = P.op("scalar",
                      lambda e, own=own: e.dma_start(out=posr[:, :],
                                                     in_=pos_d[own:own + 1, :].broadcast_to([64, CH])),
                      waits=[(ld, tbl_ld), (ve, prev_tv)], inc=sd, inc_by=16)
            tv = P.op("vector", lambda e: e.tensor_scalar_mul(th[:, :], posr[:, :], invf[0:64, 0:1]),
                      waits=[(sd, pr), (sc, iv)], inc=ve)
            prev_tv = tv
            for trig, off in (("sin", 0.0), ("cos", 0.25)):
                P.op("vector", lambda e, off=off: e.tensor_scalar(u1[:, :], th[:, :], 1.0 / TWO_PI, off, ALU.mult, ALU.add),
                     waits=[(ve, tv), (sc, prev_tr)])
                P.op("vector", lambda e: e.tensor_scalar_add(n1[:, :], u1[:, :], MAGIC))
                P.op("vector", lambda e: e.tensor_scalar_add(n1[:, :], n1[:, :], -MAGIC))
                P.op("vector", lambda e: e.tensor_sub(r1[:, :], u1[:, :], n1[:, :]))
                rv = P.op("vector", lambda e: e.tensor_scalar_mul(r1[:, :], r1[:, :], TWO_PI), inc=ve)
                if trig == "sin":
                    sn = P.op("scalar", lambda e, own=own: e.activation(sinq[64:128, own * 512:(own + 1) * 512],
                                                                        r1[:, :], AF.Sin),
                              waits=[(ve, rv)], inc=sc)
                    prev_tr = sn
                    P.op("vector", lambda e, own=own: e.tensor_scalar_mul(
                        sinq[0:64, own * 512:(own + 1) * 512], sinq[64:128, own * 512:(own + 1) * 512], -1.0),
                        waits=[(sc, sn)], inc=ve)
                else:
                    cs = P.op("scalar", lambda e, own=own: e.activation(cosq[0:64, own * 512:(own + 1) * 512],
                                                                        r1[:, :], AF.Sin),
                              waits=[(ve, rv)], inc=sc)
                    prev_tr = cs
                    P.op("scalar", lambda e, own=own: e.dma_start(
                        out=cosq[64:128, own * 512:(own + 1) * 512],
                        in_=cosq[0:64, own * 512:(own + 1) * 512]),
                        waits=[(sc, cs)], inc=sd, inc_by=16)
    rope_tabs_ve = ve.n
    rope_tabs_sd = sd.n

    s1 = contextlib.ExitStack()
    E1 = s1.enter_context
    xn = E1(nc.sbuf_tensor("xn", [128, 2 * NF * 512], BF16))
    xst = E1(nc.sbuf_tensor("xst", [128, 2 * 512], F32))
    xbst = E1(nc.sbuf_tensor("xbst", [128, 2 * 512], BF16))
    xsq = E1(nc.sbuf_tensor("xsq", [128, 2 * 512], BF16))
    wf32 = E1(nc.sbuf_tensor("wf32", [128, 2 * 1024], F32))
    wbf = E1(nc.sbuf_tensor("wbf", [128, 2 * 4096], BF16))
    mean_sb = E1(nc.sbuf_tensor("mean_sb", [128, 1024], F32))[0:128]
    rstd_sb = E1(nc.sbuf_tensor("rstd_sb", [128, 1024], F32))[0:128]
    qf = E1(nc.sbuf_tensor("qf", [128, 1024], F32))[0:128]
    qs = E1(nc.sbuf_tensor("qs", [128, 1024], F32))[0:128]
    t1s = E1(nc.sbuf_tensor("t1s", [128, 1024], F32))[0:128]
    wabf = qslot[:, 0:NF * 68]
    lg_sb = vslotB[:, 0:2048].bitcast(F32)[0:4]
    gt_sb = vslotA[:, 0:2048].bitcast(F32)[0:4]
    gb_sb = kslot[:, 0:1024].bitcast(F32)[0:64]
    gsm = E1(nc.sbuf_tensor("gsm", [128, 128], F32))
    gsm1 = E1(nc.sbuf_tensor("gsm1", [128, 48], F32))

    wal = dma("sync", wf32[:, 0:2040].rearrange("p (f e) -> p f e", e=68),
              wa.ap()[0:30].rearrange("f i e -> i f e"), ld)
    wc1 = P.op("vector", lambda e: e.tensor_copy(wabf[:, 0:2040], wf32[:, 0:2040]),
               waits=[(ld, wal)], inc=ve)
    wal2 = dma("sync", wf32[:, 0:136].rearrange("p (f e) -> p f e", e=68),
               wa.ap()[30:32].rearrange("f i e -> i f e"), ld, waits=[(ve, wc1)])
    P.op("vector", lambda e: e.tensor_copy(wabf[:, 2040:2176], wf32[:, 0:136]),
         waits=[(ld, wal2)], inc=ve)
    wacast = P.op("vector", lambda e: e.tensor_copy(spcr[:, :], spcr[:, :]), inc=ve)

    with contextlib.ExitStack() as ph:
        EE = ph.enter_context
        mean_ps = EE(nc.psum_tensor("mean_ps", [128, 1024], F32))
        sq_ps = EE(nc.psum_tensor("sq_ps", [128, 1024], F32))
        xcons = {}
        xsrc = (xTA, xTB)
        for own in range(2):
            for f in range(NF):
                i = own * NF + f
                b = i % 2
                l_ = dma("sync", xst[:, b * 512:(b + 1) * 512], xsrc[own][f * 128:(f + 1) * 128, :], ld,
                         waits=[(mm, xcons.get(i - 2, 0)), (ve, wacast if i == 0 else 0)])
                cv_ = P.op("vector", lambda e, b=b: e.tensor_copy(xbst[:, b * 512:(b + 1) * 512],
                                                                  xst[:, b * 512:(b + 1) * 512]),
                           waits=[(ld, l_)], inc=ve)
                sq_ = P.op("scalar", lambda e, b=b: e.activation(xsq[:, b * 512:(b + 1) * 512],
                                                                 xst[:, b * 512:(b + 1) * 512], AF.Square),
                           waits=[(ld, l_)], inc=sc)
                P.op("tensor", lambda e, own=own, f=f, b=b: e.matmul(
                    mean_ps[:, own * 512:(own + 1) * 512], onesinv[:, :], xbst[:, b * 512:(b + 1) * 512],
                    start=(f == 0), stop=(f == NF - 1)),
                    waits=[(ve, cv_)])
                xcons[i] = P.op("tensor", lambda e, own=own, f=f, b=b: e.matmul(
                    sq_ps[:, own * 512:(own + 1) * 512], onesinv[:, :], xsq[:, b * 512:(b + 1) * 512],
                    start=(f == 0), stop=(f == NF - 1)),
                    waits=[(sc, sq_)], inc=mm)
        stats_mm = mm.n
        P.op("vector", lambda e: e.tensor_copy(mean_sb, mean_ps[:, :]), waits=[(mm, stats_mm)])
        P.op("vector", lambda e: e.tensor_mul(qf, mean_sb, mean_sb))
        vfin = P.op("vector", lambda e: e.tensor_tensor(qf, sq_ps[:, :], qf, ALU.subtract), inc=ve)
        sfin = P.op("scalar", lambda e: e.activation(qf, qf, AF.Sqrt, bias=epsv[:, 0:1]),
                    waits=[(ve, vfin)], inc=sc)
        rfin = P.op("vector", lambda e: e.reciprocal(rstd_sb, qf), waits=[(sc, sfin)], inc=ve)

        xn_done = {}
        for own in range(2):
            for f in range(NF):
                i = own * NF + f
                b = i % 2
                l_ = dma("sync", xst[:, b * 512:(b + 1) * 512], xsrc[own][f * 128:(f + 1) * 128, :], ld,
                         waits=[(ve, xn_done.get(i - 2, rfin))])
                P.op("vector", lambda e, b=b, own=own: e.tensor_tensor(
                    qf[:, 0:512], xst[:, b * 512:(b + 1) * 512], mean_sb[:, own * 512:(own + 1) * 512],
                    ALU.subtract), waits=[(ld, l_), (ve, rfin)])
                xn_done[i] = P.op("vector", lambda e, i=i, own=own: e.tensor_tensor(
                    xn[:, i * 512:(i + 1) * 512], qf[:, 0:512], rstd_sb[:, own * 512:(own + 1) * 512],
                    ALU.mult), inc=ve)
        xn_all = ve.n

    rope_state = {"last_swap": 0}

    def rope_emit(ps_ap, out_bf_ap, wait_mm, final_wait_sd=0):
        c1 = P.op("vector", lambda e: e.tensor_copy(qf, ps_ap),
                  waits=[(mm, wait_mm), (sd, rope_state["last_swap"])], inc=ve)
        s1_ = P.op("scalar", lambda e: e.dma_start(out=qs[0:64], in_=qf[64:128]),
                   waits=[(ve, c1)], inc=sd, inc_by=16)
        s2_ = P.op("scalar", lambda e: e.dma_start(out=qs[64:128], in_=qf[0:64, :]),
                   inc=sd, inc_by=16)
        rope_state["last_swap"] = s2_
        P.op("vector", lambda e: e.tensor_mul(t1s, qf, cosq[:, :]),
             waits=[(ve, rope_tabs_ve), (sd, rope_tabs_sd)])
        P.op("vector", lambda e: e.tensor_mul(qs, qs, sinq[:, :]), waits=[(sd, s2_)])
        done = P.op("vector", lambda e: e.tensor_add(out_bf_ap, t1s, qs),
                    waits=[(sd, final_wait_sd)], inc=ve)
        return c1, done

    with contextlib.ExitStack() as ph:
        EE = ph.enter_context
        kps = [EE(nc.psum_tensor(f"kps{j}", [128, 1024], F32)) for j in range(2)]
        kfree = {}
        kagw = []
        cast_done = {}
        mm_done = {}
        def stage_k(kvh):
            b = kvh % 2
            prevc = cast_done.get(kvh - 2, wacast if kvh < 2 else 0)
            cpiece = prevc
            for qtr in range(4):
                lq = dma("sync", wf32[:, b * 1024:(b + 1) * 1024],
                         wk[kvh][:, qtr * 1024:(qtr + 1) * 1024], ld,
                         waits=[(ve, cpiece)])
                cpiece = P.op("vector", lambda e, b=b, kvh=kvh, qtr=qtr: e.tensor_copy(
                    wbf[:, b * 4096 + qtr * 1024: b * 4096 + (qtr + 1) * 1024],
                    wf32[:, b * 1024:(b + 1) * 1024]),
                    waits=[(ld, lq), (mm, mm_done.get(kvh - 2, 0) if qtr == 0 else 0)], inc=ve)
            cast_done[kvh] = P.op("vector", lambda e: e.tensor_copy(spcr[:, :], spcr[:, :]), inc=ve)

        stage_k(0)
        stage_k(1)
        for kvh in range(NKVH):
            b = kvh % 2
            for own in range(2):
                for f in range(NF):
                    w = [(ve, cast_done[kvh]), (ve, xn_all)]
                    if f == 0 and own == 0:
                        w.append((ve, kfree.get(kvh - 2, 0)))
                    last = (own == 1 and f == NF - 1)
                    mm_done[kvh] = P.op("tensor", lambda e, kvh=kvh, own=own, f=f, b=b: e.matmul(
                        kps[b][:, own * 512:(own + 1) * 512],
                        wbf[:, b * 4096 + f * 128: b * 4096 + (f + 1) * 128],
                        xn[:, (own * NF + f) * 512:(own * NF + f + 1) * 512],
                        start=(f == 0), stop=(f == NF - 1)),
                        waits=w, inc=(mm if last else None))
            if kvh + 2 < NKVH:
                stage_k(kvh + 2)
            kf, kdone = rope_emit(kps[b][:, :], k_own[:, kvh * 1024:(kvh + 1) * 1024], mm_done[kvh])
            kfree[kvh] = kf
            for own in range(2):
                kagw.append(P.op("scalar", lambda e, kvh=kvh, own=own: e.dma_start(
                    out=ag_in[kvh * 256 + own: kvh * 256 + 256: 2, :],
                    in_=k_own[:, kvh * 1024 + own * 512: kvh * 1024 + (own + 1) * 512]),
                    waits=[(ve, kdone)], inc=sd, inc_by=16))
    k_phase_ve = ve.n
    kcast_last = cast_done[NKVH - 1]

    P.barrier(["tensor"], [ve])
    with contextlib.ExitStack() as ph:
        EE = ph.enter_context
        vps = [EE(nc.psum_tensor(f"vps{j}", [128, 512], F32)) for j in range(8)]
        vagw = []
        vcast = {}
        vfcons = {}
        for h in range(2):
            vmm = None
            for f in range(NF):
                b = f % 2
                l_ = dma("sync", wf32[:, b * 1024: b * 1024 + 512],
                         wv[h][f][:, :], ld,
                         waits=[(ve, vcast.get((h, f - 2),
                                               vcast.get((0, NF - 2 + f), 0) if h == 1 else 0)),
                                (ve, kcast_last)])
                P.op("vector", lambda e, b=b: e.tensor_copy(
                    wbf[:, b * 512:(b + 1) * 512], wf32[:, b * 1024: b * 1024 + 512]),
                    waits=[(ld, l_), (mm, vfcons.get((h, f - 2),
                                                    vfcons.get((0, NF - 1), 0) if h == 1 else 0))], inc=ve)
                vcast[(h, f)] = P.op("vector", lambda e: e.tensor_copy(spcr[:, :], spcr[:, :]), inc=ve)
                for own in range(2):
                    for kt in range(4):
                        j = own * 4 + kt
                        w = [(ve, vcast[(h, f)])]
                        if f == 0:
                            w.append((ve, k_phase_ve if h == 0 else 0))
                            w.append((ve, vevac if h == 1 else 0))
                        last = (own == 1 and kt == 3)
                        vmm = P.op("tensor", lambda e, own=own, kt=kt, f=f, b=b, j=j: e.matmul(
                            vps[j][:, :],
                            xn[:, (own * NF + f) * 512 + kt * 128: (own * NF + f) * 512 + (kt + 1) * 128],
                            wbf[:, b * 512:(b + 1) * 512],
                            start=(f == 0), stop=(f == NF - 1)),
                            waits=w, inc=(mm if last else None))
                vfcons[(h, f)] = vmm
            ev = []
            for own in range(2):
                for kt in range(4):
                    j = own * 4 + kt
                    ev.append(P.op("vector", lambda e, own=own, kt=kt, j=j, h=h: e.tensor_copy(
                        v_own[:, (own * 4 + kt) * 1024 + h * 512: (own * 4 + kt) * 1024 + (h + 1) * 512],
                        vps[j][:, :]), waits=[(mm, vmm)], inc=ve))
            vevac = ve.n
            for own in range(2):
                for kt in range(4):
                    for kk in range(4):
                        kvh = h * 4 + kk
                        base = 8192 + kvh * 1024 + own * 512 + kt * 128
                        vagw.append(P.op("scalar", lambda e, own=own, kt=kt, kk=kk, base=base, h=h: e.dma_start(
                            out=ag_in.ap().rearrange("r (a b) -> (r a) b", a=4, b=128)[base: base + 128, :],
                            in_=v_own[:, (own * 4 + kt) * 1024 + h * 512 + kk * 128:
                                      (own * 4 + kt) * 1024 + h * 512 + (kk + 1) * 128]),
                            waits=[(ve, vevac)], inc=sd, inc_by=16))

    agq = P.op("gpsimd", lambda e: e.collective_compute(
        "AllGather", ALU.bypass, replica_groups=[list(range(NCORE))],
        ins=[ag_in[:, :]], outs=[ag_out[0:NCORE * 4096, :]]),
        waits=[(sd, sd.n), (gd, gd.n)], inc=cc)

    vcast_last = vcast[(1, NF - 1)]
    v_mm_last = vfcons[(1, NF - 1)]
    P.barrier(["tensor"], [ve])
    with contextlib.ExitStack() as ph:
        EE = ph.enter_context
        qps = [EE(nc.psum_tensor(f"qps{j}", [128, 1024], F32)) for j in range(2)]
        qfree = {}
        qspw = {}
        qcast = {}
        qmm = {}
        def stage_q(qh):
            b = qh % 2
            cpiece = qcast.get(qh - 2, vcast_last)
            for qtr in range(4):
                lq = dma("sync", wf32[:, b * 1024:(b + 1) * 1024],
                         wq[qh][:, qtr * 1024:(qtr + 1) * 1024], ld,
                         waits=[(ve, cpiece)])
                cpiece = P.op("vector", lambda e, b=b, qtr=qtr: e.tensor_copy(
                    wbf[:, b * 4096 + qtr * 1024: b * 4096 + (qtr + 1) * 1024],
                    wf32[:, b * 1024:(b + 1) * 1024]),
                    waits=[(ld, lq), (mm, qmm.get(qh - 2, v_mm_last) if qtr == 0 else 0)], inc=ve)
            qcast[qh] = P.op("vector", lambda e: e.tensor_copy(spcr[:, :], spcr[:, :]), inc=ve)

        stage_q(0)
        stage_q(1)
        for qh in range(NQH):
            b = qh % 2
            for own in range(2):
                for f in range(NF):
                    w = [(ve, qcast[qh])]
                    if f == 0 and own == 0:
                        w.append((ve, qfree.get(qh - 2, 0)))
                    last = (own == 1 and f == NF - 1)
                    qmm[qh] = P.op("tensor", lambda e, own=own, f=f, b=b: e.matmul(
                        qps[b][:, own * 512:(own + 1) * 512],
                        wbf[:, b * 4096 + f * 128: b * 4096 + (f + 1) * 128],
                        xn[:, (own * NF + f) * 512:(own * NF + f + 1) * 512],
                        start=(f == 0), stop=(f == NF - 1)),
                        waits=w, inc=(mm if last else None))
            if qh + 2 < NQH:
                stage_q(qh + 2)
            qfh, qdone = rope_emit(qps[b][:, :], xsq[:, :], qmm[qh],
                                   final_wait_sd=qspw.get((qh - 1, 1), 0))
            qfree[qh] = qfh
            for own in range(2):
                qspw[(qh, own)] = P.op("scalar", lambda e, qh=qh, own=own: e.dma_start(
                    out=qsp[qh * 256 + own: qh * 256 + 256: 2, :],
                    in_=xsq[:, own * 512:(own + 1) * 512]),
                    waits=[(ve, qdone)], inc=sd, inc_by=16)
    q_phase_done_sd = sd.n
    q_phase_done_ve = ve.n

    P.barrier(["tensor"], [ve])
    with contextlib.ExitStack() as ph:
        EE = ph.enter_context
        maug = EE(nc.psum_tensor("maug", [68, 1024], F32))
        smps = EE(nc.psum_tensor("smps", [128, 4], F32))
        gtps = EE(nc.psum_tensor("gtps", [4, 128], F32))
        gbps = EE(nc.psum_tensor("gbps", [64, 512], F32))
        amm = None
        for own in range(2):
            for f in range(NF):
                last = (own == 1 and f == NF - 1)
                amm = P.op("tensor", lambda e, own=own, f=f: e.matmul(
                    maug[:, own * 512:(own + 1) * 512],
                    wabf[:, f * 68:(f + 1) * 68],
                    xn[:, (own * NF + f) * 512:(own * NF + f + 1) * 512],
                    start=(f == 0), stop=(f == NF - 1)),
                    waits=[(ve, wacast), (ve, xn_all)], inc=(mm if last else None))
        lgc = P.op("vector", lambda e: e.tensor_copy(lg_sb, maug[64:68, :]), waits=[(mm, amm)], inc=ve)
        tdone = {}
        prev_lt = 0
        for k in range(8):
            own, s = divmod(k, 4)
            tmm = P.op("tensor", lambda e, own=own, s=s: e.matmul(
                smps[:, :], lg_sb[:, own * 512 + s * 128: own * 512 + (s + 1) * 128],
                id_sb[0:4, 0:4], start=True, stop=True),
                waits=[(ve, lgc), (ve, prev_lt)], inc=mm)
            prev_lt = P.op("vector", lambda e, k=k: e.tensor_copy(gsm[:, 4 * k:4 * k + 4], smps[:, :]),
                           waits=[(mm, tmm)], inc=ve)
        GL, GE, T1, T2 = 0, 32, 64, 96
        M1, NM, M2, ZS, RZ, RZ4 = 0, 8, 16, 24, 32, 40
        for k in range(8):
            P.op("vector", lambda e, k=k: e.reduce_max(gsm1[:, M1 + k:M1 + k + 1],
                                                       gsm[:, 4 * k:4 * k + 4], axis=AX.X))
        nmv = None
        for k in range(8):
            nmv = P.op("vector", lambda e, k=k: e.tensor_scalar_mul(
                gsm1[:, NM + k:NM + k + 1], gsm1[:, M1 + k:M1 + k + 1], -1.0),
                inc=(ve if k == 7 else None))
        eev = None
        for k in range(8):
            eev = P.op("scalar", lambda e, k=k: e.activation(
                gsm[:, GE + 4 * k:GE + 4 * k + 4], gsm[:, 4 * k:4 * k + 4], AF.Exp,
                bias=gsm1[:, NM + k:NM + k + 1]),
                waits=[(ve, nmv)], inc=(sc if k == 7 else None))
        for k in range(8):
            P.op("vector", lambda e, k=k: e.tensor_scalar(
                gsm[:, T1 + 4 * k:T1 + 4 * k + 4], gsm[:, 4 * k:4 * k + 4],
                gsm1[:, M1 + k:M1 + k + 1], None, ALU.is_ge))
        for k in range(8):
            P.op("vector", lambda e, k=k: e.tensor_scalar_mul(
                gsm[:, T2 + 4 * k:T2 + 4 * k + 4], gsm[:, T1 + 4 * k:T1 + 4 * k + 4], -30000.0))
        for k in range(8):
            P.op("vector", lambda e, k=k: e.tensor_add(
                gsm[:, T1 + 4 * k:T1 + 4 * k + 4], gsm[:, T2 + 4 * k:T2 + 4 * k + 4],
                gsm[:, 4 * k:4 * k + 4]))
        for k in range(8):
            P.op("vector", lambda e, k=k: e.reduce_max(gsm1[:, M2 + k:M2 + k + 1],
                                                       gsm[:, T1 + 4 * k:T1 + 4 * k + 4], axis=AX.X))
        for k in range(8):
            w = [(sc, eev)] if k == 0 else []
            P.op("vector", lambda e, k=k: e.tensor_scalar(
                gsm[:, T2 + 4 * k:T2 + 4 * k + 4], gsm[:, 4 * k:4 * k + 4],
                gsm1[:, M2 + k:M2 + k + 1], None, ALU.is_ge), waits=w)
        for k in range(8):
            P.op("vector", lambda e, k=k: e.tensor_mul(
                gsm[:, T1 + 4 * k:T1 + 4 * k + 4], gsm[:, GE + 4 * k:GE + 4 * k + 4],
                gsm[:, T2 + 4 * k:T2 + 4 * k + 4]))
        for k in range(8):
            P.op("vector", lambda e, k=k: e.reduce_sum(gsm1[:, ZS + k:ZS + k + 1],
                                                       gsm[:, T1 + 4 * k:T1 + 4 * k + 4], axis=AX.X))
        for k in range(8):
            P.op("vector", lambda e, k=k: e.reciprocal(gsm1[:, RZ + k:RZ + k + 1],
                                                       gsm1[:, ZS + k:ZS + k + 1]))
        for k in range(8):
            P.op("vector", lambda e, k=k: e.tensor_scalar_mul(
                gsm1[:, RZ4 + k:RZ4 + k + 1], gsm1[:, RZ + k:RZ + k + 1], 4.0))
        gv = None
        for k in range(8):
            gv = P.op("vector", lambda e, k=k: e.tensor_scalar_mul(
                gsm[:, GE + 4 * k:GE + 4 * k + 4], gsm[:, T1 + 4 * k:T1 + 4 * k + 4],
                gsm1[:, RZ4 + k:RZ4 + k + 1]), inc=(ve if k == 7 else None))
        prev_gt = 0
        for k in range(8):
            own, s = divmod(k, 4)
            t2m = P.op("tensor", lambda e, k=k: e.matmul(gtps[:, :], gsm[:, GE + 4 * k:GE + 4 * k + 4],
                                                         id_sb[:, :], start=True, stop=True),
                       waits=[(ve, gv), (ve, prev_gt)], inc=mm)
            prev_gt = P.op("vector", lambda e, own=own, s=s: e.tensor_copy(
                gt_sb[:, own * 512 + s * 128: own * 512 + (s + 1) * 128], gtps[:, :]),
                waits=[(mm, t2m)], inc=ve)
        for own in range(2):
            gbm = P.op("tensor", lambda e, own=own: e.matmul(
                gbps[:, :], e2g_sb[:, :], gt_sb[:, own * 512:(own + 1) * 512], start=True, stop=True),
                waits=[(ve, ve.n)], inc=mm)
            gbc = P.op("vector", lambda e: e.tensor_copy(gb_sb, gbps[:, :]), waits=[(mm, gbm)], inc=ve)
            P.op("vector", lambda e, own=own: e.tensor_tensor(
                mfin[0:64, own * 512:(own + 1) * 512], maug[0:64, own * 512:(own + 1) * 512],
                gb_sb, ALU.mult), waits=[(ve, gbc)], inc=ve)
    ffn_done_ve = ve.n

    s1.close()

    ctx = E(nc.sbuf_tensor("ctx", [128, NQH * 2 * 512], BF16))
    sA = contextlib.ExitStack()
    EA = sA.enter_context
    qdiag = EA(nc.sbuf_tensor("qdiag", [128, 2 * 1024], BF16))
    ptb = EA(nc.sbuf_tensor("ptb", [128, 3 * 512], BF16))
    rcp = EA(nc.sbuf_tensor("rcp", [128, 512], F32))
    sps = [EA(nc.psum_tensor(f"sps{j}", [128, 512], F32)) for j in range(3)]
    ctxA_ps = EA(nc.psum_tensor("ctxA_ps", [128, 512], F32))
    ctxB_ps = EA(nc.psum_tensor("ctxB_ps", [128, 512], F32))
    sumA_ps = EA(nc.psum_tensor("sumA_ps", [128, 512], F32))
    sumB_ps = EA(nc.psum_tensor("sumB_ps", [128, 512], F32))

    P.barrier(["gpsimd"], [cc])
    P.barrier(["tensor", "scalar", "vector", "gpsimd", "sync"], [ve, mm, sc])

    expd = {}
    ptfree = {}
    sfree = {}
    gath = {}
    qgat = {}
    slotcons = {}
    tcons = {}
    ctxfree = {}
    g = 0

    for kvh in range(NKVH):
        bk = kvh % 2
        for p in range(NSLOT):
            w = [(mm, slotcons.get(kvh - 1, 0))]
            P.op("gpsimd", lambda e, kvh=kvh, p=p, bk=bk: e.indirect_dma_start(
                kslot[:, p * 512:(p + 1) * 512], None,
                ag_k, IndirectOffsetOnAxis(kidx_sb[:, kvh * NSLOT + p: kvh * NSLOT + p + 1], 0)),
                waits=w, inc=gd, inc_by=16)
            for kt in range(4):
                col = kvh * NSLOT * 4 + p * 4 + kt
                P.op("gpsimd", lambda e, col=col, bk=bk, p=p, kt=kt: e.indirect_dma_start(
                    vslotA[:, p * 512 + kt * 128: p * 512 + (kt + 1) * 128],
                    None, ag_v, IndirectOffsetOnAxis(vai_sb[:, col:col + 1], 0)),
                    inc=gd, inc_by=16)
                gath[kvh] = P.op("gpsimd", lambda e, col=col, bk=bk, p=p, kt=kt: e.indirect_dma_start(
                    vslotB[:, p * 512 + kt * 128: p * 512 + (kt + 1) * 128],
                    None, ag_v, IndirectOffsetOnAxis(vbi_sb[:, col:col + 1], 0)),
                    inc=gd, inc_by=16)

        for qq in range(4):
            qh = kvh * 4 + qq
            t = qh
            bt = t % 2
            qd0 = P.op("sync", lambda e, qh=qh, bt=bt: e.dma_start(
                out=qdiag[:, bt * 1024: bt * 1024 + 512],
                in_=qsp[qh * 256: qh * 256 + 256: 2, :]),
                waits=[(sd, qspw[(qh, 0)]), (mm, tcons.get(t - 1, 0))], inc=ld, inc_by=16)
            qd1 = P.op("sync", lambda e, qh=qh, bt=bt: e.dma_start(
                out=qdiag[:, bt * 1024 + 512: bt * 1024 + 1024],
                in_=qsp[qh * 256 + 1: qh * 256 + 256: 2, :]),
                waits=[(sd, qspw[(qh, 1)])], inc=ld, inc_by=16)
            for p in range(NSLOT):
                qgat[t] = P.op("gpsimd", lambda e, qh=qh, p=p, bt=bt: e.indirect_dma_start(
                    qslot[:, (bt * NSLOT + p) * 512:(bt * NSLOT + p + 1) * 512], None,
                    qsp[:, :],
                    IndirectOffsetOnAxis(qix_sb[:, qh * NSLOT + p: qh * NSLOT + p + 1], 0)),
                    waits=[(sd, qspw[(qh, 1)]), (mm, tcons.get(t - 2, 0))], inc=gd, inc_by=16)

            for own in range(2):
                own_ps_c = ctxA_ps if own == 0 else ctxB_ps
                own_ps_s = sumA_ps if own == 0 else sumB_ps
                for kt in range(4):
                    r = g % 3
                    n0 = kt * 128
                    smm = P.op("tensor", lambda e, r=r, kvh=kvh, own=own, kt=kt, n0=n0, bt=bt: e.matmul(
                        sps[r][:, n0:512],
                        k_own[:, kvh * 1024 + own * 512 + kt * 128: kvh * 1024 + own * 512 + (kt + 1) * 128],
                        qdiag[:, bt * 1024 + own * 512 + n0: bt * 1024 + (own + 1) * 512],
                        start=True, stop=True, skip_group_check=True),
                        waits=[(sc, sfree.get(g - 3, 0)), (ld, qd1 if own == 0 and kt == 0 else 0),
                               (ve, ctxfree.get(t - 1, 0) if kt == 0 else 0)],
                        inc=mm)
                    tad = P.op("vector", lambda e, r=r, n0=n0: e.tensor_tensor(
                        sps[r][:, n0:n0 + 128], sps[r][:, n0:n0 + 128], tri_sb[:, :], ALU.add),
                        waits=[(mm, smm)], inc=ve)
                    expd[g] = P.op("scalar", lambda e, r=r, n0=n0: e.activation(
                        ptb[:, r * 512 + n0: (r + 1) * 512], sps[r][:, n0:512], AF.Exp,
                        bias=biasv[:, 0:1], scale=SCALE),
                        waits=[(ve, tad), (mm, ptfree.get(g - 3, 0))], inc=sc)
                    sfree[g] = expd[g]
                    P.op("tensor", lambda e, r=r, n0=n0, ps=own_ps_s, kt=kt: e.matmul(
                        ps[:, n0:512], ones128[:, :], ptb[:, r * 512 + n0: (r + 1) * 512],
                        start=(kt == 0), stop=False, skip_group_check=True),
                        waits=[(sc, expd[g])])
                    ptfree[g] = P.op("tensor", lambda e, r=r, n0=n0, ps=own_ps_c, own=own, kt=kt, kvh=kvh: e.matmul(
                        ps[:, n0:512],
                        v_own[:, (own * 4 + kt) * 1024 + kvh * 128: (own * 4 + kt) * 1024 + (kvh + 1) * 128],
                        ptb[:, r * 512 + n0: (r + 1) * 512],
                        start=(kt == 0), stop=False, skip_group_check=True),
                        inc=mm)
                    g += 1

            for p in range(NSLOT):
                for kt in range(4):
                    r = g % 3
                    smm = P.op("tensor", lambda e, r=r, bk=bk, bt=bt, p=p, kt=kt: e.matmul(
                        sps[r][:, :],
                        kslot[:, p * 512 + kt * 128: p * 512 + (kt + 1) * 128],
                        qslot[:, (bt * NSLOT + p) * 512:(bt * NSLOT + p + 1) * 512],
                        start=True, stop=True, skip_group_check=True),
                        waits=[(sc, sfree.get(g - 3, 0)), (gd, gath[kvh]), (gd, qgat[t])],
                        inc=mm)
                    expd[g] = P.op("scalar", lambda e, r=r: e.activation(
                        ptb[:, r * 512:(r + 1) * 512], sps[r][:, :], AF.Exp,
                        bias=biasv[:, 0:1], scale=SCALE),
                        waits=[(mm, smm), (mm, ptfree.get(g - 3, 0))], inc=sc)
                    sfree[g] = expd[g]
                    P.op("tensor", lambda e, r=r, p=p: e.matmul(
                        sumA_ps[:, :], flagA[:, p * 128:(p + 1) * 128], ptb[:, r * 512:(r + 1) * 512],
                        start=False, stop=False, skip_group_check=True),
                        waits=[(sc, expd[g])])
                    P.op("tensor", lambda e, r=r, p=p: e.matmul(
                        sumB_ps[:, :], flagB[:, p * 128:(p + 1) * 128], ptb[:, r * 512:(r + 1) * 512],
                        start=False, stop=False, skip_group_check=True))
                    P.op("tensor", lambda e, r=r, bk=bk, p=p, kt=kt: e.matmul(
                        ctxA_ps[:, :],
                        vslotA[:, p * 512 + kt * 128: p * 512 + (kt + 1) * 128],
                        ptb[:, r * 512:(r + 1) * 512],
                        start=False, stop=False, skip_group_check=True))
                    ptfree[g] = P.op("tensor", lambda e, r=r, bk=bk, p=p, kt=kt: e.matmul(
                        ctxB_ps[:, :],
                        vslotB[:, p * 512 + kt * 128: p * 512 + (kt + 1) * 128],
                        ptb[:, r * 512:(r + 1) * 512],
                        start=False, stop=False, skip_group_check=True),
                        inc=mm)
                    g += 1
            slotcons[kvh] = ptfree[g - 1]
            tcons[t] = ptfree[g - 1]

            last_mm = ptfree[g - 1]
            fin = None
            for own in range(2):
                own_ps_c = ctxA_ps if own == 0 else ctxB_ps
                own_ps_s = sumA_ps if own == 0 else sumB_ps
                P.op("vector", lambda e, ps=own_ps_s: e.reciprocal(rcp[:, :], ps[:, :]),
                     waits=[(mm, last_mm)])
                fin = P.op("vector", lambda e, ps=own_ps_c, t=t, own=own: e.tensor_tensor(
                    ctx[:, (t * 2 + own) * 512:(t * 2 + own + 1) * 512], ps[:, :], rcp[:, :], ALU.mult),
                    inc=ve)
            ctxfree[t] = fin
    attn_ve = ve.n

    sA.close()

    sW = contextlib.ExitStack()
    EW = sW.enter_context
    wof = k_own[:, :].bitcast(F32)
    wof2 = v_own[:, :].bitcast(F32)
    wbbf = EW(nc.sbuf_tensor("wbbf", [64, 4096], BF16))
    wobf = EW(nc.sbuf_tensor("wobf", [128, 2 * 4096], BF16))
    xres = EW(nc.sbuf_tensor("xres", [128, 2 * 1024], F32))
    osb = EW(nc.sbuf_tensor("osb", [128, 2 * 1024], F32))
    ops_ = [EW(nc.psum_tensor(f"ops{j}", [128, 1024], F32)) for j in range(2)]

    P.barrier(["tensor", "vector", "sync"], [ve, mm])

    wbl = dma("sync", wof[0:64, :].rearrange("p (o j) -> p o j", j=128),
              wb.ap().rearrange("o e j -> e o j"), ld)
    P.op("vector", lambda e: e.tensor_copy(wbbf[:, :], wof[0:64, :]), waits=[(ld, wbl)], inc=ve)
    wbc = P.op("vector", lambda e: e.tensor_copy(spcr[:, :], spcr[:, :]), inc=ve)

    ocast = {}
    oevac = {}
    ommd = {}
    ostw = {}
    def stage_o(oc):
        b = oc % 2
        buf = wof if b == 0 else wof2
        cpiece = ocast.get(oc - 2, wbc)
        for qtr in range(2):
            lq = dma("sync", buf[:, qtr * 2048:(qtr + 1) * 2048],
                     wo[oc][:, qtr * 2048:(qtr + 1) * 2048], ld,
                     waits=[(ve, cpiece)])
            cpiece = P.op("vector", lambda e, b=b, qtr=qtr, buf=buf: e.tensor_copy(
                wobf[:, b * 4096 + qtr * 2048: b * 4096 + (qtr + 1) * 2048],
                buf[:, qtr * 2048:(qtr + 1) * 2048]),
                waits=[(ld, lq), (mm, ommd.get(oc - 2, 0) if qtr == 0 else 0)], inc=ve)
        ocast[oc] = P.op("vector", lambda e: e.tensor_copy(spcr[:, :], spcr[:, :]), inc=ve)

    stage_o(0)
    stage_o(1)
    for oc in range(NQH):
        b = oc % 2
        if oc >= 2:
            pass
        xl0 = dma("sync", xres[:, b * 1024: b * 1024 + 512], xTA[oc * 128:(oc + 1) * 128, :], ld,
                  waits=[(ve, oevac.get(oc - 2, 0))])
        xl1 = dma("sync", xres[:, b * 1024 + 512: b * 1024 + 1024], xTB[oc * 128:(oc + 1) * 128, :], ld)
        omm = None
        for own in range(2):
            for ccc in range(NF):
                w = [(ve, ocast[oc]), (ve, attn_ve), (ve, wbc)]
                if ccc == 0 and own == 0:
                    w.append((ve, oevac.get(oc - 2, 0)))
                P.op("tensor", lambda e, oc=oc, own=own, ccc=ccc, b=b: e.matmul(
                    ops_[b][:, own * 512:(own + 1) * 512],
                    wobf[:, b * 4096 + ccc * 128: b * 4096 + (ccc + 1) * 128],
                    ctx[:, (ccc * 2 + own) * 512:(ccc * 2 + own + 1) * 512],
                    start=(ccc == 0), stop=False, skip_group_check=True),
                    waits=w)
            omm = P.op("tensor", lambda e, oc=oc, own=own, b=b: e.matmul(
                ops_[b][:, own * 512:(own + 1) * 512],
                wbbf[:, oc * 128:(oc + 1) * 128],
                mfin[0:64, own * 512:(own + 1) * 512],
                start=False, stop=True, skip_group_check=True),
                waits=[(ve, ffn_done_ve)], inc=mm)
        ommd[oc] = omm
        if oc + 2 < NQH:
            stage_o(oc + 2)
        ev = []
        for own in range(2):
            ev.append(P.op("vector", lambda e, own=own, b=b: e.tensor_tensor(
                osb[:, b * 1024 + own * 512: b * 1024 + (own + 1) * 512],
                ops_[b][:, own * 512:(own + 1) * 512],
                xres[:, b * 1024 + own * 512: b * 1024 + (own + 1) * 512], ALU.add),
                waits=[(mm, omm), (ld, xl1), (st, ostw.get(oc - 2, 0))], inc=ve))
        oevac[oc] = ev[-1]
        P.op("sync", lambda e, oc=oc, b=b: e.dma_start(
            out=outA[oc * 128:(oc + 1) * 128, :], in_=osb[:, b * 1024: b * 1024 + 512]),
            waits=[(ve, ev[0])], inc=st, inc_by=16)
        ostw[oc] = P.op("sync", lambda e, oc=oc, b=b: e.dma_start(
            out=outB[oc * 128:(oc + 1) * 128, :], in_=osb[:, b * 1024 + 512: b * 1024 + 1024]),
            waits=[(ve, ev[1])], inc=st, inc_by=16)

    P.op("sync", lambda e: None, waits=[(st, st.n)])

    with (nc.semaphore("s_ld") as h_ld, nc.semaphore("s_gd") as h_gd,
          nc.semaphore("s_mm") as h_mm, nc.semaphore("s_ve") as h_ve,
          nc.semaphore("s_sc") as h_sc, nc.semaphore("s_cc") as h_cc,
          nc.semaphore("s_st") as h_st, nc.semaphore("s_sd") as h_sd,
          nc.semaphore("s_io") as h_io, nc.Block() as block):
        H = {"ld": h_ld, "gd": h_gd, "mm": h_mm, "ve": h_ve, "sc": h_sc, "cc": h_cc,
             "st": h_st, "sd": h_sd, "io": h_io}

        def runner(name):
            def run(eng):
                for fn, ws, inc in P.ops[name]:
                    for s, v in ws:
                        eng.wait_ge(H[s.name], v)
                    inst = fn(eng) if fn is not None else None
                    if inc is not None:
                        if inst is not None:
                            inst.then_inc(H[inc[0].name], inc[1])
                        else:
                            eng.sem_inc(H[inc[0].name], inc[1])
            return run

        block.sync(runner("sync"))
        block.gpsimd(runner("gpsimd"))
        block.vector(runner("vector"))
        block.scalar(runner("scalar"))
        block.tensor(runner("tensor"))

    sW.close()
    stack.close()
    return nc



def _stage_shared(Wq, Wk, Wv, Wo, router_w, A, Bm):
    perm = np.empty(128, np.int64)
    perm[0:64] = np.arange(64) * 2
    perm[64:128] = np.arange(64) * 2 + 1
    wq = np.ascontiguousarray(Wq.reshape(NQH, 128, NF, 128)[:, perm].transpose(0, 3, 2, 1)
                              .reshape(NQH, 128, D))
    wk = np.ascontiguousarray(Wk.reshape(NKVH, 128, NF, 128)[:, perm].transpose(0, 3, 2, 1)
                              .reshape(NKVH, 128, D))
    wv = np.ascontiguousarray(Wv.reshape(2, 512, NF, 128).transpose(0, 2, 3, 1))
    wo = np.ascontiguousarray(Wo.reshape(NQH, 128, NF, 128).transpose(0, 3, 2, 1)
                              .reshape(NQH, 128, D))
    a_aug = np.concatenate([A.reshape(64, D), router_w], axis=0)
    wa = np.ascontiguousarray(a_aug.reshape(68, NF, 128).transpose(1, 2, 0))
    wb = np.ascontiguousarray(Bm.reshape(4, 32, 128, 16).transpose(1, 0, 3, 2).reshape(32, 64, 128))
    tri = np.where(np.arange(128)[:, None] <= np.arange(128)[None, :], 0.0, -1e9).astype(np.float32)
    ident = np.eye(128, dtype=np.float32)
    e2g = np.zeros((4, 64), np.float32)
    for ee in range(4):
        e2g[ee, ee * 16:(ee + 1) * 16] = 1.0
    return wq, wk, wv, wo, wa, wb, tri, ident, e2g


def _stage_core(c):
    cA, cB = c, 7 - c
    pos = np.stack([512 * cA + np.arange(CH, dtype=np.float32),
                    512 * cB + np.arange(CH, dtype=np.float32)]).astype(np.float32)
    slots = []
    for p in range(NSLOT):
        if p < cA:
            slots.append((p, 0, 0))
        else:
            j = p - cA
            slots.append((7 - j, 1, 1))
    dims = np.arange(128, dtype=np.int64)
    kidx = np.zeros((128, NKVH * NSLOT), np.int32)
    vai = np.zeros((128, NKVH * NSLOT * 4), np.int32)
    vbi = np.zeros((128, NKVH * NSLOT * 4), np.int32)
    qix = np.zeros((128, NQH * NSLOT), np.int32)
    fix = np.zeros((128, 2 * NSLOT), np.int32)
    ZROW = NCORE * 4096 * 4
    for p, (rank, seg, own) in enumerate(slots):
        for qh in range(NQH):
            qix[:, qh * NSLOT + p] = qh * 256 + dims * 2 + own
        fix[:, 0 * NSLOT + p] = 0 if own == 0 else 32
        fix[:, 1 * NSLOT + p] = 0 if own == 1 else 32
        for kvh in range(NKVH):
            kidx[:, kvh * NSLOT + p] = rank * 4096 + kvh * 256 + dims * 2 + seg
            for kt in range(4):
                col = kvh * NSLOT * 4 + p * 4 + kt
                vrow = rank * 16384 + 8192 + kvh * 1024 + seg * 512 + kt * 128 + dims
                if own == 0:
                    vai[:, col] = vrow
                    vbi[:, col] = ZROW + dims
                else:
                    vai[:, col] = ZROW + dims
                    vbi[:, col] = vrow
    return pos, kidx, vai, vbi, qix, fix


def kernel(L0_attn_hidden_states, L0_attn_Wq, L0_attn_Wk, L0_attn_Wv, L0_attn_Wo,
           L0_ffn_router_w, L0_ffn_A, L0_ffn_B):
    x = np.asarray(L0_attn_hidden_states, dtype=np.float32)
    wq, wk, wv, wo, wa, wb, tri, ident, e2g = _stage_shared(
        np.asarray(L0_attn_Wq, np.float32), np.asarray(L0_attn_Wk, np.float32),
        np.asarray(L0_attn_Wv, np.float32), np.asarray(L0_attn_Wo, np.float32),
        np.asarray(L0_ffn_router_w, np.float32), np.asarray(L0_ffn_A, np.float32),
        np.asarray(L0_ffn_B, np.float32))

    in_maps = []
    for c in range(NCORE):
        cA, cB = c, 7 - c
        pos, kidx, vai, vbi, qix, fix = _stage_core(c)
        in_maps.append({
            "xTA": np.ascontiguousarray(x[0, 512 * cA:512 * (cA + 1), :].T),
            "xTB": np.ascontiguousarray(x[1, 512 * cB:512 * (cB + 1), :].T),
            "wq": wq, "wk": wk, "wv": wv, "wo": wo, "wa": wa, "wb": wb,
            "tri": tri, "ident": ident, "e2g": e2g, "pos": pos,
            "kidx": kidx, "vai": vai, "vbi": vbi, "qix": qix, "fix": fix,
        })

    nc = build_program()
    res = run_bass_kernel_spmd(nc, in_maps, core_ids=list(range(NCORE)))

    out = np.empty((2, 4096, 4096), dtype=np.float32)
    for c in range(NCORE):
        cA, cB = c, 7 - c
        out[0, 512 * cA:512 * (cA + 1), :] = np.asarray(res.results[c]["outA"]).T
        out[1, 512 * cB:512 * (cB + 1), :] = np.asarray(res.results[c]["outB"]).T
    return out


# version 08
# speedup vs baseline: 1.2593x; 1.0682x over previous
import contextlib
import numpy as np
import ml_dtypes
import concourse.bass as bass
import concourse.mybir as mybir
from concourse.bass import IndirectOffsetOnAxis
from concourse.bass_utils import run_bass_kernel_spmd

F32 = mybir.dt.float32
BF16 = mybir.dt.bfloat16
I32 = mybir.dt.int32
AF = mybir.ActivationFunctionType
ALU = mybir.AluOpType
AX = mybir.AxisListType

NCORE = 8
D = 4096
NF = 32
CH = 512
NQH, NKVH, HD = 32, 8, 128
NSLOT = 7
SCALE = 1.0 / float(np.sqrt(128.0))
EXPB = -35.0
TWO_PI = 6.283185307179586
MAGIC = 12582912.0
LN10K = float(np.log(10000.0))


class SemT:
    def __init__(self, name):
        self.name = name
        self.n = 0

class Plan:
    ENGS = ("sync", "gpsimd", "vector", "scalar", "tensor")

    def __init__(self):
        self.ops = {e: [] for e in self.ENGS}
        self.lastwait = {}

    def op(self, eng, fn, waits=(), inc=None, inc_by=1):
        ws = []
        for s, v in waits:
            if v <= 0:
                continue
            key = (eng, s.name)
            if self.lastwait.get(key, -1) >= v:
                continue
            self.lastwait[key] = v
            ws.append((s, v))
        tgt = None
        if inc is not None:
            inc.n += inc_by
            tgt = inc.n
        self.ops[eng].append((fn, ws, (inc, inc_by) if inc is not None else None))
        return tgt

    def barrier(self, engs, sems, exclude=()):
        for e in engs:
            for s in sems:
                if s.name in exclude:
                    continue
                v = s.n
                if v <= 0:
                    continue
                key = (e, s.name)
                if self.lastwait.get(key, -1) >= v:
                    continue
                self.lastwait[key] = v
                self.ops[e].append((None, [(s, v)], None))



def build_program():
    nc = bass.Bass("TRN2", target_bir_lowering=False, detect_race_conditions=False)

    xTA = nc.dram_tensor("xTA", [D, CH], F32, kind="ExternalInput")
    xTB = nc.dram_tensor("xTB", [D, CH], F32, kind="ExternalInput")
    wq = nc.dram_tensor("wq", [NQH, 128, D], F32, kind="ExternalInput")
    wk = nc.dram_tensor("wk", [NKVH, 128, D], F32, kind="ExternalInput")
    wv = nc.dram_tensor("wv", [2, NF, 128, 512], F32, kind="ExternalInput")
    wo = nc.dram_tensor("wo", [NQH, 128, D], F32, kind="ExternalInput")
    wa = nc.dram_tensor("wa", [NF, 128, 68], F32, kind="ExternalInput")
    wb = nc.dram_tensor("wb", [32, 64, 128], F32, kind="ExternalInput")
    tri_d = nc.dram_tensor("tri", [128, 128], F32, kind="ExternalInput")
    id_d = nc.dram_tensor("ident", [128, 128], F32, kind="ExternalInput")
    e2g_d = nc.dram_tensor("e2g", [4, 64], F32, kind="ExternalInput")
    pos_d = nc.dram_tensor("pos", [2, CH], F32, kind="ExternalInput")
    kidx_d = nc.dram_tensor("kidx", [128, NKVH * NSLOT], I32, kind="ExternalInput")
    vai_d = nc.dram_tensor("vai", [128, NKVH * NSLOT * 4], I32, kind="ExternalInput")
    vbi_d = nc.dram_tensor("vbi", [128, NKVH * NSLOT * 4], I32, kind="ExternalInput")
    qix_d = nc.dram_tensor("qix", [128, NQH * NSLOT], I32, kind="ExternalInput")
    fix_d = nc.dram_tensor("fix", [128, 2 * NSLOT], I32, kind="ExternalInput")

    outA = nc.dram_tensor("outA", [D, CH], F32, kind="ExternalOutput")
    outB = nc.dram_tensor("outB", [D, CH], F32, kind="ExternalOutput")

    ag_in = nc.dram_tensor("ag_in", [4096, 512], BF16)
    ag_out = nc.dram_tensor("ag_out", [NCORE * 4096 + 32, 512], BF16)
    qsp = nc.dram_tensor("qsp", [NQH * 256, 512], BF16)
    c2 = nc.dram_tensor("c2", [33, 128], F32)

    ag_k = ag_out.ap()
    ag_v = ag_out.ap().rearrange("r (a b) -> (r a) b", a=4, b=128)

    P = Plan()
    ld = SemT("ld")
    gd = SemT("gd")
    mm = SemT("mm")
    ve = SemT("ve")
    sc = SemT("sc")
    cc = SemT("cc")
    st = SemT("st")
    sd = SemT("sd")
    io = SemT("io")
    SEMS = [ld, gd, mm, ve, sc, cc, st, sd, io]

    stack = contextlib.ExitStack()
    E = stack.enter_context

    kslot = E(nc.sbuf_tensor("kslot", [128, 2 * NSLOT * 512], BF16))
    vslotA = E(nc.sbuf_tensor("vslotA", [128, 2 * NSLOT * 512], BF16))
    vslotB = E(nc.sbuf_tensor("vslotB", [128, 2 * NSLOT * 512], BF16))
    qslot = E(nc.sbuf_tensor("qslot", [128, 2 * NSLOT * 512], BF16))
    flg_f = E(nc.sbuf_tensor("flg_f", [128, 128], F32))
    spcr = E(nc.sbuf_tensor("spcr", [128, 8], F32))
    cosq = E(nc.sbuf_tensor("cosq", [128, 1024], F32))
    sinq = E(nc.sbuf_tensor("sinq", [128, 1024], F32))
    k_own = E(nc.sbuf_tensor("k_own", [128, NKVH * 1024], BF16))
    v_own = E(nc.sbuf_tensor("v_own", [128, 8 * 1024], BF16))
    mfin = E(nc.sbuf_tensor("mfin", [128, 1024], BF16))
    flagA = E(nc.sbuf_tensor("flagA", [128, NSLOT * 128], BF16))
    flagB = E(nc.sbuf_tensor("flagB", [128, NSLOT * 128], BF16))
    tri_sb = E(nc.sbuf_tensor("tri_sb", [128, 128], F32))
    id_sb = E(nc.sbuf_tensor("id_sb", [128, 128], F32))
    e2g_sb = E(nc.sbuf_tensor("e2g_sb", [4, 64], F32))
    kidx_sb = E(nc.sbuf_tensor("kidx_sb", [128, NKVH * NSLOT], I32))
    vai_sb = E(nc.sbuf_tensor("vai_sb", [128, NKVH * NSLOT * 4], I32))
    vbi_sb = E(nc.sbuf_tensor("vbi_sb", [128, NKVH * NSLOT * 4], I32))
    qix_sb = E(nc.sbuf_tensor("qix_sb", [128, NQH * NSLOT], I32))
    fix_sb = E(nc.sbuf_tensor("fix_sb", [128, 2 * NSLOT], I32))
    ones128 = E(nc.sbuf_tensor("ones128", [128, 128], BF16))
    onesinv = E(nc.sbuf_tensor("onesinv", [128, 128], BF16))
    biasv = E(nc.sbuf_tensor("biasv", [128, 1], F32))
    epsv = E(nc.sbuf_tensor("epsv", [128, 1], F32))
    zrow = E(nc.sbuf_tensor("zrow", [32, 512], BF16))
    c2_sb = E(nc.sbuf_tensor("c2_sb", [33, 128], F32))

    def dma(eng, out_ap, in_ap, sem, waits=()):
        return P.op(eng, lambda e, o=out_ap, i=in_ap: e.dma_start(out=o, in_=i),
                    waits=waits, inc=sem, inc_by=16)

    t0 = []
    t0.append(dma("sync", tri_sb[:, :], tri_d[:, :], ld))
    t0.append(dma("sync", id_sb[:, :], id_d[:, :], ld))
    t0.append(dma("sync", e2g_sb[:, :], e2g_d[:, :], ld))
    t0.append(dma("sync", kidx_sb[:, :], kidx_d[:, :], ld))
    t0.append(dma("sync", vai_sb[:, :], vai_d[:, :], ld))
    t0.append(dma("sync", vbi_sb[:, :], vbi_d[:, :], ld))
    t0.append(dma("sync", qix_sb[:, :], qix_d[:, :], ld))
    t0.append(dma("sync", fix_sb[:, :], fix_d[:, :], ld))
    tbl_ld = ld.n

    mset = []
    P.op("vector", lambda e: e.memset(ones128[:, :], 1.0))
    P.op("vector", lambda e: e.memset(onesinv[:, :], 1.0 / 4096.0))
    P.op("vector", lambda e: e.memset(biasv[:, :], EXPB))
    P.op("vector", lambda e: e.memset(epsv[:, :], 1e-5))
    P.op("vector", lambda e: e.memset(zrow[:, :], 0.0))
    P.op("vector", lambda e: e.memset(c2_sb[:, :], 0.0))
    msets_done = P.op("vector", lambda e: e.memset(c2_sb[0:1, :], 1.0), inc=ve)

    c2w = dma("scalar", c2[:, :], c2_sb[:, :], sd, waits=[(ve, msets_done)])
    zpw = dma("scalar", ag_out[NCORE * 4096: NCORE * 4096 + 32, :], zrow[:, :], sd,
              waits=[(ve, msets_done)])

    flg_done = []
    for t in range(2):
        for p in range(NSLOT):
            g = P.op("gpsimd",
                     lambda e, t=t, p=p: e.indirect_dma_start(
                         flg_f[:, :], None, c2[:, :],
                         IndirectOffsetOnAxis(fix_sb[:, t * NSLOT + p: t * NSLOT + p + 1], 0)),
                     waits=[(ld, tbl_ld), (sd, c2w), (ve, flg_done[-1] if flg_done else 0)],
                     inc=gd, inc_by=16)
            dstf = flagA if t == 0 else flagB
            flg_done.append(P.op("vector",
                                 lambda e, d=dstf, p=p: e.tensor_copy(d[:, p * 128:(p + 1) * 128], flg_f[:, :]),
                                 waits=[(gd, g)], inc=ve))
    flags_ve = ve.n

    with contextlib.ExitStack() as ph:
        EE = ph.enter_context
        invf = EE(nc.sbuf_tensor("invf", [64, 1], F32))
        posr = EE(nc.sbuf_tensor("posr", [64, 512], F32))
        th = EE(nc.sbuf_tensor("th", [64, 512], F32))
        u1 = EE(nc.sbuf_tensor("u1", [64, 512], F32))
        n1 = EE(nc.sbuf_tensor("n1", [64, 512], F32))
        r1 = EE(nc.sbuf_tensor("r1", [64, 512], F32))
        P.op("gpsimd", lambda e: e.iota(invf[:, :], pattern=[[0, 1]], channel_multiplier=1,
                                        allow_small_or_imprecise_dtypes=True), inc=io, inc_by=1)
        iv = P.op("scalar", lambda e: e.activation(invf[:, :], invf[:, :], AF.Exp, scale=-LN10K / 64.0),
                  waits=[(io, 1)], inc=sc)
        prev_tv = 0
        prev_tr = 0
        for own in range(2):
            pr = P.op("scalar",
                      lambda e, own=own: e.dma_start(out=posr[:, :],
                                                     in_=pos_d[own:own + 1, :].broadcast_to([64, CH])),
                      waits=[(ld, tbl_ld), (ve, prev_tv)], inc=sd, inc_by=16)
            tv = P.op("vector", lambda e: e.tensor_scalar_mul(th[:, :], posr[:, :], invf[0:64, 0:1]),
                      waits=[(sd, pr), (sc, iv)], inc=ve)
            prev_tv = tv
            for trig, off in (("sin", 0.0), ("cos", 0.25)):
                P.op("vector", lambda e, off=off: e.tensor_scalar(u1[:, :], th[:, :], 1.0 / TWO_PI, off, ALU.mult, ALU.add),
                     waits=[(ve, tv), (sc, prev_tr)])
                P.op("vector", lambda e: e.tensor_scalar_add(n1[:, :], u1[:, :], MAGIC))
                P.op("vector", lambda e: e.tensor_scalar_add(n1[:, :], n1[:, :], -MAGIC))
                P.op("vector", lambda e: e.tensor_sub(r1[:, :], u1[:, :], n1[:, :]))
                rv = P.op("vector", lambda e: e.tensor_scalar_mul(r1[:, :], r1[:, :], TWO_PI), inc=ve)
                if trig == "sin":
                    sn = P.op("scalar", lambda e, own=own: e.activation(sinq[64:128, own * 512:(own + 1) * 512],
                                                                        r1[:, :], AF.Sin),
                              waits=[(ve, rv)], inc=sc)
                    prev_tr = sn
                    P.op("vector", lambda e, own=own: e.tensor_scalar_mul(
                        sinq[0:64, own * 512:(own + 1) * 512], sinq[64:128, own * 512:(own + 1) * 512], -1.0),
                        waits=[(sc, sn)], inc=ve)
                else:
                    cs = P.op("scalar", lambda e, own=own: e.activation(cosq[0:64, own * 512:(own + 1) * 512],
                                                                        r1[:, :], AF.Sin),
                              waits=[(ve, rv)], inc=sc)
                    prev_tr = cs
                    P.op("scalar", lambda e, own=own: e.dma_start(
                        out=cosq[64:128, own * 512:(own + 1) * 512],
                        in_=cosq[0:64, own * 512:(own + 1) * 512]),
                        waits=[(sc, cs)], inc=sd, inc_by=16)
    rope_tabs_ve = ve.n
    rope_tabs_sd = sd.n

    s1 = contextlib.ExitStack()
    E1 = s1.enter_context
    xn = E1(nc.sbuf_tensor("xn", [128, 2 * NF * 512], BF16))
    xst = E1(nc.sbuf_tensor("xst", [128, 2 * 512], F32))
    xbst = E1(nc.sbuf_tensor("xbst", [128, 2 * 512], BF16))
    xsq = E1(nc.sbuf_tensor("xsq", [128, 2 * 512], BF16))
    wf32 = E1(nc.sbuf_tensor("wf32", [128, 2 * 1024], F32))
    wbf = E1(nc.sbuf_tensor("wbf", [128, 2 * 4096], BF16))
    mean_sb = vslotA[:, 3584:5632].bitcast(F32)
    rstd_sb = vslotB[:, 3584:5632].bitcast(F32)
    qf = kslot[:, 3584:5632].bitcast(F32)
    qs = qslot[:, 3584:5632].bitcast(F32)
    wabf = qslot[:, 0:NF * 68]
    lg_sb = vslotB[:, 0:2048].bitcast(F32)[0:4]
    gt_sb = vslotA[:, 0:2048].bitcast(F32)[0:4]
    gb_sb = kslot[:, 0:1024].bitcast(F32)[0:64]
    gsm = E1(nc.sbuf_tensor("gsm", [128, 128], F32))
    gsm1 = E1(nc.sbuf_tensor("gsm1", [128, 48], F32))

    wal = dma("sync", wf32[:, 0:2040].rearrange("p (f e) -> p f e", e=68),
              wa.ap()[0:30].rearrange("f i e -> i f e"), ld)
    wc1 = P.op("vector", lambda e: e.tensor_copy(wabf[:, 0:2040], wf32[:, 0:2040]),
               waits=[(ld, wal)], inc=ve)
    wal2 = dma("sync", wf32[:, 0:136].rearrange("p (f e) -> p f e", e=68),
               wa.ap()[30:32].rearrange("f i e -> i f e"), ld, waits=[(ve, wc1)])
    P.op("vector", lambda e: e.tensor_copy(wabf[:, 2040:2176], wf32[:, 0:136]),
         waits=[(ld, wal2)], inc=ve)
    wacast = P.op("vector", lambda e: e.tensor_copy(spcr[:, :], spcr[:, :]), inc=ve)

    with contextlib.ExitStack() as ph:
        EE = ph.enter_context
        mean_ps = EE(nc.psum_tensor("mean_ps", [128, 1024], F32))
        sq_ps = EE(nc.psum_tensor("sq_ps", [128, 1024], F32))
        xcons = {}
        xsrc = (xTA, xTB)
        for own in range(2):
            for f in range(NF):
                i = own * NF + f
                b = i % 2
                l_ = dma("sync", xst[:, b * 512:(b + 1) * 512], xsrc[own][f * 128:(f + 1) * 128, :], ld,
                         waits=[(mm, xcons.get(i - 2, 0)), (ve, wacast if i == 0 else 0)])
                cv_ = P.op("vector", lambda e, b=b: e.tensor_copy(xbst[:, b * 512:(b + 1) * 512],
                                                                  xst[:, b * 512:(b + 1) * 512]),
                           waits=[(ld, l_)], inc=ve)
                sq_ = P.op("scalar", lambda e, b=b: e.activation(xsq[:, b * 512:(b + 1) * 512],
                                                                 xst[:, b * 512:(b + 1) * 512], AF.Square),
                           waits=[(ld, l_)], inc=sc)
                P.op("tensor", lambda e, own=own, f=f, b=b: e.matmul(
                    mean_ps[:, own * 512:(own + 1) * 512], onesinv[:, :], xbst[:, b * 512:(b + 1) * 512],
                    start=(f == 0), stop=(f == NF - 1)),
                    waits=[(ve, cv_)])
                xcons[i] = P.op("tensor", lambda e, own=own, f=f, b=b: e.matmul(
                    sq_ps[:, own * 512:(own + 1) * 512], onesinv[:, :], xsq[:, b * 512:(b + 1) * 512],
                    start=(f == 0), stop=(f == NF - 1)),
                    waits=[(sc, sq_)], inc=mm)
        stats_mm = mm.n
        P.op("vector", lambda e: e.tensor_copy(mean_sb, mean_ps[:, :]), waits=[(mm, stats_mm)])
        P.op("vector", lambda e: e.tensor_mul(qf, mean_sb, mean_sb))
        vfin = P.op("vector", lambda e: e.tensor_tensor(qf, sq_ps[:, :], qf, ALU.subtract), inc=ve)
        sfin = P.op("scalar", lambda e: e.activation(qf, qf, AF.Sqrt, bias=epsv[:, 0:1]),
                    waits=[(ve, vfin)], inc=sc)
        rfin = P.op("vector", lambda e: e.reciprocal(rstd_sb, qf), waits=[(sc, sfin)], inc=ve)

        xn_done = {}
        for own in range(2):
            for f in range(NF):
                i = own * NF + f
                b = i % 2
                l_ = dma("sync", xst[:, b * 512:(b + 1) * 512], xsrc[own][f * 128:(f + 1) * 128, :], ld,
                         waits=[(ve, xn_done.get(i - 2, rfin))])
                P.op("vector", lambda e, b=b, own=own: e.tensor_tensor(
                    qf[:, 0:512], xst[:, b * 512:(b + 1) * 512], mean_sb[:, own * 512:(own + 1) * 512],
                    ALU.subtract), waits=[(ld, l_), (ve, rfin)])
                xn_done[i] = P.op("vector", lambda e, i=i, own=own: e.tensor_tensor(
                    xn[:, i * 512:(i + 1) * 512], qf[:, 0:512], rstd_sb[:, own * 512:(own + 1) * 512],
                    ALU.mult), inc=ve)
        xn_all = ve.n

    rope_state = {"last_swap": 0}

    def rope_emit(ps_ap, out_bf_ap, wait_mm, final_wait_sd=0):
        c1 = P.op("vector", lambda e: e.tensor_copy(qf, ps_ap),
                  waits=[(mm, wait_mm), (sd, rope_state["last_swap"])], inc=ve)
        s1_ = P.op("scalar", lambda e: e.dma_start(out=qs[0:64], in_=qf[64:128]),
                   waits=[(ve, c1)], inc=sd, inc_by=16)
        s2_ = P.op("scalar", lambda e: e.dma_start(out=qs[64:128], in_=qf[0:64, :]),
                   inc=sd, inc_by=16)
        rope_state["last_swap"] = s2_
        P.op("vector", lambda e: e.tensor_mul(qf, qf, cosq[:, :]),
             waits=[(ve, rope_tabs_ve), (sd, rope_tabs_sd), (sd, s2_)])
        P.op("vector", lambda e: e.tensor_mul(qs, qs, sinq[:, :]))
        done = P.op("vector", lambda e: e.tensor_add(out_bf_ap, qf, qs),
                    waits=[(sd, final_wait_sd)], inc=ve)
        return c1, done

    with contextlib.ExitStack() as ph:
        EE = ph.enter_context
        kps = [EE(nc.psum_tensor(f"kps{j}", [128, 1024], F32)) for j in range(2)]
        kfree = {}
        kagw = []
        cast_done = {}
        mm_done = {}
        def stage_k(kvh):
            b = kvh % 2
            prevc = cast_done.get(kvh - 2, wacast if kvh < 2 else 0)
            cpiece = prevc
            for qtr in range(4):
                lq = dma("sync", wf32[:, b * 1024:(b + 1) * 1024],
                         wk[kvh][:, qtr * 1024:(qtr + 1) * 1024], ld,
                         waits=[(ve, cpiece)])
                cpiece = P.op("vector", lambda e, b=b, kvh=kvh, qtr=qtr: e.tensor_copy(
                    wbf[:, b * 4096 + qtr * 1024: b * 4096 + (qtr + 1) * 1024],
                    wf32[:, b * 1024:(b + 1) * 1024]),
                    waits=[(ld, lq), (mm, mm_done.get(kvh - 2, 0) if qtr == 0 else 0)], inc=ve)
            cast_done[kvh] = P.op("vector", lambda e: e.tensor_copy(spcr[:, :], spcr[:, :]), inc=ve)

        stage_k(0)
        stage_k(1)
        for kvh in range(NKVH):
            b = kvh % 2
            for own in range(2):
                for f in range(NF):
                    w = [(ve, cast_done[kvh]), (ve, xn_all)]
                    if f == 0 and own == 0:
                        w.append((ve, kfree.get(kvh - 2, 0)))
                    last = (own == 1 and f == NF - 1)
                    mm_done[kvh] = P.op("tensor", lambda e, kvh=kvh, own=own, f=f, b=b: e.matmul(
                        kps[b][:, own * 512:(own + 1) * 512],
                        wbf[:, b * 4096 + f * 128: b * 4096 + (f + 1) * 128],
                        xn[:, (own * NF + f) * 512:(own * NF + f + 1) * 512],
                        start=(f == 0), stop=(f == NF - 1)),
                        waits=w, inc=(mm if last else None))
            if kvh + 2 < NKVH:
                stage_k(kvh + 2)
            kf, kdone = rope_emit(kps[b][:, :], k_own[:, kvh * 1024:(kvh + 1) * 1024], mm_done[kvh])
            kfree[kvh] = kf
            for own in range(2):
                kagw.append(P.op("scalar", lambda e, kvh=kvh, own=own: e.dma_start(
                    out=ag_in[kvh * 256 + own: kvh * 256 + 256: 2, :],
                    in_=k_own[:, kvh * 1024 + own * 512: kvh * 1024 + (own + 1) * 512]),
                    waits=[(ve, kdone)], inc=sd, inc_by=16))
    k_phase_ve = ve.n
    kcast_last = cast_done[NKVH - 1]

    P.barrier(["tensor"], [ve])
    with contextlib.ExitStack() as ph:
        EE = ph.enter_context
        vps = [EE(nc.psum_tensor(f"vps{j}", [128, 512], F32)) for j in range(8)]
        vagw = []
        vcast = {}
        vfcons = {}
        for h in range(2):
            vmm = None
            for f in range(NF):
                b = f % 2
                l_ = dma("sync", wf32[:, b * 1024: b * 1024 + 512],
                         wv[h][f][:, :], ld,
                         waits=[(ve, vcast.get((h, f - 2),
                                               vcast.get((0, NF - 2 + f), 0) if h == 1 else 0)),
                                (ve, kcast_last)])
                P.op("vector", lambda e, b=b: e.tensor_copy(
                    wbf[:, b * 512:(b + 1) * 512], wf32[:, b * 1024: b * 1024 + 512]),
                    waits=[(ld, l_), (mm, vfcons.get((h, f - 2),
                                                    vfcons.get((0, NF - 1), 0) if h == 1 else 0))], inc=ve)
                vcast[(h, f)] = P.op("vector", lambda e: e.tensor_copy(spcr[:, :], spcr[:, :]), inc=ve)
                for own in range(2):
                    for kt in range(4):
                        j = own * 4 + kt
                        w = [(ve, vcast[(h, f)])]
                        if f == 0:
                            w.append((ve, k_phase_ve if h == 0 else 0))
                            w.append((ve, vevac if h == 1 else 0))
                        last = (own == 1 and kt == 3)
                        vmm = P.op("tensor", lambda e, own=own, kt=kt, f=f, b=b, j=j: e.matmul(
                            vps[j][:, :],
                            xn[:, (own * NF + f) * 512 + kt * 128: (own * NF + f) * 512 + (kt + 1) * 128],
                            wbf[:, b * 512:(b + 1) * 512],
                            start=(f == 0), stop=(f == NF - 1)),
                            waits=w, inc=(mm if last else None))
                vfcons[(h, f)] = vmm
            ev = []
            for own in range(2):
                for kt in range(4):
                    j = own * 4 + kt
                    ev.append(P.op("vector", lambda e, own=own, kt=kt, j=j, h=h: e.tensor_copy(
                        v_own[:, (own * 4 + kt) * 1024 + h * 512: (own * 4 + kt) * 1024 + (h + 1) * 512],
                        vps[j][:, :]), waits=[(mm, vmm)], inc=ve))
            vevac = ve.n
            for own in range(2):
                for kt in range(4):
                    for kk in range(4):
                        kvh = h * 4 + kk
                        base = 8192 + kvh * 1024 + own * 512 + kt * 128
                        vagw.append(P.op("scalar", lambda e, own=own, kt=kt, kk=kk, base=base, h=h: e.dma_start(
                            out=ag_in.ap().rearrange("r (a b) -> (r a) b", a=4, b=128)[base: base + 128, :],
                            in_=v_own[:, (own * 4 + kt) * 1024 + h * 512 + kk * 128:
                                      (own * 4 + kt) * 1024 + h * 512 + (kk + 1) * 128]),
                            waits=[(ve, vevac)], inc=sd, inc_by=16))

    agq = P.op("gpsimd", lambda e: e.collective_compute(
        "AllGather", ALU.bypass, replica_groups=[list(range(NCORE))],
        ins=[ag_in[:, :]], outs=[ag_out[0:NCORE * 4096, :]]),
        waits=[(sd, sd.n), (gd, gd.n)], inc=cc)

    vcast_last = vcast[(1, NF - 1)]
    v_mm_last = vfcons[(1, NF - 1)]
    P.barrier(["tensor"], [ve])
    with contextlib.ExitStack() as ph:
        EE = ph.enter_context
        qps = [EE(nc.psum_tensor(f"qps{j}", [128, 1024], F32)) for j in range(2)]
        qfree = {}
        qspw = {}
        qcast = {}
        qmm = {}
        def stage_q(qh):
            b = qh % 2
            cpiece = qcast.get(qh - 2, vcast_last)
            for qtr in range(4):
                lq = dma("sync", wf32[:, b * 1024:(b + 1) * 1024],
                         wq[qh][:, qtr * 1024:(qtr + 1) * 1024], ld,
                         waits=[(ve, cpiece)])
                cpiece = P.op("vector", lambda e, b=b, qtr=qtr: e.tensor_copy(
                    wbf[:, b * 4096 + qtr * 1024: b * 4096 + (qtr + 1) * 1024],
                    wf32[:, b * 1024:(b + 1) * 1024]),
                    waits=[(ld, lq), (mm, qmm.get(qh - 2, v_mm_last) if qtr == 0 else 0)], inc=ve)
            qcast[qh] = P.op("vector", lambda e: e.tensor_copy(spcr[:, :], spcr[:, :]), inc=ve)

        stage_q(0)
        stage_q(1)
        for qh in range(NQH):
            b = qh % 2
            for own in range(2):
                for f in range(NF):
                    w = [(ve, qcast[qh])]
                    if f == 0 and own == 0:
                        w.append((ve, qfree.get(qh - 2, 0)))
                    last = (own == 1 and f == NF - 1)
                    qmm[qh] = P.op("tensor", lambda e, own=own, f=f, b=b: e.matmul(
                        qps[b][:, own * 512:(own + 1) * 512],
                        wbf[:, b * 4096 + f * 128: b * 4096 + (f + 1) * 128],
                        xn[:, (own * NF + f) * 512:(own * NF + f + 1) * 512],
                        start=(f == 0), stop=(f == NF - 1)),
                        waits=w, inc=(mm if last else None))
            if qh + 2 < NQH:
                stage_q(qh + 2)
            qfh, qdone = rope_emit(qps[b][:, :], xsq[:, :], qmm[qh],
                                   final_wait_sd=qspw.get((qh - 1, 1), 0))
            qfree[qh] = qfh
            for own in range(2):
                qspw[(qh, own)] = P.op("scalar", lambda e, qh=qh, own=own: e.dma_start(
                    out=qsp[qh * 256 + own: qh * 256 + 256: 2, :],
                    in_=xsq[:, own * 512:(own + 1) * 512]),
                    waits=[(ve, qdone)], inc=sd, inc_by=16)
    q_phase_done_sd = sd.n
    q_phase_done_ve = ve.n

    P.barrier(["tensor"], [ve])
    with contextlib.ExitStack() as ph:
        EE = ph.enter_context
        maug = EE(nc.psum_tensor("maug", [68, 1024], F32))
        smps = EE(nc.psum_tensor("smps", [128, 4], F32))
        gtps = EE(nc.psum_tensor("gtps", [4, 128], F32))
        gbps = EE(nc.psum_tensor("gbps", [64, 512], F32))
        amm = None
        for own in range(2):
            for f in range(NF):
                last = (own == 1 and f == NF - 1)
                amm = P.op("tensor", lambda e, own=own, f=f: e.matmul(
                    maug[:, own * 512:(own + 1) * 512],
                    wabf[:, f * 68:(f + 1) * 68],
                    xn[:, (own * NF + f) * 512:(own * NF + f + 1) * 512],
                    start=(f == 0), stop=(f == NF - 1)),
                    waits=[(ve, wacast), (ve, xn_all)], inc=(mm if last else None))
        lgc = P.op("vector", lambda e: e.tensor_copy(lg_sb, maug[64:68, :]), waits=[(mm, amm)], inc=ve)
        tdone = {}
        prev_lt = 0
        for k in range(8):
            own, s = divmod(k, 4)
            tmm = P.op("tensor", lambda e, own=own, s=s: e.matmul(
                smps[:, :], lg_sb[:, own * 512 + s * 128: own * 512 + (s + 1) * 128],
                id_sb[0:4, 0:4], start=True, stop=True),
                waits=[(ve, lgc), (ve, prev_lt)], inc=mm)
            prev_lt = P.op("vector", lambda e, k=k: e.tensor_copy(gsm[:, 4 * k:4 * k + 4], smps[:, :]),
                           waits=[(mm, tmm)], inc=ve)
        GL, GE, T1, T2 = 0, 32, 64, 96
        M1, NM, M2, ZS, RZ, RZ4 = 0, 8, 16, 24, 32, 40
        for k in range(8):
            P.op("vector", lambda e, k=k: e.reduce_max(gsm1[:, M1 + k:M1 + k + 1],
                                                       gsm[:, 4 * k:4 * k + 4], axis=AX.X))
        nmv = None
        for k in range(8):
            nmv = P.op("vector", lambda e, k=k: e.tensor_scalar_mul(
                gsm1[:, NM + k:NM + k + 1], gsm1[:, M1 + k:M1 + k + 1], -1.0),
                inc=(ve if k == 7 else None))
        eev = None
        for k in range(8):
            eev = P.op("scalar", lambda e, k=k: e.activation(
                gsm[:, GE + 4 * k:GE + 4 * k + 4], gsm[:, 4 * k:4 * k + 4], AF.Exp,
                bias=gsm1[:, NM + k:NM + k + 1]),
                waits=[(ve, nmv)], inc=(sc if k == 7 else None))
        for k in range(8):
            P.op("vector", lambda e, k=k: e.tensor_scalar(
                gsm[:, T1 + 4 * k:T1 + 4 * k + 4], gsm[:, 4 * k:4 * k + 4],
                gsm1[:, M1 + k:M1 + k + 1], None, ALU.is_ge))
        for k in range(8):
            P.op("vector", lambda e, k=k: e.tensor_scalar_mul(
                gsm[:, T2 + 4 * k:T2 + 4 * k + 4], gsm[:, T1 + 4 * k:T1 + 4 * k + 4], -30000.0))
        for k in range(8):
            P.op("vector", lambda e, k=k: e.tensor_add(
                gsm[:, T1 + 4 * k:T1 + 4 * k + 4], gsm[:, T2 + 4 * k:T2 + 4 * k + 4],
                gsm[:, 4 * k:4 * k + 4]))
        for k in range(8):
            P.op("vector", lambda e, k=k: e.reduce_max(gsm1[:, M2 + k:M2 + k + 1],
                                                       gsm[:, T1 + 4 * k:T1 + 4 * k + 4], axis=AX.X))
        for k in range(8):
            w = [(sc, eev)] if k == 0 else []
            P.op("vector", lambda e, k=k: e.tensor_scalar(
                gsm[:, T2 + 4 * k:T2 + 4 * k + 4], gsm[:, 4 * k:4 * k + 4],
                gsm1[:, M2 + k:M2 + k + 1], None, ALU.is_ge), waits=w)
        for k in range(8):
            P.op("vector", lambda e, k=k: e.tensor_mul(
                gsm[:, T1 + 4 * k:T1 + 4 * k + 4], gsm[:, GE + 4 * k:GE + 4 * k + 4],
                gsm[:, T2 + 4 * k:T2 + 4 * k + 4]))
        for k in range(8):
            P.op("vector", lambda e, k=k: e.reduce_sum(gsm1[:, ZS + k:ZS + k + 1],
                                                       gsm[:, T1 + 4 * k:T1 + 4 * k + 4], axis=AX.X))
        for k in range(8):
            P.op("vector", lambda e, k=k: e.reciprocal(gsm1[:, RZ + k:RZ + k + 1],
                                                       gsm1[:, ZS + k:ZS + k + 1]))
        for k in range(8):
            P.op("vector", lambda e, k=k: e.tensor_scalar_mul(
                gsm1[:, RZ4 + k:RZ4 + k + 1], gsm1[:, RZ + k:RZ + k + 1], 4.0))
        gv = None
        for k in range(8):
            gv = P.op("vector", lambda e, k=k: e.tensor_scalar_mul(
                gsm[:, GE + 4 * k:GE + 4 * k + 4], gsm[:, T1 + 4 * k:T1 + 4 * k + 4],
                gsm1[:, RZ4 + k:RZ4 + k + 1]), inc=(ve if k == 7 else None))
        prev_gt = 0
        for k in range(8):
            own, s = divmod(k, 4)
            t2m = P.op("tensor", lambda e, k=k: e.matmul(gtps[:, :], gsm[:, GE + 4 * k:GE + 4 * k + 4],
                                                         id_sb[:, :], start=True, stop=True),
                       waits=[(ve, gv), (ve, prev_gt)], inc=mm)
            prev_gt = P.op("vector", lambda e, own=own, s=s: e.tensor_copy(
                gt_sb[:, own * 512 + s * 128: own * 512 + (s + 1) * 128], gtps[:, :]),
                waits=[(mm, t2m)], inc=ve)
        for own in range(2):
            gbm = P.op("tensor", lambda e, own=own: e.matmul(
                gbps[:, :], e2g_sb[:, :], gt_sb[:, own * 512:(own + 1) * 512], start=True, stop=True),
                waits=[(ve, ve.n)], inc=mm)
            gbc = P.op("vector", lambda e: e.tensor_copy(gb_sb, gbps[:, :]), waits=[(mm, gbm)], inc=ve)
            P.op("vector", lambda e, own=own: e.tensor_tensor(
                mfin[0:64, own * 512:(own + 1) * 512], maug[0:64, own * 512:(own + 1) * 512],
                gb_sb, ALU.mult), waits=[(ve, gbc)], inc=ve)
    ffn_done_ve = ve.n

    s1.close()

    ctx = E(nc.sbuf_tensor("ctx", [128, NQH * 2 * 512], BF16))
    sA = contextlib.ExitStack()
    EA = sA.enter_context
    qdiag = EA(nc.sbuf_tensor("qdiag", [128, 2 * 1024], BF16))
    ptb = EA(nc.sbuf_tensor("ptb", [128, 3 * 512], BF16))
    rcp = EA(nc.sbuf_tensor("rcp", [128, 512], F32))
    sps = [EA(nc.psum_tensor(f"sps{j}", [128, 512], F32)) for j in range(3)]
    ctxA_ps = EA(nc.psum_tensor("ctxA_ps", [128, 512], F32))
    ctxB_ps = EA(nc.psum_tensor("ctxB_ps", [128, 512], F32))
    sumA_ps = EA(nc.psum_tensor("sumA_ps", [128, 512], F32))
    sumB_ps = EA(nc.psum_tensor("sumB_ps", [128, 512], F32))

    P.barrier(["gpsimd"], [cc])
    P.barrier(["tensor", "scalar", "vector", "gpsimd", "sync"], [ve, mm, sc])

    expd = {}
    ptfree = {}
    sfree = {}
    gath = {}
    qgat = {}
    slotcons = {}
    tcons = {}
    ctxfree = {}
    g = 0

    for kvh in range(NKVH):
        bk = kvh % 2
        for p in range(NSLOT):
            w = [(mm, slotcons.get(kvh - 2, 0))]
            P.op("gpsimd", lambda e, kvh=kvh, p=p, bk=bk: e.indirect_dma_start(
                kslot[:, (bk * NSLOT + p) * 512:(bk * NSLOT + p + 1) * 512], None,
                ag_k, IndirectOffsetOnAxis(kidx_sb[:, kvh * NSLOT + p: kvh * NSLOT + p + 1], 0)),
                waits=w, inc=gd, inc_by=16)
            for kt in range(4):
                col = kvh * NSLOT * 4 + p * 4 + kt
                P.op("gpsimd", lambda e, col=col, bk=bk, p=p, kt=kt: e.indirect_dma_start(
                    vslotA[:, (bk * NSLOT + p) * 512 + kt * 128:(bk * NSLOT + p) * 512 + (kt + 1) * 128],
                    None, ag_v, IndirectOffsetOnAxis(vai_sb[:, col:col + 1], 0)),
                    inc=gd, inc_by=16)
                gath[kvh] = P.op("gpsimd", lambda e, col=col, bk=bk, p=p, kt=kt: e.indirect_dma_start(
                    vslotB[:, (bk * NSLOT + p) * 512 + kt * 128:(bk * NSLOT + p) * 512 + (kt + 1) * 128],
                    None, ag_v, IndirectOffsetOnAxis(vbi_sb[:, col:col + 1], 0)),
                    inc=gd, inc_by=16)

        for qq in range(4):
            qh = kvh * 4 + qq
            t = qh
            bt = t % 2
            qd0 = P.op("sync", lambda e, qh=qh, bt=bt: e.dma_start(
                out=qdiag[:, bt * 1024: bt * 1024 + 512],
                in_=qsp[qh * 256: qh * 256 + 256: 2, :]),
                waits=[(sd, qspw[(qh, 0)]), (mm, tcons.get(t - 1, 0))], inc=ld, inc_by=16)
            qd1 = P.op("sync", lambda e, qh=qh, bt=bt: e.dma_start(
                out=qdiag[:, bt * 1024 + 512: bt * 1024 + 1024],
                in_=qsp[qh * 256 + 1: qh * 256 + 256: 2, :]),
                waits=[(sd, qspw[(qh, 1)])], inc=ld, inc_by=16)
            for p in range(NSLOT):
                qgat[t] = P.op("gpsimd", lambda e, qh=qh, p=p, bt=bt: e.indirect_dma_start(
                    qslot[:, (bt * NSLOT + p) * 512:(bt * NSLOT + p + 1) * 512], None,
                    qsp[:, :],
                    IndirectOffsetOnAxis(qix_sb[:, qh * NSLOT + p: qh * NSLOT + p + 1], 0)),
                    waits=[(sd, qspw[(qh, 1)]), (mm, tcons.get(t - 2, 0))], inc=gd, inc_by=16)

            for own in range(2):
                own_ps_c = ctxA_ps if own == 0 else ctxB_ps
                own_ps_s = sumA_ps if own == 0 else sumB_ps
                for kt in range(4):
                    r = g % 3
                    n0 = kt * 128
                    smm = P.op("tensor", lambda e, r=r, kvh=kvh, own=own, kt=kt, n0=n0, bt=bt: e.matmul(
                        sps[r][:, n0:512],
                        k_own[:, kvh * 1024 + own * 512 + kt * 128: kvh * 1024 + own * 512 + (kt + 1) * 128],
                        qdiag[:, bt * 1024 + own * 512 + n0: bt * 1024 + (own + 1) * 512],
                        start=True, stop=True, skip_group_check=True),
                        waits=[(sc, sfree.get(g - 3, 0)), (ld, qd1 if own == 0 and kt == 0 else 0),
                               (ve, ctxfree.get(t - 1, 0) if kt == 0 else 0)],
                        inc=mm)
                    tad = P.op("vector", lambda e, r=r, n0=n0: e.tensor_tensor(
                        sps[r][:, n0:n0 + 128], sps[r][:, n0:n0 + 128], tri_sb[:, :], ALU.add),
                        waits=[(mm, smm)], inc=ve)
                    expd[g] = P.op("scalar", lambda e, r=r, n0=n0: e.activation(
                        ptb[:, r * 512 + n0: (r + 1) * 512], sps[r][:, n0:512], AF.Exp,
                        bias=biasv[:, 0:1], scale=SCALE),
                        waits=[(ve, tad), (mm, ptfree.get(g - 3, 0))], inc=sc)
                    sfree[g] = expd[g]
                    P.op("tensor", lambda e, r=r, n0=n0, ps=own_ps_s, kt=kt: e.matmul(
                        ps[:, n0:512], ones128[:, :], ptb[:, r * 512 + n0: (r + 1) * 512],
                        start=(kt == 0), stop=False, skip_group_check=True),
                        waits=[(sc, expd[g])])
                    ptfree[g] = P.op("tensor", lambda e, r=r, n0=n0, ps=own_ps_c, own=own, kt=kt, kvh=kvh: e.matmul(
                        ps[:, n0:512],
                        v_own[:, (own * 4 + kt) * 1024 + kvh * 128: (own * 4 + kt) * 1024 + (kvh + 1) * 128],
                        ptb[:, r * 512 + n0: (r + 1) * 512],
                        start=(kt == 0), stop=False, skip_group_check=True),
                        inc=mm)
                    g += 1

            for p in range(NSLOT):
                for kt in range(4):
                    r = g % 3
                    smm = P.op("tensor", lambda e, r=r, bk=bk, bt=bt, p=p, kt=kt: e.matmul(
                        sps[r][:, :],
                        kslot[:, (bk * NSLOT + p) * 512 + kt * 128:(bk * NSLOT + p) * 512 + (kt + 1) * 128],
                        qslot[:, (bt * NSLOT + p) * 512:(bt * NSLOT + p + 1) * 512],
                        start=True, stop=True, skip_group_check=True),
                        waits=[(sc, sfree.get(g - 3, 0)), (gd, gath[kvh]), (gd, qgat[t])],
                        inc=mm)
                    expd[g] = P.op("scalar", lambda e, r=r: e.activation(
                        ptb[:, r * 512:(r + 1) * 512], sps[r][:, :], AF.Exp,
                        bias=biasv[:, 0:1], scale=SCALE),
                        waits=[(mm, smm), (mm, ptfree.get(g - 3, 0))], inc=sc)
                    sfree[g] = expd[g]
                    P.op("tensor", lambda e, r=r, p=p: e.matmul(
                        sumA_ps[:, :], flagA[:, p * 128:(p + 1) * 128], ptb[:, r * 512:(r + 1) * 512],
                        start=False, stop=False, skip_group_check=True),
                        waits=[(sc, expd[g])])
                    P.op("tensor", lambda e, r=r, p=p: e.matmul(
                        sumB_ps[:, :], flagB[:, p * 128:(p + 1) * 128], ptb[:, r * 512:(r + 1) * 512],
                        start=False, stop=False, skip_group_check=True))
                    P.op("tensor", lambda e, r=r, bk=bk, p=p, kt=kt: e.matmul(
                        ctxA_ps[:, :],
                        vslotA[:, (bk * NSLOT + p) * 512 + kt * 128:(bk * NSLOT + p) * 512 + (kt + 1) * 128],
                        ptb[:, r * 512:(r + 1) * 512],
                        start=False, stop=False, skip_group_check=True))
                    ptfree[g] = P.op("tensor", lambda e, r=r, bk=bk, p=p, kt=kt: e.matmul(
                        ctxB_ps[:, :],
                        vslotB[:, (bk * NSLOT + p) * 512 + kt * 128:(bk * NSLOT + p) * 512 + (kt + 1) * 128],
                        ptb[:, r * 512:(r + 1) * 512],
                        start=False, stop=False, skip_group_check=True),
                        inc=mm)
                    g += 1
            slotcons[kvh] = ptfree[g - 1]
            tcons[t] = ptfree[g - 1]

            last_mm = ptfree[g - 1]
            fin = None
            for own in range(2):
                own_ps_c = ctxA_ps if own == 0 else ctxB_ps
                own_ps_s = sumA_ps if own == 0 else sumB_ps
                P.op("vector", lambda e, ps=own_ps_s: e.reciprocal(rcp[:, :], ps[:, :]),
                     waits=[(mm, last_mm)])
                fin = P.op("vector", lambda e, ps=own_ps_c, t=t, own=own: e.tensor_tensor(
                    ctx[:, (t * 2 + own) * 512:(t * 2 + own + 1) * 512], ps[:, :], rcp[:, :], ALU.mult),
                    inc=ve)
            ctxfree[t] = fin
    attn_ve = ve.n

    sA.close()

    sW = contextlib.ExitStack()
    EW = sW.enter_context
    wof = k_own[:, :].bitcast(F32)
    wof2 = v_own[:, :].bitcast(F32)
    wbbf = EW(nc.sbuf_tensor("wbbf", [64, 4096], BF16))
    wobf = EW(nc.sbuf_tensor("wobf", [128, 2 * 4096], BF16))
    xres = kslot[:, 0:4096].bitcast(F32)
    osb = vslotA[:, 0:4096].bitcast(F32)
    ops_ = [EW(nc.psum_tensor(f"ops{j}", [128, 1024], F32)) for j in range(2)]

    P.barrier(["tensor", "vector", "sync"], [ve, mm])

    wbl = dma("sync", wof[0:64, :].rearrange("p (o j) -> p o j", j=128),
              wb.ap().rearrange("o e j -> e o j"), ld)
    P.op("vector", lambda e: e.tensor_copy(wbbf[:, :], wof[0:64, :]), waits=[(ld, wbl)], inc=ve)
    wbc = P.op("vector", lambda e: e.tensor_copy(spcr[:, :], spcr[:, :]), inc=ve)

    ocast = {}
    oevac = {}
    ommd = {}
    ostw = {}
    def stage_o(oc):
        b = oc % 2
        buf = wof if b == 0 else wof2
        cpiece = ocast.get(oc - 2, wbc)
        for qtr in range(2):
            lq = dma("sync", buf[:, qtr * 2048:(qtr + 1) * 2048],
                     wo[oc][:, qtr * 2048:(qtr + 1) * 2048], ld,
                     waits=[(ve, cpiece)])
            cpiece = P.op("vector", lambda e, b=b, qtr=qtr, buf=buf: e.tensor_copy(
                wobf[:, b * 4096 + qtr * 2048: b * 4096 + (qtr + 1) * 2048],
                buf[:, qtr * 2048:(qtr + 1) * 2048]),
                waits=[(ld, lq), (mm, ommd.get(oc - 2, 0) if qtr == 0 else 0)], inc=ve)
        ocast[oc] = P.op("vector", lambda e: e.tensor_copy(spcr[:, :], spcr[:, :]), inc=ve)

    stage_o(0)
    stage_o(1)
    for oc in range(NQH):
        b = oc % 2
        if oc >= 2:
            pass
        xl0 = dma("sync", xres[:, b * 1024: b * 1024 + 512], xTA[oc * 128:(oc + 1) * 128, :], ld,
                  waits=[(ve, oevac.get(oc - 2, 0))])
        xl1 = dma("sync", xres[:, b * 1024 + 512: b * 1024 + 1024], xTB[oc * 128:(oc + 1) * 128, :], ld)
        omm = None
        for own in range(2):
            for ccc in range(NF):
                w = [(ve, ocast[oc]), (ve, attn_ve), (ve, wbc)]
                if ccc == 0 and own == 0:
                    w.append((ve, oevac.get(oc - 2, 0)))
                P.op("tensor", lambda e, oc=oc, own=own, ccc=ccc, b=b: e.matmul(
                    ops_[b][:, own * 512:(own + 1) * 512],
                    wobf[:, b * 4096 + ccc * 128: b * 4096 + (ccc + 1) * 128],
                    ctx[:, (ccc * 2 + own) * 512:(ccc * 2 + own + 1) * 512],
                    start=(ccc == 0), stop=False, skip_group_check=True),
                    waits=w)
            omm = P.op("tensor", lambda e, oc=oc, own=own, b=b: e.matmul(
                ops_[b][:, own * 512:(own + 1) * 512],
                wbbf[:, oc * 128:(oc + 1) * 128],
                mfin[0:64, own * 512:(own + 1) * 512],
                start=False, stop=True, skip_group_check=True),
                waits=[(ve, ffn_done_ve)], inc=mm)
        ommd[oc] = omm
        if oc + 2 < NQH:
            stage_o(oc + 2)
        ev = []
        for own in range(2):
            ev.append(P.op("vector", lambda e, own=own, b=b: e.tensor_tensor(
                osb[:, b * 1024 + own * 512: b * 1024 + (own + 1) * 512],
                ops_[b][:, own * 512:(own + 1) * 512],
                xres[:, b * 1024 + own * 512: b * 1024 + (own + 1) * 512], ALU.add),
                waits=[(mm, omm), (ld, xl1), (st, ostw.get(oc - 2, 0))], inc=ve))
        oevac[oc] = ev[-1]
        P.op("sync", lambda e, oc=oc, b=b: e.dma_start(
            out=outA[oc * 128:(oc + 1) * 128, :], in_=osb[:, b * 1024: b * 1024 + 512]),
            waits=[(ve, ev[0])], inc=st, inc_by=16)
        ostw[oc] = P.op("sync", lambda e, oc=oc, b=b: e.dma_start(
            out=outB[oc * 128:(oc + 1) * 128, :], in_=osb[:, b * 1024 + 512: b * 1024 + 1024]),
            waits=[(ve, ev[1])], inc=st, inc_by=16)

    P.op("sync", lambda e: None, waits=[(st, st.n)])

    with (nc.semaphore("s_ld") as h_ld, nc.semaphore("s_gd") as h_gd,
          nc.semaphore("s_mm") as h_mm, nc.semaphore("s_ve") as h_ve,
          nc.semaphore("s_sc") as h_sc, nc.semaphore("s_cc") as h_cc,
          nc.semaphore("s_st") as h_st, nc.semaphore("s_sd") as h_sd,
          nc.semaphore("s_io") as h_io, nc.Block() as block):
        H = {"ld": h_ld, "gd": h_gd, "mm": h_mm, "ve": h_ve, "sc": h_sc, "cc": h_cc,
             "st": h_st, "sd": h_sd, "io": h_io}

        def runner(name):
            def run(eng):
                for fn, ws, inc in P.ops[name]:
                    for s, v in ws:
                        eng.wait_ge(H[s.name], v)
                    inst = fn(eng) if fn is not None else None
                    if inc is not None:
                        if inst is not None:
                            inst.then_inc(H[inc[0].name], inc[1])
                        else:
                            eng.sem_inc(H[inc[0].name], inc[1])
            return run

        block.sync(runner("sync"))
        block.gpsimd(runner("gpsimd"))
        block.vector(runner("vector"))
        block.scalar(runner("scalar"))
        block.tensor(runner("tensor"))

    sW.close()
    stack.close()
    return nc



def _stage_shared(Wq, Wk, Wv, Wo, router_w, A, Bm):
    perm = np.empty(128, np.int64)
    perm[0:64] = np.arange(64) * 2
    perm[64:128] = np.arange(64) * 2 + 1
    wq = np.ascontiguousarray(Wq.reshape(NQH, 128, NF, 128)[:, perm].transpose(0, 3, 2, 1)
                              .reshape(NQH, 128, D))
    wk = np.ascontiguousarray(Wk.reshape(NKVH, 128, NF, 128)[:, perm].transpose(0, 3, 2, 1)
                              .reshape(NKVH, 128, D))
    wv = np.ascontiguousarray(Wv.reshape(2, 512, NF, 128).transpose(0, 2, 3, 1))
    wo = np.ascontiguousarray(Wo.reshape(NQH, 128, NF, 128).transpose(0, 3, 2, 1)
                              .reshape(NQH, 128, D))
    a_aug = np.concatenate([A.reshape(64, D), router_w], axis=0)
    wa = np.ascontiguousarray(a_aug.reshape(68, NF, 128).transpose(1, 2, 0))
    wb = np.ascontiguousarray(Bm.reshape(4, 32, 128, 16).transpose(1, 0, 3, 2).reshape(32, 64, 128))
    tri = np.where(np.arange(128)[:, None] <= np.arange(128)[None, :], 0.0, -1e9).astype(np.float32)
    ident = np.eye(128, dtype=np.float32)
    e2g = np.zeros((4, 64), np.float32)
    for ee in range(4):
        e2g[ee, ee * 16:(ee + 1) * 16] = 1.0
    return wq, wk, wv, wo, wa, wb, tri, ident, e2g


def _stage_core(c):
    cA, cB = c, 7 - c
    pos = np.stack([512 * cA + np.arange(CH, dtype=np.float32),
                    512 * cB + np.arange(CH, dtype=np.float32)]).astype(np.float32)
    slots = []
    for p in range(NSLOT):
        if p < cA:
            slots.append((p, 0, 0))
        else:
            j = p - cA
            slots.append((7 - j, 1, 1))
    dims = np.arange(128, dtype=np.int64)
    kidx = np.zeros((128, NKVH * NSLOT), np.int32)
    vai = np.zeros((128, NKVH * NSLOT * 4), np.int32)
    vbi = np.zeros((128, NKVH * NSLOT * 4), np.int32)
    qix = np.zeros((128, NQH * NSLOT), np.int32)
    fix = np.zeros((128, 2 * NSLOT), np.int32)
    ZROW = NCORE * 4096 * 4
    for p, (rank, seg, own) in enumerate(slots):
        for qh in range(NQH):
            qix[:, qh * NSLOT + p] = qh * 256 + dims * 2 + own
        fix[:, 0 * NSLOT + p] = 0 if own == 0 else 32
        fix[:, 1 * NSLOT + p] = 0 if own == 1 else 32
        for kvh in range(NKVH):
            kidx[:, kvh * NSLOT + p] = rank * 4096 + kvh * 256 + dims * 2 + seg
            for kt in range(4):
                col = kvh * NSLOT * 4 + p * 4 + kt
                vrow = rank * 16384 + 8192 + kvh * 1024 + seg * 512 + kt * 128 + dims
                if own == 0:
                    vai[:, col] = vrow
                    vbi[:, col] = ZROW + dims
                else:
                    vai[:, col] = ZROW + dims
                    vbi[:, col] = vrow
    return pos, kidx, vai, vbi, qix, fix


def kernel(L0_attn_hidden_states, L0_attn_Wq, L0_attn_Wk, L0_attn_Wv, L0_attn_Wo,
           L0_ffn_router_w, L0_ffn_A, L0_ffn_B):
    x = np.asarray(L0_attn_hidden_states, dtype=np.float32)
    wq, wk, wv, wo, wa, wb, tri, ident, e2g = _stage_shared(
        np.asarray(L0_attn_Wq, np.float32), np.asarray(L0_attn_Wk, np.float32),
        np.asarray(L0_attn_Wv, np.float32), np.asarray(L0_attn_Wo, np.float32),
        np.asarray(L0_ffn_router_w, np.float32), np.asarray(L0_ffn_A, np.float32),
        np.asarray(L0_ffn_B, np.float32))

    in_maps = []
    for c in range(NCORE):
        cA, cB = c, 7 - c
        pos, kidx, vai, vbi, qix, fix = _stage_core(c)
        in_maps.append({
            "xTA": np.ascontiguousarray(x[0, 512 * cA:512 * (cA + 1), :].T),
            "xTB": np.ascontiguousarray(x[1, 512 * cB:512 * (cB + 1), :].T),
            "wq": wq, "wk": wk, "wv": wv, "wo": wo, "wa": wa, "wb": wb,
            "tri": tri, "ident": ident, "e2g": e2g, "pos": pos,
            "kidx": kidx, "vai": vai, "vbi": vbi, "qix": qix, "fix": fix,
        })

    nc = build_program()
    res = run_bass_kernel_spmd(nc, in_maps, core_ids=list(range(NCORE)))

    out = np.empty((2, 4096, 4096), dtype=np.float32)
    for c in range(NCORE):
        cA, cB = c, 7 - c
        out[0, 512 * cA:512 * (cA + 1), :] = np.asarray(res.results[c]["outA"]).T
        out[1, 512 * cB:512 * (cB + 1), :] = np.asarray(res.results[c]["outB"]).T
    return out


# version 09
# speedup vs baseline: 1.2659x; 1.0052x over previous
import contextlib
import numpy as np
import ml_dtypes
import concourse.bass as bass
import concourse.mybir as mybir
from concourse.bass import IndirectOffsetOnAxis
from concourse.bass_utils import run_bass_kernel_spmd

F32 = mybir.dt.float32
BF16 = mybir.dt.bfloat16
I32 = mybir.dt.int32
AF = mybir.ActivationFunctionType
ALU = mybir.AluOpType
AX = mybir.AxisListType

NCORE = 8
D = 4096
NF = 32
CH = 512
NQH, NKVH, HD = 32, 8, 128
NSLOT = 7
SCALE = 1.0 / float(np.sqrt(128.0))
EXPB = -35.0
TWO_PI = 6.283185307179586
MAGIC = 12582912.0
LN10K = float(np.log(10000.0))


class SemT:
    def __init__(self, name):
        self.name = name
        self.n = 0

class Plan:
    ENGS = ("sync", "gpsimd", "vector", "scalar", "tensor")

    def __init__(self):
        self.ops = {e: [] for e in self.ENGS}
        self.lastwait = {}

    def op(self, eng, fn, waits=(), inc=None, inc_by=1):
        ws = []
        for s, v in waits:
            if v <= 0:
                continue
            key = (eng, s.name)
            if self.lastwait.get(key, -1) >= v:
                continue
            self.lastwait[key] = v
            ws.append((s, v))
        tgt = None
        if inc is not None:
            inc.n += inc_by
            tgt = inc.n
        self.ops[eng].append((fn, ws, (inc, inc_by) if inc is not None else None))
        return tgt

    def barrier(self, engs, sems, exclude=()):
        for e in engs:
            for s in sems:
                if s.name in exclude:
                    continue
                v = s.n
                if v <= 0:
                    continue
                key = (e, s.name)
                if self.lastwait.get(key, -1) >= v:
                    continue
                self.lastwait[key] = v
                self.ops[e].append((None, [(s, v)], None))



def build_program():
    nc = bass.Bass("TRN2", target_bir_lowering=False, detect_race_conditions=False)

    xTA = nc.dram_tensor("xTA", [D, CH], F32, kind="ExternalInput")
    xTB = nc.dram_tensor("xTB", [D, CH], F32, kind="ExternalInput")
    wq = nc.dram_tensor("wq", [NQH, 128, D], F32, kind="ExternalInput")
    wk = nc.dram_tensor("wk", [NKVH, 128, D], F32, kind="ExternalInput")
    wv = nc.dram_tensor("wv", [2, NF, 128, 512], F32, kind="ExternalInput")
    wo = nc.dram_tensor("wo", [NQH, 128, D], F32, kind="ExternalInput")
    wa = nc.dram_tensor("wa", [NF, 128, 68], F32, kind="ExternalInput")
    wb = nc.dram_tensor("wb", [32, 64, 128], F32, kind="ExternalInput")
    tri_d = nc.dram_tensor("tri", [128, 128], F32, kind="ExternalInput")
    id_d = nc.dram_tensor("ident", [128, 128], F32, kind="ExternalInput")
    e2g_d = nc.dram_tensor("e2g", [4, 64], F32, kind="ExternalInput")
    pos_d = nc.dram_tensor("pos", [2, CH], F32, kind="ExternalInput")
    kidx_d = nc.dram_tensor("kidx", [128, NKVH * NSLOT], I32, kind="ExternalInput")
    vai_d = nc.dram_tensor("vai", [128, NKVH * NSLOT * 4], I32, kind="ExternalInput")
    vbi_d = nc.dram_tensor("vbi", [128, NKVH * NSLOT * 4], I32, kind="ExternalInput")
    qix_d = nc.dram_tensor("qix", [128, NQH * NSLOT], I32, kind="ExternalInput")
    fix_d = nc.dram_tensor("fix", [128, 2 * NSLOT], I32, kind="ExternalInput")

    outA = nc.dram_tensor("outA", [D, CH], F32, kind="ExternalOutput")
    outB = nc.dram_tensor("outB", [D, CH], F32, kind="ExternalOutput")

    ag_in = nc.dram_tensor("ag_in", [4096, 512], BF16)
    ag_out = nc.dram_tensor("ag_out", [NCORE * 4096 + 32, 512], BF16)
    qsp = nc.dram_tensor("qsp", [NQH * 256, 512], BF16)
    c2 = nc.dram_tensor("c2", [33, 128], F32)

    ag_k = ag_out.ap()
    ag_v = ag_out.ap().rearrange("r (a b) -> (r a) b", a=4, b=128)

    P = Plan()
    ld = SemT("ld")
    gd = SemT("gd")
    mm = SemT("mm")
    ve = SemT("ve")
    sc = SemT("sc")
    cc = SemT("cc")
    st = SemT("st")
    sd = SemT("sd")
    io = SemT("io")
    SEMS = [ld, gd, mm, ve, sc, cc, st, sd, io]

    stack = contextlib.ExitStack()
    E = stack.enter_context

    kslot = E(nc.sbuf_tensor("kslot", [128, 2 * NSLOT * 512], BF16))
    vslotA = E(nc.sbuf_tensor("vslotA", [128, 2 * NSLOT * 512], BF16))
    vslotB = E(nc.sbuf_tensor("vslotB", [128, 2 * NSLOT * 512], BF16))
    qslot = E(nc.sbuf_tensor("qslot", [128, 2 * NSLOT * 512], BF16))
    flg_f = E(nc.sbuf_tensor("flg_f", [128, 128], F32))
    spcr = E(nc.sbuf_tensor("spcr", [128, 8], F32))
    cosq = E(nc.sbuf_tensor("cosq", [128, 1024], F32))
    sinq = E(nc.sbuf_tensor("sinq", [128, 1024], F32))
    k_own = E(nc.sbuf_tensor("k_own", [128, NKVH * 1024], BF16))
    v_own = E(nc.sbuf_tensor("v_own", [128, 8 * 1024], BF16))
    mfin = E(nc.sbuf_tensor("mfin", [128, 1024], BF16))
    flagA = E(nc.sbuf_tensor("flagA", [128, NSLOT * 128], BF16))
    flagB = E(nc.sbuf_tensor("flagB", [128, NSLOT * 128], BF16))
    tri_sb = E(nc.sbuf_tensor("tri_sb", [128, 128], F32))
    id_sb = E(nc.sbuf_tensor("id_sb", [128, 128], F32))
    e2g_sb = E(nc.sbuf_tensor("e2g_sb", [4, 64], F32))
    kidx_sb = E(nc.sbuf_tensor("kidx_sb", [128, NKVH * NSLOT], I32))
    vai_sb = E(nc.sbuf_tensor("vai_sb", [128, NKVH * NSLOT * 4], I32))
    vbi_sb = E(nc.sbuf_tensor("vbi_sb", [128, NKVH * NSLOT * 4], I32))
    qix_sb = E(nc.sbuf_tensor("qix_sb", [128, NQH * NSLOT], I32))
    fix_sb = E(nc.sbuf_tensor("fix_sb", [128, 2 * NSLOT], I32))
    ones128 = E(nc.sbuf_tensor("ones128", [128, 128], BF16))
    onesinv = E(nc.sbuf_tensor("onesinv", [128, 128], BF16))
    biasv = E(nc.sbuf_tensor("biasv", [128, 1], F32))
    epsv = E(nc.sbuf_tensor("epsv", [128, 1], F32))
    zrow = E(nc.sbuf_tensor("zrow", [32, 512], BF16))
    c2_sb = E(nc.sbuf_tensor("c2_sb", [33, 128], F32))

    def dma(eng, out_ap, in_ap, sem, waits=()):
        return P.op(eng, lambda e, o=out_ap, i=in_ap: e.dma_start(out=o, in_=i),
                    waits=waits, inc=sem, inc_by=16)

    t0 = []
    t0.append(dma("sync", tri_sb[:, :], tri_d[:, :], ld))
    t0.append(dma("sync", id_sb[:, :], id_d[:, :], ld))
    t0.append(dma("sync", e2g_sb[:, :], e2g_d[:, :], ld))
    t0.append(dma("sync", kidx_sb[:, :], kidx_d[:, :], ld))
    t0.append(dma("sync", vai_sb[:, :], vai_d[:, :], ld))
    t0.append(dma("sync", vbi_sb[:, :], vbi_d[:, :], ld))
    t0.append(dma("sync", qix_sb[:, :], qix_d[:, :], ld))
    t0.append(dma("sync", fix_sb[:, :], fix_d[:, :], ld))
    tbl_ld = ld.n

    mset = []
    P.op("vector", lambda e: e.memset(ones128[:, :], 1.0))
    P.op("vector", lambda e: e.memset(onesinv[:, :], 1.0 / 4096.0))
    P.op("vector", lambda e: e.memset(biasv[:, :], EXPB))
    P.op("vector", lambda e: e.memset(epsv[:, :], 1e-5))
    P.op("vector", lambda e: e.memset(zrow[:, :], 0.0))
    P.op("vector", lambda e: e.memset(c2_sb[:, :], 0.0))
    msets_done = P.op("vector", lambda e: e.memset(c2_sb[0:1, :], 1.0), inc=ve)

    c2w = dma("scalar", c2[:, :], c2_sb[:, :], sd, waits=[(ve, msets_done)])
    zpw = dma("scalar", ag_out[NCORE * 4096: NCORE * 4096 + 32, :], zrow[:, :], sd,
              waits=[(ve, msets_done)])

    flg_done = []
    for t in range(2):
        for p in range(NSLOT):
            g = P.op("gpsimd",
                     lambda e, t=t, p=p: e.indirect_dma_start(
                         flg_f[:, :], None, c2[:, :],
                         IndirectOffsetOnAxis(fix_sb[:, t * NSLOT + p: t * NSLOT + p + 1], 0)),
                     waits=[(ld, tbl_ld), (sd, c2w), (ve, flg_done[-1] if flg_done else 0)],
                     inc=gd, inc_by=16)
            dstf = flagA if t == 0 else flagB
            flg_done.append(P.op("vector",
                                 lambda e, d=dstf, p=p: e.tensor_copy(d[:, p * 128:(p + 1) * 128], flg_f[:, :]),
                                 waits=[(gd, g)], inc=ve))
    flags_ve = ve.n

    with contextlib.ExitStack() as ph:
        EE = ph.enter_context
        invf = EE(nc.sbuf_tensor("invf", [64, 1], F32))
        posr = EE(nc.sbuf_tensor("posr", [64, 512], F32))
        th = EE(nc.sbuf_tensor("th", [64, 512], F32))
        u1 = EE(nc.sbuf_tensor("u1", [64, 512], F32))
        n1 = EE(nc.sbuf_tensor("n1", [64, 512], F32))
        r1 = EE(nc.sbuf_tensor("r1", [64, 512], F32))
        P.op("gpsimd", lambda e: e.iota(invf[:, :], pattern=[[0, 1]], channel_multiplier=1,
                                        allow_small_or_imprecise_dtypes=True), inc=io, inc_by=1)
        iv = P.op("scalar", lambda e: e.activation(invf[:, :], invf[:, :], AF.Exp, scale=-LN10K / 64.0),
                  waits=[(io, 1)], inc=sc)
        prev_tv = 0
        prev_tr = 0
        for own in range(2):
            pr = P.op("scalar",
                      lambda e, own=own: e.dma_start(out=posr[:, :],
                                                     in_=pos_d[own:own + 1, :].broadcast_to([64, CH])),
                      waits=[(ld, tbl_ld), (ve, prev_tv)], inc=sd, inc_by=16)
            tv = P.op("vector", lambda e: e.tensor_scalar_mul(th[:, :], posr[:, :], invf[0:64, 0:1]),
                      waits=[(sd, pr), (sc, iv)], inc=ve)
            prev_tv = tv
            for trig, off in (("sin", 0.0), ("cos", 0.25)):
                P.op("vector", lambda e, off=off: e.tensor_scalar(u1[:, :], th[:, :], 1.0 / TWO_PI, off, ALU.mult, ALU.add),
                     waits=[(ve, tv), (sc, prev_tr)])
                P.op("vector", lambda e: e.tensor_scalar_add(n1[:, :], u1[:, :], MAGIC))
                P.op("vector", lambda e: e.tensor_scalar_add(n1[:, :], n1[:, :], -MAGIC))
                P.op("vector", lambda e: e.tensor_sub(r1[:, :], u1[:, :], n1[:, :]))
                rv = P.op("vector", lambda e: e.tensor_scalar_mul(r1[:, :], r1[:, :], TWO_PI), inc=ve)
                if trig == "sin":
                    sn = P.op("scalar", lambda e, own=own: e.activation(sinq[64:128, own * 512:(own + 1) * 512],
                                                                        r1[:, :], AF.Sin),
                              waits=[(ve, rv)], inc=sc)
                    prev_tr = sn
                    P.op("vector", lambda e, own=own: e.tensor_scalar_mul(
                        sinq[0:64, own * 512:(own + 1) * 512], sinq[64:128, own * 512:(own + 1) * 512], -1.0),
                        waits=[(sc, sn)], inc=ve)
                else:
                    cs = P.op("scalar", lambda e, own=own: e.activation(cosq[0:64, own * 512:(own + 1) * 512],
                                                                        r1[:, :], AF.Sin),
                              waits=[(ve, rv)], inc=sc)
                    prev_tr = cs
                    P.op("scalar", lambda e, own=own: e.dma_start(
                        out=cosq[64:128, own * 512:(own + 1) * 512],
                        in_=cosq[0:64, own * 512:(own + 1) * 512]),
                        waits=[(sc, cs)], inc=sd, inc_by=16)
    rope_tabs_ve = ve.n
    rope_tabs_sd = sd.n

    s1 = contextlib.ExitStack()
    E1 = s1.enter_context
    xn = E1(nc.sbuf_tensor("xn", [128, 2 * NF * 512], BF16))
    xst = E1(nc.sbuf_tensor("xst", [128, 2 * 512], F32))
    xbst = E1(nc.sbuf_tensor("xbst", [128, 2 * 512], BF16))
    xsq = E1(nc.sbuf_tensor("xsq", [128, 2 * 512], BF16))
    wf32 = E1(nc.sbuf_tensor("wf32", [128, 2 * 1024], F32))
    wbf = E1(nc.sbuf_tensor("wbf", [128, 2 * 4096], BF16))
    mean_sb = vslotA[:, 3584:5632].bitcast(F32)
    rstd_sb = vslotB[:, 3584:5632].bitcast(F32)
    qf = kslot[:, 3584:5632].bitcast(F32)
    qs = qslot[:, 3584:5632].bitcast(F32)
    wabf = qslot[:, 0:NF * 68]
    lg_sb = vslotB[:, 0:2048].bitcast(F32)[0:4]
    gt_sb = vslotA[:, 0:2048].bitcast(F32)[0:4]
    gb_sb = kslot[:, 0:1024].bitcast(F32)[0:64]
    gsm = E1(nc.sbuf_tensor("gsm", [128, 128], F32))
    gsm1 = E1(nc.sbuf_tensor("gsm1", [128, 48], F32))

    wal = dma("sync", wf32[:, 0:2040].rearrange("p (f e) -> p f e", e=68),
              wa.ap()[0:30].rearrange("f i e -> i f e"), ld)
    wc1 = P.op("vector", lambda e: e.tensor_copy(wabf[:, 0:2040], wf32[:, 0:2040]),
               waits=[(ld, wal)], inc=ve)
    wal2 = dma("sync", wf32[:, 0:136].rearrange("p (f e) -> p f e", e=68),
               wa.ap()[30:32].rearrange("f i e -> i f e"), ld, waits=[(ve, wc1)])
    P.op("vector", lambda e: e.tensor_copy(wabf[:, 2040:2176], wf32[:, 0:136]),
         waits=[(ld, wal2)], inc=ve)
    wacast = P.op("vector", lambda e: e.tensor_copy(spcr[:, :], spcr[:, :]), inc=ve)

    xn_done = {}

    with contextlib.ExitStack() as ph:
        EE = ph.enter_context
        mean_ps = EE(nc.psum_tensor("mean_ps", [128, 1024], F32))
        sq_ps = EE(nc.psum_tensor("sq_ps", [128, 1024], F32))
        xcons = {}
        xsrc = (xTA, xTB)
        for own in range(2):
            for f in range(NF):
                i = own * NF + f
                b = i % 2
                l_ = dma("sync", xst[:, b * 512:(b + 1) * 512], xsrc[own][f * 128:(f + 1) * 128, :], ld,
                         waits=[(mm, xcons.get(i - 2, 0)), (ve, wacast if i == 0 else 0)])
                cv_ = P.op("vector", lambda e, b=b: e.tensor_copy(xbst[:, b * 512:(b + 1) * 512],
                                                                  xst[:, b * 512:(b + 1) * 512]),
                           waits=[(ld, l_)], inc=ve)
                sq_ = P.op("scalar", lambda e, b=b: e.activation(xsq[:, b * 512:(b + 1) * 512],
                                                                 xst[:, b * 512:(b + 1) * 512], AF.Square),
                           waits=[(ld, l_)], inc=sc)
                P.op("tensor", lambda e, own=own, f=f, b=b: e.matmul(
                    mean_ps[:, own * 512:(own + 1) * 512], onesinv[:, :], xbst[:, b * 512:(b + 1) * 512],
                    start=(f == 0), stop=(f == NF - 1)),
                    waits=[(ve, cv_)])
                xcons[i] = P.op("tensor", lambda e, own=own, f=f, b=b: e.matmul(
                    sq_ps[:, own * 512:(own + 1) * 512], onesinv[:, :], xsq[:, b * 512:(b + 1) * 512],
                    start=(f == 0), stop=(f == NF - 1)),
                    waits=[(sc, sq_)], inc=mm)
        stats_mm = mm.n
        P.op("vector", lambda e: e.tensor_copy(mean_sb, mean_ps[:, :]), waits=[(mm, stats_mm)])
        P.op("vector", lambda e: e.tensor_mul(qf, mean_sb, mean_sb))
        vfin = P.op("vector", lambda e: e.tensor_tensor(qf, sq_ps[:, :], qf, ALU.subtract), inc=ve)
        sfin = P.op("scalar", lambda e: e.activation(qf, qf, AF.Sqrt, bias=epsv[:, 0:1]),
                    waits=[(ve, vfin)], inc=sc)
        rfin = P.op("vector", lambda e: e.reciprocal(rstd_sb, qf), waits=[(sc, sfin)], inc=ve)

        for own in range(2):
            for f in range(NF):
                i = own * NF + f
                b = i % 2
                l_ = dma("sync", xst[:, b * 512:(b + 1) * 512], xsrc[own][f * 128:(f + 1) * 128, :], ld,
                         waits=[(ve, xn_done.get(i - 2, rfin))])
                P.op("vector", lambda e, b=b, own=own: e.tensor_tensor(
                    qf[:, 0:512], xst[:, b * 512:(b + 1) * 512], mean_sb[:, own * 512:(own + 1) * 512],
                    ALU.subtract), waits=[(ld, l_), (ve, rfin)])
                xn_done[i] = P.op("vector", lambda e, i=i, own=own: e.tensor_tensor(
                    xn[:, i * 512:(i + 1) * 512], qf[:, 0:512], rstd_sb[:, own * 512:(own + 1) * 512],
                    ALU.mult), inc=ve)
        xn_all = ve.n

    xn_done_outer = xn_done

    rope_state = {"last_swap": 0}

    def rope_emit(ps_ap, out_bf_ap, wait_mm, final_wait_sd=0):
        c1 = P.op("vector", lambda e: e.tensor_copy(qf, ps_ap),
                  waits=[(mm, wait_mm), (sd, rope_state["last_swap"])], inc=ve)
        s1_ = P.op("scalar", lambda e: e.dma_start(out=qs[0:64], in_=qf[64:128]),
                   waits=[(ve, c1)], inc=sd, inc_by=16)
        s2_ = P.op("scalar", lambda e: e.dma_start(out=qs[64:128], in_=qf[0:64, :]),
                   inc=sd, inc_by=16)
        rope_state["last_swap"] = s2_
        P.op("vector", lambda e: e.tensor_mul(qf, qf, cosq[:, :]),
             waits=[(ve, rope_tabs_ve), (sd, rope_tabs_sd), (sd, s2_)])
        P.op("vector", lambda e: e.tensor_mul(qs, qs, sinq[:, :]))
        done = P.op("vector", lambda e: e.tensor_add(out_bf_ap, qf, qs),
                    waits=[(sd, final_wait_sd)], inc=ve)
        return c1, done

    with contextlib.ExitStack() as ph:
        EE = ph.enter_context
        kps = [EE(nc.psum_tensor(f"kps{j}", [128, 1024], F32)) for j in range(2)]
        kfree = {}
        kagw = []
        cast_done = {}
        mm_done = {}
        def stage_k(kvh):
            b = kvh % 2
            prevc = cast_done.get(kvh - 2, wacast if kvh < 2 else 0)
            cpiece = prevc
            for qtr in range(4):
                lq = dma("sync", wf32[:, b * 1024:(b + 1) * 1024],
                         wk[kvh][:, qtr * 1024:(qtr + 1) * 1024], ld,
                         waits=[(ve, cpiece)])
                cpiece = P.op("vector", lambda e, b=b, kvh=kvh, qtr=qtr: e.tensor_copy(
                    wbf[:, b * 4096 + qtr * 1024: b * 4096 + (qtr + 1) * 1024],
                    wf32[:, b * 1024:(b + 1) * 1024]),
                    waits=[(ld, lq), (mm, mm_done.get(kvh - 2, 0) if qtr == 0 else 0)], inc=ve)
            cast_done[kvh] = P.op("vector", lambda e: e.tensor_copy(spcr[:, :], spcr[:, :]), inc=ve)

        stage_k(0)
        stage_k(1)
        for kvh in range(NKVH):
            b = kvh % 2
            for own in range(2):
                for f in range(NF):
                    w = [(ve, cast_done[kvh]), (ve, xn_done[own * NF + f])]
                    if f == 0 and own == 0:
                        w.append((ve, kfree.get(kvh - 2, 0)))
                    last = (own == 1 and f == NF - 1)
                    mm_done[kvh] = P.op("tensor", lambda e, kvh=kvh, own=own, f=f, b=b: e.matmul(
                        kps[b][:, own * 512:(own + 1) * 512],
                        wbf[:, b * 4096 + f * 128: b * 4096 + (f + 1) * 128],
                        xn[:, (own * NF + f) * 512:(own * NF + f + 1) * 512],
                        start=(f == 0), stop=(f == NF - 1)),
                        waits=w, inc=(mm if last else None))
            if kvh + 2 < NKVH:
                stage_k(kvh + 2)
            kf, kdone = rope_emit(kps[b][:, :], k_own[:, kvh * 1024:(kvh + 1) * 1024], mm_done[kvh])
            kfree[kvh] = kf
            for own in range(2):
                kagw.append(P.op("scalar", lambda e, kvh=kvh, own=own: e.dma_start(
                    out=ag_in[kvh * 256 + own: kvh * 256 + 256: 2, :],
                    in_=k_own[:, kvh * 1024 + own * 512: kvh * 1024 + (own + 1) * 512]),
                    waits=[(ve, kdone)], inc=sd, inc_by=16))
    k_phase_ve = ve.n
    kcast_last = cast_done[NKVH - 1]

    P.barrier(["tensor"], [ve])
    with contextlib.ExitStack() as ph:
        EE = ph.enter_context
        vps = [EE(nc.psum_tensor(f"vps{j}", [128, 512], F32)) for j in range(8)]
        vagw = []
        vcast = {}
        vfcons = {}
        for h in range(2):
            vmm = None
            for f in range(NF):
                b = f % 2
                l_ = dma("sync", wf32[:, b * 1024: b * 1024 + 512],
                         wv[h][f][:, :], ld,
                         waits=[(ve, vcast.get((h, f - 2),
                                               vcast.get((0, NF - 2 + f), 0) if h == 1 else 0)),
                                (ve, kcast_last)])
                P.op("vector", lambda e, b=b: e.tensor_copy(
                    wbf[:, b * 512:(b + 1) * 512], wf32[:, b * 1024: b * 1024 + 512]),
                    waits=[(ld, l_), (mm, vfcons.get((h, f - 2),
                                                    vfcons.get((0, NF - 1), 0) if h == 1 else 0))], inc=ve)
                vcast[(h, f)] = P.op("vector", lambda e: e.tensor_copy(spcr[:, :], spcr[:, :]), inc=ve)
                for own in range(2):
                    for kt in range(4):
                        j = own * 4 + kt
                        w = [(ve, vcast[(h, f)])]
                        if f == 0:
                            w.append((ve, k_phase_ve if h == 0 else 0))
                            w.append((ve, vevac if h == 1 else 0))
                        last = (own == 1 and kt == 3)
                        vmm = P.op("tensor", lambda e, own=own, kt=kt, f=f, b=b, j=j: e.matmul(
                            vps[j][:, :],
                            xn[:, (own * NF + f) * 512 + kt * 128: (own * NF + f) * 512 + (kt + 1) * 128],
                            wbf[:, b * 512:(b + 1) * 512],
                            start=(f == 0), stop=(f == NF - 1)),
                            waits=w, inc=(mm if last else None))
                vfcons[(h, f)] = vmm
            ev = []
            for own in range(2):
                for kt in range(4):
                    j = own * 4 + kt
                    ev.append(P.op("vector", lambda e, own=own, kt=kt, j=j, h=h: e.tensor_copy(
                        v_own[:, (own * 4 + kt) * 1024 + h * 512: (own * 4 + kt) * 1024 + (h + 1) * 512],
                        vps[j][:, :]), waits=[(mm, vmm)], inc=ve))
            vevac = ve.n
            for own in range(2):
                for kt in range(4):
                    for kk in range(4):
                        kvh = h * 4 + kk
                        base = 8192 + kvh * 1024 + own * 512 + kt * 128
                        vagw.append(P.op("scalar", lambda e, own=own, kt=kt, kk=kk, base=base, h=h: e.dma_start(
                            out=ag_in.ap().rearrange("r (a b) -> (r a) b", a=4, b=128)[base: base + 128, :],
                            in_=v_own[:, (own * 4 + kt) * 1024 + h * 512 + kk * 128:
                                      (own * 4 + kt) * 1024 + h * 512 + (kk + 1) * 128]),
                            waits=[(ve, vevac)], inc=sd, inc_by=16))

    agq = P.op("gpsimd", lambda e: e.collective_compute(
        "AllGather", ALU.bypass, replica_groups=[list(range(NCORE))],
        ins=[ag_in[:, :]], outs=[ag_out[0:NCORE * 4096, :]]),
        waits=[(sd, sd.n), (gd, gd.n)], inc=cc)

    vcast_last = vcast[(1, NF - 1)]
    v_mm_last = vfcons[(1, NF - 1)]
    P.barrier(["tensor"], [ve])
    with contextlib.ExitStack() as ph:
        EE = ph.enter_context
        qps = [EE(nc.psum_tensor(f"qps{j}", [128, 1024], F32)) for j in range(2)]
        qfree = {}
        qspw = {}
        qcast = {}
        qmm = {}
        def stage_q(qh):
            b = qh % 2
            cpiece = qcast.get(qh - 2, vcast_last)
            for qtr in range(4):
                lq = dma("sync", wf32[:, b * 1024:(b + 1) * 1024],
                         wq[qh][:, qtr * 1024:(qtr + 1) * 1024], ld,
                         waits=[(ve, cpiece)])
                cpiece = P.op("vector", lambda e, b=b, qtr=qtr: e.tensor_copy(
                    wbf[:, b * 4096 + qtr * 1024: b * 4096 + (qtr + 1) * 1024],
                    wf32[:, b * 1024:(b + 1) * 1024]),
                    waits=[(ld, lq), (mm, qmm.get(qh - 2, v_mm_last) if qtr == 0 else 0)], inc=ve)
            qcast[qh] = P.op("vector", lambda e: e.tensor_copy(spcr[:, :], spcr[:, :]), inc=ve)

        stage_q(0)
        stage_q(1)
        for qh in range(NQH):
            b = qh % 2
            for own in range(2):
                for f in range(NF):
                    w = [(ve, qcast[qh])]
                    if f == 0 and own == 0:
                        w.append((ve, qfree.get(qh - 2, 0)))
                    last = (own == 1 and f == NF - 1)
                    qmm[qh] = P.op("tensor", lambda e, own=own, f=f, b=b: e.matmul(
                        qps[b][:, own * 512:(own + 1) * 512],
                        wbf[:, b * 4096 + f * 128: b * 4096 + (f + 1) * 128],
                        xn[:, (own * NF + f) * 512:(own * NF + f + 1) * 512],
                        start=(f == 0), stop=(f == NF - 1)),
                        waits=w, inc=(mm if last else None))
            if qh + 2 < NQH:
                stage_q(qh + 2)
            qfh, qdone = rope_emit(qps[b][:, :], xsq[:, :], qmm[qh],
                                   final_wait_sd=qspw.get((qh - 1, 1), 0))
            qfree[qh] = qfh
            for own in range(2):
                qspw[(qh, own)] = P.op("scalar", lambda e, qh=qh, own=own: e.dma_start(
                    out=qsp[qh * 256 + own: qh * 256 + 256: 2, :],
                    in_=xsq[:, own * 512:(own + 1) * 512]),
                    waits=[(ve, qdone)], inc=sd, inc_by=16)
    q_phase_done_sd = sd.n
    q_phase_done_ve = ve.n

    P.barrier(["tensor"], [ve])
    with contextlib.ExitStack() as ph:
        EE = ph.enter_context
        maug = EE(nc.psum_tensor("maug", [68, 1024], F32))
        smps = EE(nc.psum_tensor("smps", [128, 4], F32))
        gtps = EE(nc.psum_tensor("gtps", [4, 128], F32))
        gbps = EE(nc.psum_tensor("gbps", [64, 512], F32))
        amm = None
        for own in range(2):
            for f in range(NF):
                last = (own == 1 and f == NF - 1)
                amm = P.op("tensor", lambda e, own=own, f=f: e.matmul(
                    maug[:, own * 512:(own + 1) * 512],
                    wabf[:, f * 68:(f + 1) * 68],
                    xn[:, (own * NF + f) * 512:(own * NF + f + 1) * 512],
                    start=(f == 0), stop=(f == NF - 1)),
                    waits=[(ve, wacast), (ve, xn_all)], inc=(mm if last else None))
        lgc = P.op("vector", lambda e: e.tensor_copy(lg_sb, maug[64:68, :]), waits=[(mm, amm)], inc=ve)
        tdone = {}
        prev_lt = 0
        for k in range(8):
            own, s = divmod(k, 4)
            tmm = P.op("tensor", lambda e, own=own, s=s: e.matmul(
                smps[:, :], lg_sb[:, own * 512 + s * 128: own * 512 + (s + 1) * 128],
                id_sb[0:4, 0:4], start=True, stop=True),
                waits=[(ve, lgc), (ve, prev_lt)], inc=mm)
            prev_lt = P.op("vector", lambda e, k=k: e.tensor_copy(gsm[:, 4 * k:4 * k + 4], smps[:, :]),
                           waits=[(mm, tmm)], inc=ve)
        GL, GE, T1, T2 = 0, 32, 64, 96
        M1, NM, M2, ZS, RZ, RZ4 = 0, 8, 16, 24, 32, 40
        for k in range(8):
            P.op("vector", lambda e, k=k: e.reduce_max(gsm1[:, M1 + k:M1 + k + 1],
                                                       gsm[:, 4 * k:4 * k + 4], axis=AX.X))
        nmv = None
        for k in range(8):
            nmv = P.op("vector", lambda e, k=k: e.tensor_scalar_mul(
                gsm1[:, NM + k:NM + k + 1], gsm1[:, M1 + k:M1 + k + 1], -1.0),
                inc=(ve if k == 7 else None))
        eev = None
        for k in range(8):
            eev = P.op("scalar", lambda e, k=k: e.activation(
                gsm[:, GE + 4 * k:GE + 4 * k + 4], gsm[:, 4 * k:4 * k + 4], AF.Exp,
                bias=gsm1[:, NM + k:NM + k + 1]),
                waits=[(ve, nmv)], inc=(sc if k == 7 else None))
        for k in range(8):
            P.op("vector", lambda e, k=k: e.tensor_scalar(
                gsm[:, T1 + 4 * k:T1 + 4 * k + 4], gsm[:, 4 * k:4 * k + 4],
                gsm1[:, M1 + k:M1 + k + 1], None, ALU.is_ge))
        for k in range(8):
            P.op("vector", lambda e, k=k: e.tensor_scalar_mul(
                gsm[:, T2 + 4 * k:T2 + 4 * k + 4], gsm[:, T1 + 4 * k:T1 + 4 * k + 4], -30000.0))
        for k in range(8):
            P.op("vector", lambda e, k=k: e.tensor_add(
                gsm[:, T1 + 4 * k:T1 + 4 * k + 4], gsm[:, T2 + 4 * k:T2 + 4 * k + 4],
                gsm[:, 4 * k:4 * k + 4]))
        for k in range(8):
            P.op("vector", lambda e, k=k: e.reduce_max(gsm1[:, M2 + k:M2 + k + 1],
                                                       gsm[:, T1 + 4 * k:T1 + 4 * k + 4], axis=AX.X))
        for k in range(8):
            w = [(sc, eev)] if k == 0 else []
            P.op("vector", lambda e, k=k: e.tensor_scalar(
                gsm[:, T2 + 4 * k:T2 + 4 * k + 4], gsm[:, 4 * k:4 * k + 4],
                gsm1[:, M2 + k:M2 + k + 1], None, ALU.is_ge), waits=w)
        for k in range(8):
            P.op("vector", lambda e, k=k: e.tensor_mul(
                gsm[:, T1 + 4 * k:T1 + 4 * k + 4], gsm[:, GE + 4 * k:GE + 4 * k + 4],
                gsm[:, T2 + 4 * k:T2 + 4 * k + 4]))
        for k in range(8):
            P.op("vector", lambda e, k=k: e.reduce_sum(gsm1[:, ZS + k:ZS + k + 1],
                                                       gsm[:, T1 + 4 * k:T1 + 4 * k + 4], axis=AX.X))
        for k in range(8):
            P.op("vector", lambda e, k=k: e.reciprocal(gsm1[:, RZ + k:RZ + k + 1],
                                                       gsm1[:, ZS + k:ZS + k + 1]))
        for k in range(8):
            P.op("vector", lambda e, k=k: e.tensor_scalar_mul(
                gsm1[:, RZ4 + k:RZ4 + k + 1], gsm1[:, RZ + k:RZ + k + 1], 4.0))
        gv = None
        for k in range(8):
            gv = P.op("vector", lambda e, k=k: e.tensor_scalar_mul(
                gsm[:, GE + 4 * k:GE + 4 * k + 4], gsm[:, T1 + 4 * k:T1 + 4 * k + 4],
                gsm1[:, RZ4 + k:RZ4 + k + 1]), inc=(ve if k == 7 else None))
        prev_gt = 0
        for k in range(8):
            own, s = divmod(k, 4)
            t2m = P.op("tensor", lambda e, k=k: e.matmul(gtps[:, :], gsm[:, GE + 4 * k:GE + 4 * k + 4],
                                                         id_sb[:, :], start=True, stop=True),
                       waits=[(ve, gv), (ve, prev_gt)], inc=mm)
            prev_gt = P.op("vector", lambda e, own=own, s=s: e.tensor_copy(
                gt_sb[:, own * 512 + s * 128: own * 512 + (s + 1) * 128], gtps[:, :]),
                waits=[(mm, t2m)], inc=ve)
        for own in range(2):
            gbm = P.op("tensor", lambda e, own=own: e.matmul(
                gbps[:, :], e2g_sb[:, :], gt_sb[:, own * 512:(own + 1) * 512], start=True, stop=True),
                waits=[(ve, ve.n)], inc=mm)
            gbc = P.op("vector", lambda e: e.tensor_copy(gb_sb, gbps[:, :]), waits=[(mm, gbm)], inc=ve)
            P.op("vector", lambda e, own=own: e.tensor_tensor(
                mfin[0:64, own * 512:(own + 1) * 512], maug[0:64, own * 512:(own + 1) * 512],
                gb_sb, ALU.mult), waits=[(ve, gbc)], inc=ve)
    ffn_done_ve = ve.n

    s1.close()

    ctx = E(nc.sbuf_tensor("ctx", [128, NQH * 2 * 512], BF16))
    sA = contextlib.ExitStack()
    EA = sA.enter_context
    qdiag = EA(nc.sbuf_tensor("qdiag", [128, 2 * 1024], BF16))
    ptb = EA(nc.sbuf_tensor("ptb", [128, 3 * 512], BF16))
    rcp = EA(nc.sbuf_tensor("rcp", [128, 512], F32))
    sps = [EA(nc.psum_tensor(f"sps{j}", [128, 512], F32)) for j in range(3)]
    ctxA_ps = EA(nc.psum_tensor("ctxA_ps", [128, 512], F32))
    ctxB_ps = EA(nc.psum_tensor("ctxB_ps", [128, 512], F32))
    sumA_ps = EA(nc.psum_tensor("sumA_ps", [128, 512], F32))
    sumB_ps = EA(nc.psum_tensor("sumB_ps", [128, 512], F32))

    P.barrier(["gpsimd"], [cc])
    P.barrier(["tensor", "scalar", "vector", "gpsimd", "sync"], [ve, mm, sc])

    expd = {}
    ptfree = {}
    sfree = {}
    gath = {}
    qgat = {}
    slotcons = {}
    tcons = {}
    ctxfree = {}
    g = 0

    for kvh in range(NKVH):
        bk = kvh % 2
        for p in range(NSLOT):
            w = [(mm, slotcons.get(kvh - 2, 0))]
            P.op("gpsimd", lambda e, kvh=kvh, p=p, bk=bk: e.indirect_dma_start(
                kslot[:, (bk * NSLOT + p) * 512:(bk * NSLOT + p + 1) * 512], None,
                ag_k, IndirectOffsetOnAxis(kidx_sb[:, kvh * NSLOT + p: kvh * NSLOT + p + 1], 0)),
                waits=w, inc=gd, inc_by=16)
            for kt in range(4):
                col = kvh * NSLOT * 4 + p * 4 + kt
                P.op("gpsimd", lambda e, col=col, bk=bk, p=p, kt=kt: e.indirect_dma_start(
                    vslotA[:, (bk * NSLOT + p) * 512 + kt * 128:(bk * NSLOT + p) * 512 + (kt + 1) * 128],
                    None, ag_v, IndirectOffsetOnAxis(vai_sb[:, col:col + 1], 0)),
                    inc=gd, inc_by=16)
                gath[kvh] = P.op("gpsimd", lambda e, col=col, bk=bk, p=p, kt=kt: e.indirect_dma_start(
                    vslotB[:, (bk * NSLOT + p) * 512 + kt * 128:(bk * NSLOT + p) * 512 + (kt + 1) * 128],
                    None, ag_v, IndirectOffsetOnAxis(vbi_sb[:, col:col + 1], 0)),
                    inc=gd, inc_by=16)

        for qq in range(4):
            qh = kvh * 4 + qq
            t = qh
            bt = t % 2
            qd0 = P.op("sync", lambda e, qh=qh, bt=bt: e.dma_start(
                out=qdiag[:, bt * 1024: bt * 1024 + 512],
                in_=qsp[qh * 256: qh * 256 + 256: 2, :]),
                waits=[(sd, qspw[(qh, 0)]), (mm, tcons.get(t - 1, 0))], inc=ld, inc_by=16)
            qd1 = P.op("sync", lambda e, qh=qh, bt=bt: e.dma_start(
                out=qdiag[:, bt * 1024 + 512: bt * 1024 + 1024],
                in_=qsp[qh * 256 + 1: qh * 256 + 256: 2, :]),
                waits=[(sd, qspw[(qh, 1)])], inc=ld, inc_by=16)
            for p in range(NSLOT):
                qgat[t] = P.op("gpsimd", lambda e, qh=qh, p=p, bt=bt: e.indirect_dma_start(
                    qslot[:, (bt * NSLOT + p) * 512:(bt * NSLOT + p + 1) * 512], None,
                    qsp[:, :],
                    IndirectOffsetOnAxis(qix_sb[:, qh * NSLOT + p: qh * NSLOT + p + 1], 0)),
                    waits=[(sd, qspw[(qh, 1)]), (mm, tcons.get(t - 2, 0))], inc=gd, inc_by=16)

            for own in range(2):
                own_ps_c = ctxA_ps if own == 0 else ctxB_ps
                own_ps_s = sumA_ps if own == 0 else sumB_ps
                for kt in range(4):
                    r = g % 3
                    n0 = kt * 128
                    smm = P.op("tensor", lambda e, r=r, kvh=kvh, own=own, kt=kt, n0=n0, bt=bt: e.matmul(
                        sps[r][:, n0:512],
                        k_own[:, kvh * 1024 + own * 512 + kt * 128: kvh * 1024 + own * 512 + (kt + 1) * 128],
                        qdiag[:, bt * 1024 + own * 512 + n0: bt * 1024 + (own + 1) * 512],
                        start=True, stop=True, skip_group_check=True),
                        waits=[(sc, sfree.get(g - 3, 0)), (ld, qd1 if own == 0 and kt == 0 else 0),
                               (ve, ctxfree.get(t - 1, 0) if kt == 0 else 0)],
                        inc=mm)
                    tad = P.op("vector", lambda e, r=r, n0=n0: e.tensor_tensor(
                        sps[r][:, n0:n0 + 128], sps[r][:, n0:n0 + 128], tri_sb[:, :], ALU.add),
                        waits=[(mm, smm)], inc=ve)
                    expd[g] = P.op("scalar", lambda e, r=r, n0=n0: e.activation(
                        ptb[:, r * 512 + n0: (r + 1) * 512], sps[r][:, n0:512], AF.Exp,
                        bias=biasv[:, 0:1], scale=SCALE),
                        waits=[(ve, tad), (mm, ptfree.get(g - 3, 0))], inc=sc)
                    sfree[g] = expd[g]
                    P.op("tensor", lambda e, r=r, n0=n0, ps=own_ps_s, kt=kt: e.matmul(
                        ps[:, n0:512], ones128[:, :], ptb[:, r * 512 + n0: (r + 1) * 512],
                        start=(kt == 0), stop=False, skip_group_check=True),
                        waits=[(sc, expd[g])])
                    ptfree[g] = P.op("tensor", lambda e, r=r, n0=n0, ps=own_ps_c, own=own, kt=kt, kvh=kvh: e.matmul(
                        ps[:, n0:512],
                        v_own[:, (own * 4 + kt) * 1024 + kvh * 128: (own * 4 + kt) * 1024 + (kvh + 1) * 128],
                        ptb[:, r * 512 + n0: (r + 1) * 512],
                        start=(kt == 0), stop=False, skip_group_check=True),
                        inc=mm)
                    g += 1

            for p in range(NSLOT):
                for kt in range(4):
                    r = g % 3
                    smm = P.op("tensor", lambda e, r=r, bk=bk, bt=bt, p=p, kt=kt: e.matmul(
                        sps[r][:, :],
                        kslot[:, (bk * NSLOT + p) * 512 + kt * 128:(bk * NSLOT + p) * 512 + (kt + 1) * 128],
                        qslot[:, (bt * NSLOT + p) * 512:(bt * NSLOT + p + 1) * 512],
                        start=True, stop=True, skip_group_check=True),
                        waits=[(sc, sfree.get(g - 3, 0)), (gd, gath[kvh]), (gd, qgat[t])],
                        inc=mm)
                    expd[g] = P.op("scalar", lambda e, r=r: e.activation(
                        ptb[:, r * 512:(r + 1) * 512], sps[r][:, :], AF.Exp,
                        bias=biasv[:, 0:1], scale=SCALE),
                        waits=[(mm, smm), (mm, ptfree.get(g - 3, 0))], inc=sc)
                    sfree[g] = expd[g]
                    P.op("tensor", lambda e, r=r, p=p: e.matmul(
                        sumA_ps[:, :], flagA[:, p * 128:(p + 1) * 128], ptb[:, r * 512:(r + 1) * 512],
                        start=False, stop=False, skip_group_check=True),
                        waits=[(sc, expd[g])])
                    P.op("tensor", lambda e, r=r, p=p: e.matmul(
                        sumB_ps[:, :], flagB[:, p * 128:(p + 1) * 128], ptb[:, r * 512:(r + 1) * 512],
                        start=False, stop=False, skip_group_check=True))
                    P.op("tensor", lambda e, r=r, bk=bk, p=p, kt=kt: e.matmul(
                        ctxA_ps[:, :],
                        vslotA[:, (bk * NSLOT + p) * 512 + kt * 128:(bk * NSLOT + p) * 512 + (kt + 1) * 128],
                        ptb[:, r * 512:(r + 1) * 512],
                        start=False, stop=False, skip_group_check=True))
                    ptfree[g] = P.op("tensor", lambda e, r=r, bk=bk, p=p, kt=kt: e.matmul(
                        ctxB_ps[:, :],
                        vslotB[:, (bk * NSLOT + p) * 512 + kt * 128:(bk * NSLOT + p) * 512 + (kt + 1) * 128],
                        ptb[:, r * 512:(r + 1) * 512],
                        start=False, stop=False, skip_group_check=True),
                        inc=mm)
                    g += 1
            slotcons[kvh] = ptfree[g - 1]
            tcons[t] = ptfree[g - 1]

            last_mm = ptfree[g - 1]
            fin = None
            for own in range(2):
                own_ps_c = ctxA_ps if own == 0 else ctxB_ps
                own_ps_s = sumA_ps if own == 0 else sumB_ps
                P.op("vector", lambda e, ps=own_ps_s: e.reciprocal(rcp[:, :], ps[:, :]),
                     waits=[(mm, last_mm)])
                fin = P.op("vector", lambda e, ps=own_ps_c, t=t, own=own: e.tensor_tensor(
                    ctx[:, (t * 2 + own) * 512:(t * 2 + own + 1) * 512], ps[:, :], rcp[:, :], ALU.mult),
                    inc=ve)
            ctxfree[t] = fin
    attn_ve = ve.n

    sA.close()

    sW = contextlib.ExitStack()
    EW = sW.enter_context
    wof = k_own[:, :].bitcast(F32)
    wof2 = v_own[:, :].bitcast(F32)
    wbbf = EW(nc.sbuf_tensor("wbbf", [64, 4096], BF16))
    wobf = EW(nc.sbuf_tensor("wobf", [128, 2 * 4096], BF16))
    xres = kslot[:, 0:4096].bitcast(F32)
    osb = vslotA[:, 0:4096].bitcast(F32)
    ops_ = [EW(nc.psum_tensor(f"ops{j}", [128, 1024], F32)) for j in range(2)]

    P.barrier(["tensor", "vector", "sync"], [ve, mm])

    wbl = dma("sync", wof[0:64, :].rearrange("p (o j) -> p o j", j=128),
              wb.ap().rearrange("o e j -> e o j"), ld)
    P.op("vector", lambda e: e.tensor_copy(wbbf[:, :], wof[0:64, :]), waits=[(ld, wbl)], inc=ve)
    wbc = P.op("vector", lambda e: e.tensor_copy(spcr[:, :], spcr[:, :]), inc=ve)

    ocast = {}
    oevac = {}
    ommd = {}
    ostw = {}
    def stage_o(oc):
        b = oc % 2
        buf = wof if b == 0 else wof2
        cpiece = ocast.get(oc - 2, wbc)
        for qtr in range(2):
            lq = dma("sync", buf[:, qtr * 2048:(qtr + 1) * 2048],
                     wo[oc][:, qtr * 2048:(qtr + 1) * 2048], ld,
                     waits=[(ve, cpiece)])
            cpiece = P.op("vector", lambda e, b=b, qtr=qtr, buf=buf: e.tensor_copy(
                wobf[:, b * 4096 + qtr * 2048: b * 4096 + (qtr + 1) * 2048],
                buf[:, qtr * 2048:(qtr + 1) * 2048]),
                waits=[(ld, lq), (mm, ommd.get(oc - 2, 0) if qtr == 0 else 0)], inc=ve)
        ocast[oc] = P.op("vector", lambda e: e.tensor_copy(spcr[:, :], spcr[:, :]), inc=ve)

    stage_o(0)
    stage_o(1)
    for oc in range(NQH):
        b = oc % 2
        if oc >= 2:
            pass
        xl0 = dma("sync", xres[:, b * 1024: b * 1024 + 512], xTA[oc * 128:(oc + 1) * 128, :], ld,
                  waits=[(ve, oevac.get(oc - 2, 0))])
        xl1 = dma("sync", xres[:, b * 1024 + 512: b * 1024 + 1024], xTB[oc * 128:(oc + 1) * 128, :], ld)
        omm = None
        for own in range(2):
            for ccc in range(NF):
                w = [(ve, ocast[oc]), (ve, attn_ve), (ve, wbc)]
                if ccc == 0 and own == 0:
                    w.append((ve, oevac.get(oc - 2, 0)))
                P.op("tensor", lambda e, oc=oc, own=own, ccc=ccc, b=b: e.matmul(
                    ops_[b][:, own * 512:(own + 1) * 512],
                    wobf[:, b * 4096 + ccc * 128: b * 4096 + (ccc + 1) * 128],
                    ctx[:, (ccc * 2 + own) * 512:(ccc * 2 + own + 1) * 512],
                    start=(ccc == 0), stop=False, skip_group_check=True),
                    waits=w)
            omm = P.op("tensor", lambda e, oc=oc, own=own, b=b: e.matmul(
                ops_[b][:, own * 512:(own + 1) * 512],
                wbbf[:, oc * 128:(oc + 1) * 128],
                mfin[0:64, own * 512:(own + 1) * 512],
                start=False, stop=True, skip_group_check=True),
                waits=[(ve, ffn_done_ve)], inc=mm)
        ommd[oc] = omm
        if oc + 2 < NQH:
            stage_o(oc + 2)
        ev = []
        for own in range(2):
            ev.append(P.op("vector", lambda e, own=own, b=b: e.tensor_tensor(
                osb[:, b * 1024 + own * 512: b * 1024 + (own + 1) * 512],
                ops_[b][:, own * 512:(own + 1) * 512],
                xres[:, b * 1024 + own * 512: b * 1024 + (own + 1) * 512], ALU.add),
                waits=[(mm, omm), (ld, xl1), (st, ostw.get(oc - 2, 0))], inc=ve))
        oevac[oc] = ev[-1]
        P.op("sync", lambda e, oc=oc, b=b: e.dma_start(
            out=outA[oc * 128:(oc + 1) * 128, :], in_=osb[:, b * 1024: b * 1024 + 512]),
            waits=[(ve, ev[0])], inc=st, inc_by=16)
        ostw[oc] = P.op("sync", lambda e, oc=oc, b=b: e.dma_start(
            out=outB[oc * 128:(oc + 1) * 128, :], in_=osb[:, b * 1024 + 512: b * 1024 + 1024]),
            waits=[(ve, ev[1])], inc=st, inc_by=16)

    P.op("sync", lambda e: None, waits=[(st, st.n)])

    with (nc.semaphore("s_ld") as h_ld, nc.semaphore("s_gd") as h_gd,
          nc.semaphore("s_mm") as h_mm, nc.semaphore("s_ve") as h_ve,
          nc.semaphore("s_sc") as h_sc, nc.semaphore("s_cc") as h_cc,
          nc.semaphore("s_st") as h_st, nc.semaphore("s_sd") as h_sd,
          nc.semaphore("s_io") as h_io, nc.Block() as block):
        H = {"ld": h_ld, "gd": h_gd, "mm": h_mm, "ve": h_ve, "sc": h_sc, "cc": h_cc,
             "st": h_st, "sd": h_sd, "io": h_io}

        def runner(name):
            def run(eng):
                for fn, ws, inc in P.ops[name]:
                    for s, v in ws:
                        eng.wait_ge(H[s.name], v)
                    inst = fn(eng) if fn is not None else None
                    if inc is not None:
                        if inst is not None:
                            inst.then_inc(H[inc[0].name], inc[1])
                        else:
                            eng.sem_inc(H[inc[0].name], inc[1])
            return run

        block.sync(runner("sync"))
        block.gpsimd(runner("gpsimd"))
        block.vector(runner("vector"))
        block.scalar(runner("scalar"))
        block.tensor(runner("tensor"))

    sW.close()
    stack.close()
    return nc



def _stage_shared(Wq, Wk, Wv, Wo, router_w, A, Bm):
    perm = np.empty(128, np.int64)
    perm[0:64] = np.arange(64) * 2
    perm[64:128] = np.arange(64) * 2 + 1
    wq = np.ascontiguousarray(Wq.reshape(NQH, 128, NF, 128)[:, perm].transpose(0, 3, 2, 1)
                              .reshape(NQH, 128, D))
    wk = np.ascontiguousarray(Wk.reshape(NKVH, 128, NF, 128)[:, perm].transpose(0, 3, 2, 1)
                              .reshape(NKVH, 128, D))
    wv = np.ascontiguousarray(Wv.reshape(2, 512, NF, 128).transpose(0, 2, 3, 1))
    wo = np.ascontiguousarray(Wo.reshape(NQH, 128, NF, 128).transpose(0, 3, 2, 1)
                              .reshape(NQH, 128, D))
    a_aug = np.concatenate([A.reshape(64, D), router_w], axis=0)
    wa = np.ascontiguousarray(a_aug.reshape(68, NF, 128).transpose(1, 2, 0))
    wb = np.ascontiguousarray(Bm.reshape(4, 32, 128, 16).transpose(1, 0, 3, 2).reshape(32, 64, 128))
    tri = np.where(np.arange(128)[:, None] <= np.arange(128)[None, :], 0.0, -1e9).astype(np.float32)
    ident = np.eye(128, dtype=np.float32)
    e2g = np.zeros((4, 64), np.float32)
    for ee in range(4):
        e2g[ee, ee * 16:(ee + 1) * 16] = 1.0
    return wq, wk, wv, wo, wa, wb, tri, ident, e2g


def _stage_core(c):
    cA, cB = c, 7 - c
    pos = np.stack([512 * cA + np.arange(CH, dtype=np.float32),
                    512 * cB + np.arange(CH, dtype=np.float32)]).astype(np.float32)
    slots = []
    for p in range(NSLOT):
        if p < cA:
            slots.append((p, 0, 0))
        else:
            j = p - cA
            slots.append((7 - j, 1, 1))
    dims = np.arange(128, dtype=np.int64)
    kidx = np.zeros((128, NKVH * NSLOT), np.int32)
    vai = np.zeros((128, NKVH * NSLOT * 4), np.int32)
    vbi = np.zeros((128, NKVH * NSLOT * 4), np.int32)
    qix = np.zeros((128, NQH * NSLOT), np.int32)
    fix = np.zeros((128, 2 * NSLOT), np.int32)
    ZROW = NCORE * 4096 * 4
    for p, (rank, seg, own) in enumerate(slots):
        for qh in range(NQH):
            qix[:, qh * NSLOT + p] = qh * 256 + dims * 2 + own
        fix[:, 0 * NSLOT + p] = 0 if own == 0 else 32
        fix[:, 1 * NSLOT + p] = 0 if own == 1 else 32
        for kvh in range(NKVH):
            kidx[:, kvh * NSLOT + p] = rank * 4096 + kvh * 256 + dims * 2 + seg
            for kt in range(4):
                col = kvh * NSLOT * 4 + p * 4 + kt
                vrow = rank * 16384 + 8192 + kvh * 1024 + seg * 512 + kt * 128 + dims
                if own == 0:
                    vai[:, col] = vrow
                    vbi[:, col] = ZROW + dims
                else:
                    vai[:, col] = ZROW + dims
                    vbi[:, col] = vrow
    return pos, kidx, vai, vbi, qix, fix


def kernel(L0_attn_hidden_states, L0_attn_Wq, L0_attn_Wk, L0_attn_Wv, L0_attn_Wo,
           L0_ffn_router_w, L0_ffn_A, L0_ffn_B):
    x = np.asarray(L0_attn_hidden_states, dtype=np.float32)
    wq, wk, wv, wo, wa, wb, tri, ident, e2g = _stage_shared(
        np.asarray(L0_attn_Wq, np.float32), np.asarray(L0_attn_Wk, np.float32),
        np.asarray(L0_attn_Wv, np.float32), np.asarray(L0_attn_Wo, np.float32),
        np.asarray(L0_ffn_router_w, np.float32), np.asarray(L0_ffn_A, np.float32),
        np.asarray(L0_ffn_B, np.float32))

    in_maps = []
    for c in range(NCORE):
        cA, cB = c, 7 - c
        pos, kidx, vai, vbi, qix, fix = _stage_core(c)
        in_maps.append({
            "xTA": np.ascontiguousarray(x[0, 512 * cA:512 * (cA + 1), :].T),
            "xTB": np.ascontiguousarray(x[1, 512 * cB:512 * (cB + 1), :].T),
            "wq": wq, "wk": wk, "wv": wv, "wo": wo, "wa": wa, "wb": wb,
            "tri": tri, "ident": ident, "e2g": e2g, "pos": pos,
            "kidx": kidx, "vai": vai, "vbi": vbi, "qix": qix, "fix": fix,
        })

    nc = build_program()
    res = run_bass_kernel_spmd(nc, in_maps, core_ids=list(range(NCORE)))

    out = np.empty((2, 4096, 4096), dtype=np.float32)
    for c in range(NCORE):
        cA, cB = c, 7 - c
        out[0, 512 * cA:512 * (cA + 1), :] = np.asarray(res.results[c]["outA"]).T
        out[1, 512 * cB:512 * (cB + 1), :] = np.asarray(res.results[c]["outB"]).T
    return out


# version 10
# speedup vs baseline: 1.2702x; 1.0034x over previous
import contextlib
import numpy as np
import ml_dtypes
import concourse.bass as bass
import concourse.mybir as mybir
from concourse.bass import IndirectOffsetOnAxis
from concourse.bass_utils import run_bass_kernel_spmd

F32 = mybir.dt.float32
BF16 = mybir.dt.bfloat16
I32 = mybir.dt.int32
AF = mybir.ActivationFunctionType
ALU = mybir.AluOpType
AX = mybir.AxisListType

NCORE = 8
D = 4096
NF = 32
CH = 512
NQH, NKVH, HD = 32, 8, 128
NSLOT = 7
SCALE = 1.0 / float(np.sqrt(128.0))
EXPB = -35.0
TWO_PI = 6.283185307179586
MAGIC = 12582912.0
LN10K = float(np.log(10000.0))


class SemT:
    def __init__(self, name):
        self.name = name
        self.n = 0

class Plan:
    ENGS = ("sync", "gpsimd", "vector", "scalar", "tensor")

    def __init__(self):
        self.ops = {e: [] for e in self.ENGS}
        self.lastwait = {}

    def op(self, eng, fn, waits=(), inc=None, inc_by=1):
        ws = []
        for s, v in waits:
            if v <= 0:
                continue
            key = (eng, s.name)
            if self.lastwait.get(key, -1) >= v:
                continue
            self.lastwait[key] = v
            ws.append((s, v))
        tgt = None
        if inc is not None:
            inc.n += inc_by
            tgt = inc.n
        self.ops[eng].append((fn, ws, (inc, inc_by) if inc is not None else None))
        return tgt

    def barrier(self, engs, sems, exclude=()):
        for e in engs:
            for s in sems:
                if s.name in exclude:
                    continue
                v = s.n
                if v <= 0:
                    continue
                key = (e, s.name)
                if self.lastwait.get(key, -1) >= v:
                    continue
                self.lastwait[key] = v
                self.ops[e].append((None, [(s, v)], None))



def build_program():
    nc = bass.Bass("TRN2", target_bir_lowering=False, detect_race_conditions=False)

    xTA = nc.dram_tensor("xTA", [D, CH], F32, kind="ExternalInput")
    xTB = nc.dram_tensor("xTB", [D, CH], F32, kind="ExternalInput")
    wq = nc.dram_tensor("wq", [NQH, 128, D], F32, kind="ExternalInput")
    wk = nc.dram_tensor("wk", [NKVH, 128, D], F32, kind="ExternalInput")
    wv = nc.dram_tensor("wv", [2, NF, 128, 512], F32, kind="ExternalInput")
    wo = nc.dram_tensor("wo", [NQH, 128, D], F32, kind="ExternalInput")
    wa = nc.dram_tensor("wa", [NF, 128, 68], F32, kind="ExternalInput")
    wb = nc.dram_tensor("wb", [32, 64, 128], F32, kind="ExternalInput")
    tri_d = nc.dram_tensor("tri", [128, 128], F32, kind="ExternalInput")
    id_d = nc.dram_tensor("ident", [128, 128], F32, kind="ExternalInput")
    e2g_d = nc.dram_tensor("e2g", [4, 64], F32, kind="ExternalInput")
    pos_d = nc.dram_tensor("pos", [2, CH], F32, kind="ExternalInput")
    kidx_d = nc.dram_tensor("kidx", [128, NKVH * NSLOT], I32, kind="ExternalInput")
    vai_d = nc.dram_tensor("vai", [128, NKVH * NSLOT * 4], I32, kind="ExternalInput")
    vbi_d = nc.dram_tensor("vbi", [128, NKVH * NSLOT * 4], I32, kind="ExternalInput")
    qix_d = nc.dram_tensor("qix", [128, NQH * NSLOT], I32, kind="ExternalInput")
    fix_d = nc.dram_tensor("fix", [128, 2 * NSLOT], I32, kind="ExternalInput")

    outA = nc.dram_tensor("outA", [D, CH], F32, kind="ExternalOutput")
    outB = nc.dram_tensor("outB", [D, CH], F32, kind="ExternalOutput")

    ag_in = nc.dram_tensor("ag_in", [4096, 512], BF16)
    ag_out = nc.dram_tensor("ag_out", [NCORE * 4096 + 32, 512], BF16)
    qsp = nc.dram_tensor("qsp", [NQH * 256, 512], BF16)
    c2 = nc.dram_tensor("c2", [33, 128], F32)

    ag_k = ag_out.ap()
    ag_v = ag_out.ap().rearrange("r (a b) -> (r a) b", a=4, b=128)

    P = Plan()
    ld = SemT("ld")
    gd = SemT("gd")
    mm = SemT("mm")
    ve = SemT("ve")
    sc = SemT("sc")
    cc = SemT("cc")
    st = SemT("st")
    sd = SemT("sd")
    io = SemT("io")
    SEMS = [ld, gd, mm, ve, sc, cc, st, sd, io]

    stack = contextlib.ExitStack()
    E = stack.enter_context

    kslot = E(nc.sbuf_tensor("kslot", [128, 2 * NSLOT * 512], BF16))
    vslotA = E(nc.sbuf_tensor("vslotA", [128, 2 * NSLOT * 512], BF16))
    vslotB = E(nc.sbuf_tensor("vslotB", [128, 2 * NSLOT * 512], BF16))
    qslot = E(nc.sbuf_tensor("qslot", [128, 2 * NSLOT * 512], BF16))
    flg_f = E(nc.sbuf_tensor("flg_f", [128, 128], F32))
    spcr = E(nc.sbuf_tensor("spcr", [128, 8], F32))
    cosq = E(nc.sbuf_tensor("cosq", [128, 1024], F32))
    sinq = E(nc.sbuf_tensor("sinq", [128, 1024], F32))
    k_own = E(nc.sbuf_tensor("k_own", [128, NKVH * 1024], BF16))
    v_own = E(nc.sbuf_tensor("v_own", [128, 8 * 1024], BF16))
    mfin = E(nc.sbuf_tensor("mfin", [128, 1024], BF16))
    flagA = E(nc.sbuf_tensor("flagA", [128, NSLOT * 128], BF16))
    flagB = E(nc.sbuf_tensor("flagB", [128, NSLOT * 128], BF16))
    tri_sb = E(nc.sbuf_tensor("tri_sb", [128, 128], F32))
    id_sb = E(nc.sbuf_tensor("id_sb", [128, 128], F32))
    e2g_sb = E(nc.sbuf_tensor("e2g_sb", [4, 64], F32))
    kidx_sb = E(nc.sbuf_tensor("kidx_sb", [128, NKVH * NSLOT], I32))
    vai_sb = E(nc.sbuf_tensor("vai_sb", [128, NKVH * NSLOT * 4], I32))
    vbi_sb = E(nc.sbuf_tensor("vbi_sb", [128, NKVH * NSLOT * 4], I32))
    qix_sb = E(nc.sbuf_tensor("qix_sb", [128, NQH * NSLOT], I32))
    fix_sb = E(nc.sbuf_tensor("fix_sb", [128, 2 * NSLOT], I32))
    ones128 = E(nc.sbuf_tensor("ones128", [128, 128], BF16))
    onesinv = E(nc.sbuf_tensor("onesinv", [128, 128], BF16))
    biasv = E(nc.sbuf_tensor("biasv", [128, 1], F32))
    epsv = E(nc.sbuf_tensor("epsv", [128, 1], F32))
    zrow = E(nc.sbuf_tensor("zrow", [32, 512], BF16))
    c2_sb = E(nc.sbuf_tensor("c2_sb", [33, 128], F32))

    def dma(eng, out_ap, in_ap, sem, waits=()):
        return P.op(eng, lambda e, o=out_ap, i=in_ap: e.dma_start(out=o, in_=i),
                    waits=waits, inc=sem, inc_by=16)

    t0 = []
    t0.append(dma("sync", tri_sb[:, :], tri_d[:, :], ld))
    t0.append(dma("sync", id_sb[:, :], id_d[:, :], ld))
    t0.append(dma("sync", e2g_sb[:, :], e2g_d[:, :], ld))
    t0.append(dma("sync", kidx_sb[:, :], kidx_d[:, :], ld))
    t0.append(dma("sync", vai_sb[:, :], vai_d[:, :], ld))
    t0.append(dma("sync", vbi_sb[:, :], vbi_d[:, :], ld))
    t0.append(dma("sync", qix_sb[:, :], qix_d[:, :], ld))
    t0.append(dma("sync", fix_sb[:, :], fix_d[:, :], ld))
    tbl_ld = ld.n

    mset = []
    P.op("vector", lambda e: e.memset(ones128[:, :], 1.0))
    P.op("vector", lambda e: e.memset(onesinv[:, :], 1.0 / 4096.0))
    P.op("vector", lambda e: e.memset(biasv[:, :], EXPB))
    P.op("vector", lambda e: e.memset(epsv[:, :], 1e-5))
    P.op("vector", lambda e: e.memset(zrow[:, :], 0.0))
    P.op("vector", lambda e: e.memset(c2_sb[:, :], 0.0))
    msets_done = P.op("vector", lambda e: e.memset(c2_sb[0:1, :], 1.0), inc=ve)

    c2w = dma("scalar", c2[:, :], c2_sb[:, :], sd, waits=[(ve, msets_done)])
    zpw = dma("scalar", ag_out[NCORE * 4096: NCORE * 4096 + 32, :], zrow[:, :], sd,
              waits=[(ve, msets_done)])

    flg_done = []
    for t in range(2):
        for p in range(NSLOT):
            g = P.op("gpsimd",
                     lambda e, t=t, p=p: e.indirect_dma_start(
                         flg_f[:, :], None, c2[:, :],
                         IndirectOffsetOnAxis(fix_sb[:, t * NSLOT + p: t * NSLOT + p + 1], 0)),
                     waits=[(ld, tbl_ld), (sd, c2w), (ve, flg_done[-1] if flg_done else 0)],
                     inc=gd, inc_by=16)
            dstf = flagA if t == 0 else flagB
            flg_done.append(P.op("vector",
                                 lambda e, d=dstf, p=p: e.tensor_copy(d[:, p * 128:(p + 1) * 128], flg_f[:, :]),
                                 waits=[(gd, g)], inc=ve))
    flags_ve = ve.n

    with contextlib.ExitStack() as ph:
        EE = ph.enter_context
        invf = EE(nc.sbuf_tensor("invf", [64, 1], F32))
        posr = EE(nc.sbuf_tensor("posr", [64, 512], F32))
        th = EE(nc.sbuf_tensor("th", [64, 512], F32))
        u1 = EE(nc.sbuf_tensor("u1", [64, 512], F32))
        n1 = EE(nc.sbuf_tensor("n1", [64, 512], F32))
        r1 = EE(nc.sbuf_tensor("r1", [64, 512], F32))
        P.op("gpsimd", lambda e: e.iota(invf[:, :], pattern=[[0, 1]], channel_multiplier=1,
                                        allow_small_or_imprecise_dtypes=True), inc=io, inc_by=1)
        iv = P.op("scalar", lambda e: e.activation(invf[:, :], invf[:, :], AF.Exp, scale=-LN10K / 64.0),
                  waits=[(io, 1)], inc=sc)
        prev_tv = 0
        prev_tr = 0
        for own in range(2):
            pr = P.op("scalar",
                      lambda e, own=own: e.dma_start(out=posr[:, :],
                                                     in_=pos_d[own:own + 1, :].broadcast_to([64, CH])),
                      waits=[(ld, tbl_ld), (ve, prev_tv)], inc=sd, inc_by=16)
            tv = P.op("vector", lambda e: e.tensor_scalar_mul(th[:, :], posr[:, :], invf[0:64, 0:1]),
                      waits=[(sd, pr), (sc, iv)], inc=ve)
            prev_tv = tv
            for trig, off in (("sin", 0.0), ("cos", 0.25)):
                P.op("vector", lambda e, off=off: e.tensor_scalar(u1[:, :], th[:, :], 1.0 / TWO_PI, off, ALU.mult, ALU.add),
                     waits=[(ve, tv), (sc, prev_tr)])
                P.op("vector", lambda e: e.tensor_scalar_add(n1[:, :], u1[:, :], MAGIC))
                P.op("vector", lambda e: e.tensor_scalar_add(n1[:, :], n1[:, :], -MAGIC))
                P.op("vector", lambda e: e.tensor_sub(r1[:, :], u1[:, :], n1[:, :]))
                rv = P.op("vector", lambda e: e.tensor_scalar_mul(r1[:, :], r1[:, :], TWO_PI), inc=ve)
                if trig == "sin":
                    sn = P.op("scalar", lambda e, own=own: e.activation(sinq[64:128, own * 512:(own + 1) * 512],
                                                                        r1[:, :], AF.Sin),
                              waits=[(ve, rv)], inc=sc)
                    prev_tr = sn
                    P.op("vector", lambda e, own=own: e.tensor_scalar_mul(
                        sinq[0:64, own * 512:(own + 1) * 512], sinq[64:128, own * 512:(own + 1) * 512], -1.0),
                        waits=[(sc, sn)], inc=ve)
                else:
                    cs = P.op("scalar", lambda e, own=own: e.activation(cosq[0:64, own * 512:(own + 1) * 512],
                                                                        r1[:, :], AF.Sin),
                              waits=[(ve, rv)], inc=sc)
                    prev_tr = cs
                    P.op("scalar", lambda e, own=own: e.dma_start(
                        out=cosq[64:128, own * 512:(own + 1) * 512],
                        in_=cosq[0:64, own * 512:(own + 1) * 512]),
                        waits=[(sc, cs)], inc=sd, inc_by=16)
    rope_tabs_ve = ve.n
    rope_tabs_sd = sd.n

    s1 = contextlib.ExitStack()
    E1 = s1.enter_context
    xn = E1(nc.sbuf_tensor("xn", [128, 2 * NF * 512], BF16))
    xst = E1(nc.sbuf_tensor("xst", [128, 2 * 512], F32))
    xbst = E1(nc.sbuf_tensor("xbst", [128, 2 * 512], BF16))
    xsq = E1(nc.sbuf_tensor("xsq", [128, 2 * 512], BF16))
    wf32 = E1(nc.sbuf_tensor("wf32", [128, 2 * 1024], F32))
    wbf = E1(nc.sbuf_tensor("wbf", [128, 2 * 4096], BF16))
    mean_sb = vslotA[:, 3584:5632].bitcast(F32)
    rstd_sb = vslotB[:, 3584:5632].bitcast(F32)
    qf = kslot[:, 3584:5632].bitcast(F32)
    qs = qslot[:, 3584:5632].bitcast(F32)
    wabf = qslot[:, 0:NF * 68]
    lg_sb = vslotB[:, 0:2048].bitcast(F32)[0:4]
    gt_sb = vslotA[:, 0:2048].bitcast(F32)[0:4]
    gb_sb = kslot[:, 0:1024].bitcast(F32)[0:64]
    gsm = E1(nc.sbuf_tensor("gsm", [128, 128], F32))
    gsm1 = E1(nc.sbuf_tensor("gsm1", [128, 48], F32))

    wal = dma("sync", wf32[:, 0:2040].rearrange("p (f e) -> p f e", e=68),
              wa.ap()[0:30].rearrange("f i e -> i f e"), ld)
    wc1 = P.op("vector", lambda e: e.tensor_copy(wabf[:, 0:2040], wf32[:, 0:2040]),
               waits=[(ld, wal)], inc=ve)
    wal2 = dma("sync", wf32[:, 0:136].rearrange("p (f e) -> p f e", e=68),
               wa.ap()[30:32].rearrange("f i e -> i f e"), ld, waits=[(ve, wc1)])
    P.op("vector", lambda e: e.tensor_copy(wabf[:, 2040:2176], wf32[:, 0:136]),
         waits=[(ld, wal2)], inc=ve)
    wacast = P.op("vector", lambda e: e.tensor_copy(spcr[:, :], spcr[:, :]), inc=ve)

    xn_done = {}

    with contextlib.ExitStack() as ph:
        EE = ph.enter_context
        mean_ps = EE(nc.psum_tensor("mean_ps", [128, 1024], F32))
        sq_ps = EE(nc.psum_tensor("sq_ps", [128, 1024], F32))
        xcons = {}
        xsrc = (xTA, xTB)
        for own in range(2):
            for f in range(NF):
                i = own * NF + f
                b = i % 2
                l_ = dma("sync", xst[:, b * 512:(b + 1) * 512], xsrc[own][f * 128:(f + 1) * 128, :], ld,
                         waits=[(mm, xcons.get(i - 2, 0)), (ve, wacast if i == 0 else 0)])
                cv_ = P.op("vector", lambda e, b=b: e.tensor_copy(xbst[:, b * 512:(b + 1) * 512],
                                                                  xst[:, b * 512:(b + 1) * 512]),
                           waits=[(ld, l_)], inc=ve)
                sq_ = P.op("scalar", lambda e, b=b: e.activation(xsq[:, b * 512:(b + 1) * 512],
                                                                 xst[:, b * 512:(b + 1) * 512], AF.Square),
                           waits=[(ld, l_)], inc=sc)
                P.op("tensor", lambda e, own=own, f=f, b=b: e.matmul(
                    mean_ps[:, own * 512:(own + 1) * 512], onesinv[:, :], xbst[:, b * 512:(b + 1) * 512],
                    start=(f == 0), stop=(f == NF - 1)),
                    waits=[(ve, cv_)])
                xcons[i] = P.op("tensor", lambda e, own=own, f=f, b=b: e.matmul(
                    sq_ps[:, own * 512:(own + 1) * 512], onesinv[:, :], xsq[:, b * 512:(b + 1) * 512],
                    start=(f == 0), stop=(f == NF - 1)),
                    waits=[(sc, sq_)], inc=mm)
        stats_mm = mm.n
        P.op("vector", lambda e: e.tensor_copy(mean_sb, mean_ps[:, :]), waits=[(mm, stats_mm)])
        P.op("vector", lambda e: e.tensor_mul(qf, mean_sb, mean_sb))
        vfin = P.op("vector", lambda e: e.tensor_tensor(qf, sq_ps[:, :], qf, ALU.subtract), inc=ve)
        sfin = P.op("scalar", lambda e: e.activation(qf, qf, AF.Sqrt, bias=epsv[:, 0:1]),
                    waits=[(ve, vfin)], inc=sc)
        rfin = P.op("vector", lambda e: e.reciprocal(rstd_sb, qf), waits=[(sc, sfin)], inc=ve)

        for own in range(2):
            for f in range(NF):
                i = own * NF + f
                b = i % 2
                l_ = dma("sync", xst[:, b * 512:(b + 1) * 512], xsrc[own][f * 128:(f + 1) * 128, :], ld,
                         waits=[(ve, xn_done.get(i - 2, rfin))])
                P.op("vector", lambda e, b=b, own=own: e.tensor_tensor(
                    qf[:, 0:512], xst[:, b * 512:(b + 1) * 512], mean_sb[:, own * 512:(own + 1) * 512],
                    ALU.subtract), waits=[(ld, l_), (ve, rfin)])
                xn_done[i] = P.op("vector", lambda e, i=i, own=own: e.tensor_tensor(
                    xn[:, i * 512:(i + 1) * 512], qf[:, 0:512], rstd_sb[:, own * 512:(own + 1) * 512],
                    ALU.mult), inc=ve)
        xn_all = ve.n

    xn_done_outer = xn_done

    rope_state = {"last_swap": 0}

    def rope_emit(ps_ap, out_bf_ap, wait_mm, final_wait_sd=0):
        c1 = P.op("vector", lambda e: e.tensor_copy(qf, ps_ap),
                  waits=[(mm, wait_mm), (sd, rope_state["last_swap"])], inc=ve)
        s1_ = P.op("scalar", lambda e: e.dma_start(out=qs[0:64], in_=qf[64:128]),
                   waits=[(ve, c1)], inc=sd, inc_by=16)
        s2_ = P.op("scalar", lambda e: e.dma_start(out=qs[64:128], in_=qf[0:64, :]),
                   inc=sd, inc_by=16)
        rope_state["last_swap"] = s2_
        P.op("vector", lambda e: e.tensor_mul(qf, qf, cosq[:, :]),
             waits=[(ve, rope_tabs_ve), (sd, rope_tabs_sd), (sd, s2_)])
        P.op("vector", lambda e: e.tensor_mul(qs, qs, sinq[:, :]))
        done = P.op("vector", lambda e: e.tensor_add(out_bf_ap, qf, qs),
                    waits=[(sd, final_wait_sd)], inc=ve)
        return c1, done

    with contextlib.ExitStack() as ph:
        EE = ph.enter_context
        kps = [EE(nc.psum_tensor(f"kps{j}", [128, 1024], F32)) for j in range(2)]
        kfree = {}
        kagw = []
        cast_done = {}
        mm_done = {}
        def stage_k(kvh):
            b = kvh % 2
            prevc = cast_done.get(kvh - 2, wacast if kvh < 2 else 0)
            cpiece = prevc
            for qtr in range(4):
                lq = dma("sync", wf32[:, b * 1024:(b + 1) * 1024],
                         wk[kvh][:, qtr * 1024:(qtr + 1) * 1024], ld,
                         waits=[(ve, cpiece)])
                cpiece = P.op("vector", lambda e, b=b, kvh=kvh, qtr=qtr: e.tensor_copy(
                    wbf[:, b * 4096 + qtr * 1024: b * 4096 + (qtr + 1) * 1024],
                    wf32[:, b * 1024:(b + 1) * 1024]),
                    waits=[(ld, lq), (mm, mm_done.get(kvh - 2, 0) if qtr == 0 else 0)], inc=ve)
            cast_done[kvh] = P.op("vector", lambda e: e.tensor_copy(spcr[:, :], spcr[:, :]), inc=ve)

        stage_k(0)
        stage_k(1)
        for kvh in range(NKVH):
            b = kvh % 2
            for own in range(2):
                for f in range(NF):
                    w = [(ve, cast_done[kvh]), (ve, xn_all)]
                    if f == 0 and own == 0:
                        w.append((ve, kfree.get(kvh - 2, 0)))
                    last = (own == 1 and f == NF - 1)
                    mm_done[kvh] = P.op("tensor", lambda e, kvh=kvh, own=own, f=f, b=b: e.matmul(
                        kps[b][:, own * 512:(own + 1) * 512],
                        wbf[:, b * 4096 + f * 128: b * 4096 + (f + 1) * 128],
                        xn[:, (own * NF + f) * 512:(own * NF + f + 1) * 512],
                        start=(f == 0), stop=(f == NF - 1)),
                        waits=w, inc=(mm if last else None))
            if kvh + 2 < NKVH:
                stage_k(kvh + 2)
            kf, kdone = rope_emit(kps[b][:, :], k_own[:, kvh * 1024:(kvh + 1) * 1024], mm_done[kvh])
            kfree[kvh] = kf
            for own in range(2):
                kagw.append(P.op("scalar", lambda e, kvh=kvh, own=own: e.dma_start(
                    out=ag_in[kvh * 256 + own: kvh * 256 + 256: 2, :],
                    in_=k_own[:, kvh * 1024 + own * 512: kvh * 1024 + (own + 1) * 512]),
                    waits=[(ve, kdone)], inc=sd, inc_by=16))
    k_phase_ve = ve.n
    kcast_last = cast_done[NKVH - 1]

    P.barrier(["tensor"], [ve])
    with contextlib.ExitStack() as ph:
        EE = ph.enter_context
        vps = [EE(nc.psum_tensor(f"vps{j}", [128, 512], F32)) for j in range(8)]
        vagw = []
        vcast = {}
        vfcons = {}
        for h in range(2):
            vmm = None
            for f in range(NF):
                b = f % 2
                l_ = dma("sync", wf32[:, b * 1024: b * 1024 + 512],
                         wv[h][f][:, :], ld,
                         waits=[(ve, vcast.get((h, f - 2),
                                               vcast.get((0, NF - 2 + f), 0) if h == 1 else 0)),
                                (ve, kcast_last)])
                P.op("vector", lambda e, b=b: e.tensor_copy(
                    wbf[:, b * 512:(b + 1) * 512], wf32[:, b * 1024: b * 1024 + 512]),
                    waits=[(ld, l_), (mm, vfcons.get((h, f - 2),
                                                    vfcons.get((0, NF - 1), 0) if h == 1 else 0))], inc=ve)
                vcast[(h, f)] = P.op("vector", lambda e: e.tensor_copy(spcr[:, :], spcr[:, :]), inc=ve)
                for own in range(2):
                    for kt in range(4):
                        j = own * 4 + kt
                        w = [(ve, vcast[(h, f)])]
                        if f == 0:
                            w.append((ve, k_phase_ve if h == 0 else 0))
                            w.append((ve, vevac if h == 1 else 0))
                        last = (own == 1 and kt == 3)
                        vmm = P.op("tensor", lambda e, own=own, kt=kt, f=f, b=b, j=j: e.matmul(
                            vps[j][:, :],
                            xn[:, (own * NF + f) * 512 + kt * 128: (own * NF + f) * 512 + (kt + 1) * 128],
                            wbf[:, b * 512:(b + 1) * 512],
                            start=(f == 0), stop=(f == NF - 1)),
                            waits=w, inc=(mm if last else None))
                vfcons[(h, f)] = vmm
            ev = []
            for own in range(2):
                for kt in range(4):
                    j = own * 4 + kt
                    ev.append(P.op("vector", lambda e, own=own, kt=kt, j=j, h=h: e.tensor_copy(
                        v_own[:, (own * 4 + kt) * 1024 + h * 512: (own * 4 + kt) * 1024 + (h + 1) * 512],
                        vps[j][:, :]), waits=[(mm, vmm)], inc=ve))
            vevac = ve.n
            for own in range(2):
                for kt in range(4):
                    for kk in range(4):
                        kvh = h * 4 + kk
                        base = 8192 + kvh * 1024 + own * 512 + kt * 128
                        vagw.append(P.op("scalar", lambda e, own=own, kt=kt, kk=kk, base=base, h=h: e.dma_start(
                            out=ag_in.ap().rearrange("r (a b) -> (r a) b", a=4, b=128)[base: base + 128, :],
                            in_=v_own[:, (own * 4 + kt) * 1024 + h * 512 + kk * 128:
                                      (own * 4 + kt) * 1024 + h * 512 + (kk + 1) * 128]),
                            waits=[(ve, vevac)], inc=sd, inc_by=16))

    agq = P.op("gpsimd", lambda e: e.collective_compute(
        "AllGather", ALU.bypass, replica_groups=[list(range(NCORE))],
        ins=[ag_in[:, :]], outs=[ag_out[0:NCORE * 4096, :]]),
        waits=[(sd, sd.n), (gd, gd.n)], inc=cc)

    vcast_last = vcast[(1, NF - 1)]
    v_mm_last = vfcons[(1, NF - 1)]
    P.barrier(["tensor"], [ve])
    with contextlib.ExitStack() as ph:
        EE = ph.enter_context
        qps = [EE(nc.psum_tensor(f"qps{j}", [128, 1024], F32)) for j in range(2)]
        qfree = {}
        qspw = {}
        qcast = {}
        qmm = {}
        def stage_q(qh):
            b = qh % 2
            cpiece = qcast.get(qh - 2, vcast_last)
            for qtr in range(4):
                lq = dma("sync", wf32[:, b * 1024:(b + 1) * 1024],
                         wq[qh][:, qtr * 1024:(qtr + 1) * 1024], ld,
                         waits=[(ve, cpiece)])
                cpiece = P.op("vector", lambda e, b=b, qtr=qtr: e.tensor_copy(
                    wbf[:, b * 4096 + qtr * 1024: b * 4096 + (qtr + 1) * 1024],
                    wf32[:, b * 1024:(b + 1) * 1024]),
                    waits=[(ld, lq), (mm, qmm.get(qh - 2, v_mm_last) if qtr == 0 else 0)], inc=ve)
            qcast[qh] = P.op("vector", lambda e: e.tensor_copy(spcr[:, :], spcr[:, :]), inc=ve)

        stage_q(0)
        stage_q(1)
        for qh in range(NQH):
            b = qh % 2
            for own in range(2):
                for f in range(NF):
                    w = [(ve, qcast[qh])]
                    if f == 0 and own == 0:
                        w.append((ve, qfree.get(qh - 2, 0)))
                    last = (own == 1 and f == NF - 1)
                    qmm[qh] = P.op("tensor", lambda e, own=own, f=f, b=b: e.matmul(
                        qps[b][:, own * 512:(own + 1) * 512],
                        wbf[:, b * 4096 + f * 128: b * 4096 + (f + 1) * 128],
                        xn[:, (own * NF + f) * 512:(own * NF + f + 1) * 512],
                        start=(f == 0), stop=(f == NF - 1)),
                        waits=w, inc=(mm if last else None))
            if qh + 2 < NQH:
                stage_q(qh + 2)
            qfh, qdone = rope_emit(qps[b][:, :], xsq[:, :], qmm[qh],
                                   final_wait_sd=qspw.get((qh - 1, 1), 0))
            qfree[qh] = qfh
            for own in range(2):
                qspw[(qh, own)] = P.op("scalar", lambda e, qh=qh, own=own: e.dma_start(
                    out=qsp[qh * 256 + own: qh * 256 + 256: 2, :],
                    in_=xsq[:, own * 512:(own + 1) * 512]),
                    waits=[(ve, qdone)], inc=sd, inc_by=16)
    q_phase_done_sd = sd.n
    q_phase_done_ve = ve.n

    P.barrier(["tensor"], [ve])
    with contextlib.ExitStack() as ph:
        EE = ph.enter_context
        maug = EE(nc.psum_tensor("maug", [68, 1024], F32))
        smps = EE(nc.psum_tensor("smps", [128, 4], F32))
        gtps = EE(nc.psum_tensor("gtps", [4, 128], F32))
        gbps = EE(nc.psum_tensor("gbps", [64, 512], F32))
        amm = None
        for own in range(2):
            for f in range(NF):
                last = (own == 1 and f == NF - 1)
                amm = P.op("tensor", lambda e, own=own, f=f: e.matmul(
                    maug[:, own * 512:(own + 1) * 512],
                    wabf[:, f * 68:(f + 1) * 68],
                    xn[:, (own * NF + f) * 512:(own * NF + f + 1) * 512],
                    start=(f == 0), stop=(f == NF - 1)),
                    waits=[(ve, wacast), (ve, xn_all)], inc=(mm if last else None))
        lgc = P.op("vector", lambda e: e.tensor_copy(lg_sb, maug[64:68, :]), waits=[(mm, amm)], inc=ve)
        tdone = {}
        prev_lt = 0
        for k in range(8):
            own, s = divmod(k, 4)
            tmm = P.op("tensor", lambda e, own=own, s=s: e.matmul(
                smps[:, :], lg_sb[:, own * 512 + s * 128: own * 512 + (s + 1) * 128],
                id_sb[0:4, 0:4], start=True, stop=True),
                waits=[(ve, lgc), (ve, prev_lt)], inc=mm)
            prev_lt = P.op("vector", lambda e, k=k: e.tensor_copy(gsm[:, 4 * k:4 * k + 4], smps[:, :]),
                           waits=[(mm, tmm)], inc=ve)
        GL, GE, T1, T2 = 0, 32, 64, 96
        M1, NM, M2, ZS, RZ, RZ4 = 0, 8, 16, 24, 32, 40
        for k in range(8):
            P.op("vector", lambda e, k=k: e.reduce_max(gsm1[:, M1 + k:M1 + k + 1],
                                                       gsm[:, 4 * k:4 * k + 4], axis=AX.X))
        nmv = None
        for k in range(8):
            nmv = P.op("vector", lambda e, k=k: e.tensor_scalar_mul(
                gsm1[:, NM + k:NM + k + 1], gsm1[:, M1 + k:M1 + k + 1], -1.0),
                inc=(ve if k == 7 else None))
        eev = None
        for k in range(8):
            eev = P.op("scalar", lambda e, k=k: e.activation(
                gsm[:, GE + 4 * k:GE + 4 * k + 4], gsm[:, 4 * k:4 * k + 4], AF.Exp,
                bias=gsm1[:, NM + k:NM + k + 1]),
                waits=[(ve, nmv)], inc=(sc if k == 7 else None))
        for k in range(8):
            P.op("vector", lambda e, k=k: e.tensor_scalar(
                gsm[:, T1 + 4 * k:T1 + 4 * k + 4], gsm[:, 4 * k:4 * k + 4],
                gsm1[:, M1 + k:M1 + k + 1], None, ALU.is_ge))
        for k in range(8):
            P.op("vector", lambda e, k=k: e.tensor_scalar_mul(
                gsm[:, T2 + 4 * k:T2 + 4 * k + 4], gsm[:, T1 + 4 * k:T1 + 4 * k + 4], -30000.0))
        for k in range(8):
            P.op("vector", lambda e, k=k: e.tensor_add(
                gsm[:, T1 + 4 * k:T1 + 4 * k + 4], gsm[:, T2 + 4 * k:T2 + 4 * k + 4],
                gsm[:, 4 * k:4 * k + 4]))
        for k in range(8):
            P.op("vector", lambda e, k=k: e.reduce_max(gsm1[:, M2 + k:M2 + k + 1],
                                                       gsm[:, T1 + 4 * k:T1 + 4 * k + 4], axis=AX.X))
        for k in range(8):
            w = [(sc, eev)] if k == 0 else []
            P.op("vector", lambda e, k=k: e.tensor_scalar(
                gsm[:, T2 + 4 * k:T2 + 4 * k + 4], gsm[:, 4 * k:4 * k + 4],
                gsm1[:, M2 + k:M2 + k + 1], None, ALU.is_ge), waits=w)
        for k in range(8):
            P.op("vector", lambda e, k=k: e.tensor_mul(
                gsm[:, T1 + 4 * k:T1 + 4 * k + 4], gsm[:, GE + 4 * k:GE + 4 * k + 4],
                gsm[:, T2 + 4 * k:T2 + 4 * k + 4]))
        for k in range(8):
            P.op("vector", lambda e, k=k: e.reduce_sum(gsm1[:, ZS + k:ZS + k + 1],
                                                       gsm[:, T1 + 4 * k:T1 + 4 * k + 4], axis=AX.X))
        for k in range(8):
            P.op("vector", lambda e, k=k: e.reciprocal(gsm1[:, RZ + k:RZ + k + 1],
                                                       gsm1[:, ZS + k:ZS + k + 1]))
        for k in range(8):
            P.op("vector", lambda e, k=k: e.tensor_scalar_mul(
                gsm1[:, RZ4 + k:RZ4 + k + 1], gsm1[:, RZ + k:RZ + k + 1], 4.0))
        gv = None
        for k in range(8):
            gv = P.op("vector", lambda e, k=k: e.tensor_scalar_mul(
                gsm[:, GE + 4 * k:GE + 4 * k + 4], gsm[:, T1 + 4 * k:T1 + 4 * k + 4],
                gsm1[:, RZ4 + k:RZ4 + k + 1]), inc=(ve if k == 7 else None))
        prev_gt = 0
        for k in range(8):
            own, s = divmod(k, 4)
            t2m = P.op("tensor", lambda e, k=k: e.matmul(gtps[:, :], gsm[:, GE + 4 * k:GE + 4 * k + 4],
                                                         id_sb[:, :], start=True, stop=True),
                       waits=[(ve, gv), (ve, prev_gt)], inc=mm)
            prev_gt = P.op("vector", lambda e, own=own, s=s: e.tensor_copy(
                gt_sb[:, own * 512 + s * 128: own * 512 + (s + 1) * 128], gtps[:, :]),
                waits=[(mm, t2m)], inc=ve)
        for own in range(2):
            gbm = P.op("tensor", lambda e, own=own: e.matmul(
                gbps[:, :], e2g_sb[:, :], gt_sb[:, own * 512:(own + 1) * 512], start=True, stop=True),
                waits=[(ve, ve.n)], inc=mm)
            gbc = P.op("vector", lambda e: e.tensor_copy(gb_sb, gbps[:, :]), waits=[(mm, gbm)], inc=ve)
            P.op("vector", lambda e, own=own: e.tensor_tensor(
                mfin[0:64, own * 512:(own + 1) * 512], maug[0:64, own * 512:(own + 1) * 512],
                gb_sb, ALU.mult), waits=[(ve, gbc)], inc=ve)
    ffn_done_ve = ve.n

    s1.close()

    ctx = E(nc.sbuf_tensor("ctx", [128, NQH * 2 * 512], BF16))
    sA = contextlib.ExitStack()
    EA = sA.enter_context
    qdiag = EA(nc.sbuf_tensor("qdiag", [128, 2 * 1024], BF16))
    ptb = EA(nc.sbuf_tensor("ptb", [128, 3 * 512], BF16))
    rcp = EA(nc.sbuf_tensor("rcp", [128, 512], F32))
    sps = [EA(nc.psum_tensor(f"sps{j}", [128, 512], F32)) for j in range(3)]
    ctxA_ps = EA(nc.psum_tensor("ctxA_ps", [128, 512], F32))
    ctxB_ps = EA(nc.psum_tensor("ctxB_ps", [128, 512], F32))
    sumA_ps = EA(nc.psum_tensor("sumA_ps", [128, 512], F32))
    sumB_ps = EA(nc.psum_tensor("sumB_ps", [128, 512], F32))

    P.barrier(["gpsimd"], [cc])
    P.barrier(["tensor", "scalar", "vector", "gpsimd", "sync"], [ve, mm, sc])

    expd = {}
    ptfree = {}
    sfree = {}
    gath = {}
    qgat = {}
    slotcons = {}
    tcons = {}
    ctxfree = {}
    g = 0

    for kvh in range(NKVH):
        bk = kvh % 2
        for p in range(NSLOT):
            w = [(mm, slotcons.get(kvh - 2, 0))]
            P.op("gpsimd", lambda e, kvh=kvh, p=p, bk=bk: e.indirect_dma_start(
                kslot[:, (bk * NSLOT + p) * 512:(bk * NSLOT + p + 1) * 512], None,
                ag_k, IndirectOffsetOnAxis(kidx_sb[:, kvh * NSLOT + p: kvh * NSLOT + p + 1], 0)),
                waits=w, inc=gd, inc_by=16)
            for kt in range(4):
                col = kvh * NSLOT * 4 + p * 4 + kt
                P.op("gpsimd", lambda e, col=col, bk=bk, p=p, kt=kt: e.indirect_dma_start(
                    vslotA[:, (bk * NSLOT + p) * 512 + kt * 128:(bk * NSLOT + p) * 512 + (kt + 1) * 128],
                    None, ag_v, IndirectOffsetOnAxis(vai_sb[:, col:col + 1], 0)),
                    inc=gd, inc_by=16)
                gath[kvh] = P.op("gpsimd", lambda e, col=col, bk=bk, p=p, kt=kt: e.indirect_dma_start(
                    vslotB[:, (bk * NSLOT + p) * 512 + kt * 128:(bk * NSLOT + p) * 512 + (kt + 1) * 128],
                    None, ag_v, IndirectOffsetOnAxis(vbi_sb[:, col:col + 1], 0)),
                    inc=gd, inc_by=16)

        for qq in range(4):
            qh = kvh * 4 + qq
            t = qh
            bt = t % 2
            qd0 = P.op("sync", lambda e, qh=qh, bt=bt: e.dma_start(
                out=qdiag[:, bt * 1024: bt * 1024 + 512],
                in_=qsp[qh * 256: qh * 256 + 256: 2, :]),
                waits=[(sd, qspw[(qh, 0)]), (mm, tcons.get(t - 1, 0))], inc=ld, inc_by=16)
            qd1 = P.op("sync", lambda e, qh=qh, bt=bt: e.dma_start(
                out=qdiag[:, bt * 1024 + 512: bt * 1024 + 1024],
                in_=qsp[qh * 256 + 1: qh * 256 + 256: 2, :]),
                waits=[(sd, qspw[(qh, 1)])], inc=ld, inc_by=16)
            for p in range(NSLOT):
                qgat[t] = P.op("gpsimd", lambda e, qh=qh, p=p, bt=bt: e.indirect_dma_start(
                    qslot[:, (bt * NSLOT + p) * 512:(bt * NSLOT + p + 1) * 512], None,
                    qsp[:, :],
                    IndirectOffsetOnAxis(qix_sb[:, qh * NSLOT + p: qh * NSLOT + p + 1], 0)),
                    waits=[(sd, qspw[(qh, 1)]), (mm, tcons.get(t - 2, 0))], inc=gd, inc_by=16)

            for own in range(2):
                own_ps_c = ctxA_ps if own == 0 else ctxB_ps
                own_ps_s = sumA_ps if own == 0 else sumB_ps
                for kt in range(4):
                    r = g % 3
                    n0 = kt * 128
                    smm = P.op("tensor", lambda e, r=r, kvh=kvh, own=own, kt=kt, n0=n0, bt=bt: e.matmul(
                        sps[r][:, n0:512],
                        k_own[:, kvh * 1024 + own * 512 + kt * 128: kvh * 1024 + own * 512 + (kt + 1) * 128],
                        qdiag[:, bt * 1024 + own * 512 + n0: bt * 1024 + (own + 1) * 512],
                        start=True, stop=True, skip_group_check=True),
                        waits=[(sc, sfree.get(g - 3, 0)), (ld, qd1 if own == 0 and kt == 0 else 0),
                               (ve, ctxfree.get(t - 1, 0) if kt == 0 else 0)],
                        inc=mm)
                    tad = P.op("vector", lambda e, r=r, n0=n0: e.tensor_tensor(
                        sps[r][:, n0:n0 + 128], sps[r][:, n0:n0 + 128], tri_sb[:, :], ALU.add),
                        waits=[(mm, smm)], inc=ve)
                    expd[g] = P.op("scalar", lambda e, r=r, n0=n0: e.activation(
                        ptb[:, r * 512 + n0: (r + 1) * 512], sps[r][:, n0:512], AF.Exp,
                        bias=biasv[:, 0:1], scale=SCALE),
                        waits=[(ve, tad), (mm, ptfree.get(g - 3, 0))], inc=sc)
                    sfree[g] = expd[g]
                    P.op("tensor", lambda e, r=r, n0=n0, ps=own_ps_s, kt=kt: e.matmul(
                        ps[:, n0:512], ones128[:, :], ptb[:, r * 512 + n0: (r + 1) * 512],
                        start=(kt == 0), stop=False, skip_group_check=True),
                        waits=[(sc, expd[g])])
                    ptfree[g] = P.op("tensor", lambda e, r=r, n0=n0, ps=own_ps_c, own=own, kt=kt, kvh=kvh: e.matmul(
                        ps[:, n0:512],
                        v_own[:, (own * 4 + kt) * 1024 + kvh * 128: (own * 4 + kt) * 1024 + (kvh + 1) * 128],
                        ptb[:, r * 512 + n0: (r + 1) * 512],
                        start=(kt == 0), stop=False, skip_group_check=True),
                        inc=mm)
                    g += 1

            for p in range(NSLOT):
                for kt in range(4):
                    r = g % 3
                    smm = P.op("tensor", lambda e, r=r, bk=bk, bt=bt, p=p, kt=kt: e.matmul(
                        sps[r][:, :],
                        kslot[:, (bk * NSLOT + p) * 512 + kt * 128:(bk * NSLOT + p) * 512 + (kt + 1) * 128],
                        qslot[:, (bt * NSLOT + p) * 512:(bt * NSLOT + p + 1) * 512],
                        start=True, stop=True, skip_group_check=True),
                        waits=[(sc, sfree.get(g - 3, 0)), (gd, gath[kvh]), (gd, qgat[t])],
                        inc=mm)
                    expd[g] = P.op("scalar", lambda e, r=r: e.activation(
                        ptb[:, r * 512:(r + 1) * 512], sps[r][:, :], AF.Exp,
                        bias=biasv[:, 0:1], scale=SCALE),
                        waits=[(mm, smm), (mm, ptfree.get(g - 3, 0))], inc=sc)
                    sfree[g] = expd[g]
                    P.op("tensor", lambda e, r=r, p=p: e.matmul(
                        sumA_ps[:, :], flagA[:, p * 128:(p + 1) * 128], ptb[:, r * 512:(r + 1) * 512],
                        start=False, stop=False, skip_group_check=True),
                        waits=[(sc, expd[g])])
                    P.op("tensor", lambda e, r=r, p=p: e.matmul(
                        sumB_ps[:, :], flagB[:, p * 128:(p + 1) * 128], ptb[:, r * 512:(r + 1) * 512],
                        start=False, stop=False, skip_group_check=True))
                    P.op("tensor", lambda e, r=r, bk=bk, p=p, kt=kt: e.matmul(
                        ctxA_ps[:, :],
                        vslotA[:, (bk * NSLOT + p) * 512 + kt * 128:(bk * NSLOT + p) * 512 + (kt + 1) * 128],
                        ptb[:, r * 512:(r + 1) * 512],
                        start=False, stop=False, skip_group_check=True))
                    ptfree[g] = P.op("tensor", lambda e, r=r, bk=bk, p=p, kt=kt: e.matmul(
                        ctxB_ps[:, :],
                        vslotB[:, (bk * NSLOT + p) * 512 + kt * 128:(bk * NSLOT + p) * 512 + (kt + 1) * 128],
                        ptb[:, r * 512:(r + 1) * 512],
                        start=False, stop=False, skip_group_check=True),
                        inc=mm)
                    g += 1
            slotcons[kvh] = ptfree[g - 1]
            tcons[t] = ptfree[g - 1]

            last_mm = ptfree[g - 1]
            fin = None
            for own in range(2):
                own_ps_c = ctxA_ps if own == 0 else ctxB_ps
                own_ps_s = sumA_ps if own == 0 else sumB_ps
                P.op("vector", lambda e, ps=own_ps_s: e.reciprocal(rcp[:, :], ps[:, :]),
                     waits=[(mm, last_mm)])
                fin = P.op("vector", lambda e, ps=own_ps_c, t=t, own=own: e.tensor_tensor(
                    ctx[:, (t * 2 + own) * 512:(t * 2 + own + 1) * 512], ps[:, :], rcp[:, :], ALU.mult),
                    inc=ve)
            ctxfree[t] = fin
    attn_ve = ve.n

    sA.close()

    sW = contextlib.ExitStack()
    EW = sW.enter_context
    wof = k_own[:, :].bitcast(F32)
    wof2 = v_own[:, :].bitcast(F32)
    wbbf = EW(nc.sbuf_tensor("wbbf", [64, 4096], BF16))
    wobf = EW(nc.sbuf_tensor("wobf", [128, 2 * 4096], BF16))
    xres = kslot[:, 0:4096].bitcast(F32)
    osb = vslotA[:, 0:4096].bitcast(F32)
    ops_ = [EW(nc.psum_tensor(f"ops{j}", [128, 1024], F32)) for j in range(2)]

    P.barrier(["tensor", "vector", "sync"], [ve, mm])

    wbl = dma("sync", wof[0:64, :].rearrange("p (o j) -> p o j", j=128),
              wb.ap().rearrange("o e j -> e o j"), ld)
    P.op("vector", lambda e: e.tensor_copy(wbbf[:, :], wof[0:64, :]), waits=[(ld, wbl)], inc=ve)
    wbc = P.op("vector", lambda e: e.tensor_copy(spcr[:, :], spcr[:, :]), inc=ve)

    ocast = {}
    oevac = {}
    ommd = {}
    ostw = {}
    def stage_o(oc):
        b = oc % 2
        buf = wof if b == 0 else wof2
        cpiece = ocast.get(oc - 2, wbc)
        for qtr in range(2):
            lq = dma("sync", buf[:, qtr * 2048:(qtr + 1) * 2048],
                     wo[oc][:, qtr * 2048:(qtr + 1) * 2048], ld,
                     waits=[(ve, cpiece)])
            cpiece = P.op("vector", lambda e, b=b, qtr=qtr, buf=buf: e.tensor_copy(
                wobf[:, b * 4096 + qtr * 2048: b * 4096 + (qtr + 1) * 2048],
                buf[:, qtr * 2048:(qtr + 1) * 2048]),
                waits=[(ld, lq), (mm, ommd.get(oc - 2, 0) if qtr == 0 else 0)], inc=ve)
        ocast[oc] = P.op("vector", lambda e: e.tensor_copy(spcr[:, :], spcr[:, :]), inc=ve)

    stage_o(0)
    stage_o(1)
    for oc in range(NQH):
        b = oc % 2
        if oc >= 2:
            pass
        xl0 = dma("sync", xres[:, b * 1024: b * 1024 + 512], xTA[oc * 128:(oc + 1) * 128, :], ld,
                  waits=[(ve, oevac.get(oc - 2, 0))])
        xl1 = dma("sync", xres[:, b * 1024 + 512: b * 1024 + 1024], xTB[oc * 128:(oc + 1) * 128, :], ld)
        omm = None
        for own in range(2):
            for ccc in range(NF):
                w = [(ve, ocast[oc]), (ve, attn_ve), (ve, wbc)]
                if ccc == 0 and own == 0:
                    w.append((ve, oevac.get(oc - 2, 0)))
                P.op("tensor", lambda e, oc=oc, own=own, ccc=ccc, b=b: e.matmul(
                    ops_[b][:, own * 512:(own + 1) * 512],
                    wobf[:, b * 4096 + ccc * 128: b * 4096 + (ccc + 1) * 128],
                    ctx[:, (ccc * 2 + own) * 512:(ccc * 2 + own + 1) * 512],
                    start=(ccc == 0), stop=False, skip_group_check=True),
                    waits=w)
            omm = P.op("tensor", lambda e, oc=oc, own=own, b=b: e.matmul(
                ops_[b][:, own * 512:(own + 1) * 512],
                wbbf[:, oc * 128:(oc + 1) * 128],
                mfin[0:64, own * 512:(own + 1) * 512],
                start=False, stop=True, skip_group_check=True),
                waits=[(ve, ffn_done_ve)], inc=mm)
        ommd[oc] = omm
        if oc + 2 < NQH:
            stage_o(oc + 2)
        ev = []
        for own in range(2):
            ev.append(P.op("vector", lambda e, own=own, b=b: e.tensor_tensor(
                osb[:, b * 1024 + own * 512: b * 1024 + (own + 1) * 512],
                ops_[b][:, own * 512:(own + 1) * 512],
                xres[:, b * 1024 + own * 512: b * 1024 + (own + 1) * 512], ALU.add),
                waits=[(mm, omm), (ld, xl1), (st, ostw.get(oc - 2, 0))], inc=ve))
        oevac[oc] = ev[-1]
        P.op("sync", lambda e, oc=oc, b=b: e.dma_start(
            out=outA[oc * 128:(oc + 1) * 128, :], in_=osb[:, b * 1024: b * 1024 + 512]),
            waits=[(ve, ev[0])], inc=st, inc_by=16)
        ostw[oc] = P.op("sync", lambda e, oc=oc, b=b: e.dma_start(
            out=outB[oc * 128:(oc + 1) * 128, :], in_=osb[:, b * 1024 + 512: b * 1024 + 1024]),
            waits=[(ve, ev[1])], inc=st, inc_by=16)

    P.op("sync", lambda e: None, waits=[(st, st.n)])

    with (nc.semaphore("s_ld") as h_ld, nc.semaphore("s_gd") as h_gd,
          nc.semaphore("s_mm") as h_mm, nc.semaphore("s_ve") as h_ve,
          nc.semaphore("s_sc") as h_sc, nc.semaphore("s_cc") as h_cc,
          nc.semaphore("s_st") as h_st, nc.semaphore("s_sd") as h_sd,
          nc.semaphore("s_io") as h_io, nc.Block() as block):
        H = {"ld": h_ld, "gd": h_gd, "mm": h_mm, "ve": h_ve, "sc": h_sc, "cc": h_cc,
             "st": h_st, "sd": h_sd, "io": h_io}

        def runner(name):
            def run(eng):
                for fn, ws, inc in P.ops[name]:
                    for s, v in ws:
                        eng.wait_ge(H[s.name], v)
                    inst = fn(eng) if fn is not None else None
                    if inc is not None:
                        if inst is not None:
                            inst.then_inc(H[inc[0].name], inc[1])
                        else:
                            eng.sem_inc(H[inc[0].name], inc[1])
            return run

        block.sync(runner("sync"))
        block.gpsimd(runner("gpsimd"))
        block.vector(runner("vector"))
        block.scalar(runner("scalar"))
        block.tensor(runner("tensor"))

    sW.close()
    stack.close()
    return nc



def _stage_shared(Wq, Wk, Wv, Wo, router_w, A, Bm):
    perm = np.empty(128, np.int64)
    perm[0:64] = np.arange(64) * 2
    perm[64:128] = np.arange(64) * 2 + 1
    wq = np.ascontiguousarray(Wq.reshape(NQH, 128, NF, 128)[:, perm].transpose(0, 3, 2, 1)
                              .reshape(NQH, 128, D))
    wk = np.ascontiguousarray(Wk.reshape(NKVH, 128, NF, 128)[:, perm].transpose(0, 3, 2, 1)
                              .reshape(NKVH, 128, D))
    wv = np.ascontiguousarray(Wv.reshape(2, 512, NF, 128).transpose(0, 2, 3, 1))
    wo = np.ascontiguousarray(Wo.reshape(NQH, 128, NF, 128).transpose(0, 3, 2, 1)
                              .reshape(NQH, 128, D))
    a_aug = np.concatenate([A.reshape(64, D), router_w], axis=0)
    wa = np.ascontiguousarray(a_aug.reshape(68, NF, 128).transpose(1, 2, 0))
    wb = np.ascontiguousarray(Bm.reshape(4, 32, 128, 16).transpose(1, 0, 3, 2).reshape(32, 64, 128))
    tri = np.where(np.arange(128)[:, None] <= np.arange(128)[None, :], 0.0, -1e9).astype(np.float32)
    ident = np.eye(128, dtype=np.float32)
    e2g = np.zeros((4, 64), np.float32)
    for ee in range(4):
        e2g[ee, ee * 16:(ee + 1) * 16] = 1.0
    return wq, wk, wv, wo, wa, wb, tri, ident, e2g


def _stage_core(c):
    cA, cB = c, 7 - c
    pos = np.stack([512 * cA + np.arange(CH, dtype=np.float32),
                    512 * cB + np.arange(CH, dtype=np.float32)]).astype(np.float32)
    slots = []
    for p in range(NSLOT):
        if p < cA:
            slots.append((p, 0, 0))
        else:
            j = p - cA
            slots.append((7 - j, 1, 1))
    dims = np.arange(128, dtype=np.int64)
    kidx = np.zeros((128, NKVH * NSLOT), np.int32)
    vai = np.zeros((128, NKVH * NSLOT * 4), np.int32)
    vbi = np.zeros((128, NKVH * NSLOT * 4), np.int32)
    qix = np.zeros((128, NQH * NSLOT), np.int32)
    fix = np.zeros((128, 2 * NSLOT), np.int32)
    ZROW = NCORE * 4096 * 4
    for p, (rank, seg, own) in enumerate(slots):
        for qh in range(NQH):
            qix[:, qh * NSLOT + p] = qh * 256 + dims * 2 + own
        fix[:, 0 * NSLOT + p] = 0 if own == 0 else 32
        fix[:, 1 * NSLOT + p] = 0 if own == 1 else 32
        for kvh in range(NKVH):
            kidx[:, kvh * NSLOT + p] = rank * 4096 + kvh * 256 + dims * 2 + seg
            for kt in range(4):
                col = kvh * NSLOT * 4 + p * 4 + kt
                vrow = rank * 16384 + 8192 + kvh * 1024 + seg * 512 + kt * 128 + dims
                if own == 0:
                    vai[:, col] = vrow
                    vbi[:, col] = ZROW + dims
                else:
                    vai[:, col] = ZROW + dims
                    vbi[:, col] = vrow
    return pos, kidx, vai, vbi, qix, fix


def kernel(L0_attn_hidden_states, L0_attn_Wq, L0_attn_Wk, L0_attn_Wv, L0_attn_Wo,
           L0_ffn_router_w, L0_ffn_A, L0_ffn_B):
    x = np.asarray(L0_attn_hidden_states, dtype=np.float32)
    wq, wk, wv, wo, wa, wb, tri, ident, e2g = _stage_shared(
        np.asarray(L0_attn_Wq, np.float32), np.asarray(L0_attn_Wk, np.float32),
        np.asarray(L0_attn_Wv, np.float32), np.asarray(L0_attn_Wo, np.float32),
        np.asarray(L0_ffn_router_w, np.float32), np.asarray(L0_ffn_A, np.float32),
        np.asarray(L0_ffn_B, np.float32))

    in_maps = []
    for c in range(NCORE):
        cA, cB = c, 7 - c
        pos, kidx, vai, vbi, qix, fix = _stage_core(c)
        in_maps.append({
            "xTA": np.ascontiguousarray(x[0, 512 * cA:512 * (cA + 1), :].T),
            "xTB": np.ascontiguousarray(x[1, 512 * cB:512 * (cB + 1), :].T),
            "wq": wq, "wk": wk, "wv": wv, "wo": wo, "wa": wa, "wb": wb,
            "tri": tri, "ident": ident, "e2g": e2g, "pos": pos,
            "kidx": kidx, "vai": vai, "vbi": vbi, "qix": qix, "fix": fix,
        })

    nc = build_program()
    res = run_bass_kernel_spmd(nc, in_maps, core_ids=list(range(NCORE)))

    out = np.empty((2, 4096, 4096), dtype=np.float32)
    for c in range(NCORE):
        cA, cB = c, 7 - c
        out[0, 512 * cA:512 * (cA + 1), :] = np.asarray(res.results[c]["outA"]).T
        out[1, 512 * cB:512 * (cB + 1), :] = np.asarray(res.results[c]["outB"]).T
    return out
